# version 02
# baseline (speedup 1.0000x reference)
import numpy as np
import ml_dtypes
from contextlib import ExitStack

import concourse.bass as bass
import concourse.mybir as mybir
from concourse.bass_utils import run_bass_kernel_spmd

F32 = mybir.dt.float32
BF16 = mybir.dt.bfloat16
I32 = mybir.dt.int32
AF = mybir.ActivationFunctionType
ALU = mybir.AluOpType
AX = mybir.AxisListType

B, S, D = 4, 2048, 4096
T = 1024
NQH, NKV, HD = 32, 8, 128
BLK = 64
KVT = 2048
WIN = 960
NKC = 11
KEYS = NKC * 128
IFF = 2048
NCH = 8
EPS_LN = 1e-5
EPS_RMS = 1e-5
SM_SCALE = 1.0 / HD
NEG = -1e30
NSUB = 4
DEBUG = None


def _emask():
    nb = S // BLK
    qb = np.arange(nb)[:, None]
    kb = np.arange(nb)[None, :]
    bm = (kb <= qb) & ((qb - kb < 16) | ((kb + 1) % 8 == 0))
    em = np.kron(bm, np.ones((BLK, BLK), dtype=bool))
    i = np.arange(S)
    em &= i[:, None] >= i[None, :]
    return em


def _kchunk_starts(p):
    return [128 * p + 128 * c for c in range(9)] + [384, 896]


def host_prep(inputs):
    f32 = np.float32
    x = np.asarray(inputs["L0_attn_a_hidden_states"], f32)

    wqkv = np.asarray(inputs["L0_attn_a_w_qkv"], f32)
    bqkv = np.asarray(inputs["L0_attn_a_b_qkv"], f32)
    rows = np.arange(6144).reshape(NKV, 6, HD)
    q_rows = rows[:, :4].reshape(-1)
    kv_rows = np.concatenate([rows[:, 4].reshape(-1), rows[:, 5].reshape(-1)])
    m0 = dict(
        wqaT=np.ascontiguousarray(wqkv[q_rows].T),
        wkvaT=np.ascontiguousarray(wqkv[kv_rows].T),
        bqa=np.ascontiguousarray(bqkv[q_rows].reshape(NQH, HD).T),
        bkva=np.ascontiguousarray(bqkv[kv_rows].reshape(16, HD).T),
        wdaT=np.ascontiguousarray(np.asarray(inputs["L0_attn_a_w_dense"], f32).T),
        bda=np.ascontiguousarray(np.asarray(inputs["L0_attn_a_b_dense"], f32).reshape(32, HD).T),
        wgT=np.ascontiguousarray(np.asarray(inputs["L0_ffn_w_gate"], f32).T),
        wuT=np.ascontiguousarray(np.asarray(inputs["L0_ffn_w_up"], f32).T),
        wdfT=np.ascontiguousarray(np.asarray(inputs["L0_ffn_w_down"], f32).T),
        s_wg_c=np.ascontiguousarray(np.asarray(inputs["L0_ffn_s_wgate"], f32).reshape(16, HD).T),
        s_wu_c=np.ascontiguousarray(np.asarray(inputs["L0_ffn_s_wup"], f32).reshape(16, HD).T),
        s_wd_c=np.ascontiguousarray(np.asarray(inputs["L0_ffn_s_wdown"], f32).reshape(32, HD).T),
        inv_g_rep=np.broadcast_to(np.asarray(inputs["L0_ffn_inv_gate"], f32), (128, D)).copy(),
        inv_u_rep=np.broadcast_to(np.asarray(inputs["L0_ffn_inv_up"], f32), (128, D)).copy(),
        inv_i_rep=np.broadcast_to(np.asarray(inputs["L0_ffn_inv_inter"], f32), (128, IFF)).copy(),
        wqbT=np.ascontiguousarray(np.asarray(inputs["L0_attn_b_Wq"], f32).T),
        wgbT=np.ascontiguousarray(np.asarray(inputs["L0_attn_b_Wg"], f32).T),
        wobT=np.ascontiguousarray(np.asarray(inputs["L0_attn_b_Wo"], f32).T),
        dtb_rep=np.broadcast_to(np.asarray(inputs["L0_attn_b_dt_bias"], f32), (128, 32)).copy(),
        alog_rep=np.broadcast_to(np.asarray(inputs["L0_attn_b_A_log_bias"], f32), (128, 32)).copy(),
        gnw_rep=np.broadcast_to(np.asarray(inputs["L0_attn_b_gnorm_w"], f32), (128, HD)).copy(),
        ident=np.eye(128, dtype=f32),
    )
    winp = np.zeros((128, D), f32)
    winp[:32] = np.asarray(inputs["L0_attn_b_Win"], f32)
    m0["wkvdT"] = np.ascontiguousarray(np.concatenate(
        [np.asarray(inputs["L0_attn_b_Wk"], f32),
         np.asarray(inputs["L0_attn_b_Wv"], f32), winp], 0).T)
    e0 = np.zeros((128, 128), f32)
    e0[0, :] = 1.0
    m0["e0"] = e0
    m0["ltri"] = (np.arange(128)[:, None] <= np.arange(128)[None, :]).astype(f32)
    m0["maskTb"] = np.where(np.arange(128)[None, :] >= np.arange(128)[:, None],
                            0.0, NEG).astype(f32)

    ifa = 1.0 / (1e6 ** (np.arange(0, HD, 2, dtype=np.float64) / HD))
    ifb = 1.0 / (1e4 ** (np.arange(0, HD, 2, dtype=np.float64) / HD))
    em = _emask()

    in_maps = []
    for c in range(8):
        b, h = c >> 1, c & 1
        m = dict(m0)
        g0 = (16 * h - 15) * BLK
        xall = np.zeros((KVT, D), f32)
        lo = max(0, -g0)
        hi = min(WIN + T, S - g0)
        xall[lo:hi] = x[b, g0 + lo:g0 + hi]
        m["x_all"] = np.ascontiguousarray(xall.T)
        m["x_own"] = np.ascontiguousarray(x[b, h * T:(h + 1) * T].T)

        pos = np.maximum(g0 + np.arange(KVT, dtype=np.float64), 0.0)
        ang = pos[None, :] * np.concatenate([ifa, ifa])[:, None]
        m["cosa"] = np.cos(ang).astype(f32)
        m["sina"] = np.sin(ang).astype(f32)

        masks = np.full((8, 128, KEYS), NEG, f32)
        for p in range(8):
            qg = h * T + 128 * p + np.arange(128)
            seen = set()
            for ci, st in enumerate(_kchunk_starts(p)):
                for kj in range(128):
                    kv = st + kj
                    g = g0 + kv
                    if g < 0 or g >= S or kv >= WIN + T or kv in seen:
                        continue
                    seen.add(kv)
                    masks[p, :, ci * 128 + kj] = np.where(em[qg, g], 0.0, NEG)
        m["maska"] = np.ascontiguousarray(
            masks.transpose(1, 0, 2).reshape(128, 8 * KEYS)).astype(ml_dtypes.bfloat16)

        posb = h * T + np.arange(T, dtype=np.float64)
        angb = posb[None, :] * np.concatenate([ifb, ifb])[:, None]
        m["cosb"] = np.cos(angb).astype(f32)
        m["sinb"] = np.sin(angb).astype(f32)
        m["use_s0"] = np.full((128, 1), float(h), f32)
        in_maps.append(m)
    return in_maps


class Buf:
    __slots__ = ("w", "r")

    def __init__(self):
        self.w = None
        self.r = []


class ET:
    def __init__(self, name, attr):
        self.name = name
        self.attr = attr
        self.cval = 0
        self.dval = {}
        self.hw = {}
        self.ops = []
        self.csem = None
        self.dsems = {}


class KB:
    def __init__(self):
        self.pe = ET("pe", "tensor")
        self.dve = ET("dve", "vector")
        self.act = ET("act", "scalar")
        self.gpq = ET("gpq", "gpsimd")
        self.spq = ET("spq", "sync")
        self.ets = {e.name: e for e in (self.pe, self.dve, self.act, self.gpq, self.spq)}

    def _sem(self, key):
        e = self.ets[key[0]]
        return e.csem if key[1] == "c" else e.dsems[key[2]]

    def _collect(self, et, reads, writes, extra):
        deps = list(extra)
        for b in reads:
            if b.w is not None:
                deps.append(b.w)
        for b in writes:
            deps.extend(b.r)
            if b.w is not None:
                deps.append(b.w)
        need = {}
        for key, v in deps:
            if et.hw.get(key, 0) >= v:
                continue
            need[key] = max(need.get(key, 0), v)
        for key, v in need.items():
            et.hw[key] = v
        return list(need.items())

    @staticmethod
    def _finish(tok, reads, writes):
        for b in reads:
            b.r.append(tok)
        for b in writes:
            b.w = tok
            b.r = []
        return tok

    def op(self, et, build, reads=(), writes=(), extra=(), dma=False, fuse=True,
           stream=None):
        waits = self._collect(et, reads, writes, extra)
        if dma:
            assert stream is not None
            et.dval[stream] = et.dval.get(stream, 0) + 16
            et.dsems.setdefault(stream, None)
            tok = ((et.name, "d", stream), et.dval[stream])
            fuse = False
        else:
            et.cval += 1
            tok = ((et.name, "c"), et.cval)
        kb = self

        def closure(eng, waits=waits, build=build, et=et, tok=tok, fuse=fuse,
                    dma=dma, stream=stream):
            std = waits[:-1] if (fuse and waits) else waits
            for key, v in std:
                eng.wait_ge(kb._sem(key), v)
            inst = build(eng)
            if fuse and waits:
                key, v = waits[-1]
                inst._wait_ge(kb._sem(key), v)
            inst.then_inc(et.csem if not dma else et.dsems[stream], 1 if not dma else 16)

        et.ops.append(closure)
        return self._finish(tok, reads, writes)

    def fence(self, et, bufs):
        tok = ((et.name, "c"), et.cval)
        for b in bufs:
            b.r.append(tok)

    def barrier(self, et, stream, bufs):
        v = et.dval.get(stream, 0)
        if v == 0:
            return
        tok = ((et.name, "d", stream), v)
        for b in bufs:
            if b.w is not None and b.w[0] == tok[0]:
                b.w = tok

    def group(self, builders, reads=(), writes=(), extra=()):
        et = self.pe
        waits = self._collect(et, reads, writes, extra)
        et.cval += 1
        tok = ((et.name, "c"), et.cval)
        kb = self

        def closure(eng, waits=waits, builders=builders, et=et):
            for key, v in waits:
                eng.wait_ge(kb._sem(key), v)
            inst = None
            for b in builders:
                inst = b(eng)
            inst.then_inc(et.csem, 1)

        et.ops.append(closure)
        return self._finish(tok, reads, writes)

    def finalize(self, block, final_waits):
        kb = self
        for et in self.ets.values():
            def run(eng, ops=et.ops):
                for f in ops:
                    f(eng)
            getattr(block, et.attr)(run)

        def tail(eng):
            for key, v in final_waits:
                eng.wait_ge(kb._sem(key), v)
        block.sync(tail)

    def make_sems(self, nc, sems):
        for et in self.ets.values():
            et.csem = sems.enter_context(nc.semaphore(f"s_{et.name}_c"))
            for st in list(et.dsems):
                et.dsems[st] = sems.enter_context(nc.semaphore(f"s_{et.name}_{st}"))


def build_nc(nsub=None, debug=None):
    if nsub is None:
        nsub = NSUB
    if debug is None:
        debug = DEBUG
    nc = bass.Bass("TRN2", target_bir_lowering=False)
    kb = KB()

    def din(name, shape, dt=F32):
        return nc.dram_tensor(name, shape, dt, kind="ExternalInput").ap()

    x_all = din("x_all", [D, KVT])
    x_own = din("x_own", [D, T])
    wqaT = din("wqaT", [D, D])
    wkvaT = din("wkvaT", [D, 2048])
    bqa_in = din("bqa", [128, 32])
    bkva_in = din("bkva", [128, 16])
    wdaT = din("wdaT", [D, D])
    bda_in = din("bda", [128, 32])
    cosa_in = din("cosa", [128, KVT])
    sina_in = din("sina", [128, KVT])
    maska_in = din("maska", [128, 8 * KEYS], BF16)
    wgT_in = din("wgT", [D, IFF])
    wuT_in = din("wuT", [D, IFF])
    wdfT_in = din("wdfT", [IFF, D])
    swg_in = din("s_wg_c", [128, 16])
    swu_in = din("s_wu_c", [128, 16])
    swd_in = din("s_wd_c", [128, 32])
    invg_in = din("inv_g_rep", [128, D])
    invu_in = din("inv_u_rep", [128, D])
    invi_in = din("inv_i_rep", [128, IFF])
    wqbT = din("wqbT", [D, D])
    wkvdT = din("wkvdT", [D, 2176])
    wgbT = din("wgbT", [D, D])
    wobT = din("wobT", [D, D])
    dtb_in = din("dtb_rep", [128, 32])
    alog_in = din("alog_rep", [128, 32])
    gnw_in = din("gnw_rep", [128, HD])
    ident_in = din("ident", [128, 128])
    e0_in = din("e0", [128, 128])
    ltri_in = din("ltri", [128, 128])
    maskTb_in = din("maskTb", [128, 128])
    cosb_in = din("cosb", [128, T])
    sinb_in = din("sinb", [128, T])
    uses0_in = din("use_s0", [128, 1])

    out_fm = nc.dram_tensor("out_fm", [D, T], F32, kind="ExternalOutput").ap()

    xs = [nc.dram_tensor(f"xs{i}", [D, T], F32).ap() for i in range(2)]
    qa_hbm = nc.dram_tensor("qa_hbm", [D, T], BF16).ap()
    qb_hbm = nc.dram_tensor("qb_hbm", [D, T], BF16).ap()
    gate_hbm = nc.dram_tensor("gate_hbm", [T, D], BF16).ap()
    y_hbm = nc.dram_tensor("y_hbm", [D, T], BF16).ap()
    cc_in = nc.dram_tensor("cc_in", [D, HD], F32).ap()
    cc_out = nc.dram_tensor("cc_out", [2 * D, HD], F32).ap()

    xall_b, xown_b = Buf(), Buf()
    xs_b = [[Buf() for _ in range(32)] for _ in range(2)]
    qa_b, qb_b, gate_b, y_b = Buf(), Buf(), Buf(), Buf()
    ccin_b, ccout_b = Buf(), Buf()
    out_b = [Buf() for _ in range(32)]

    es = ExitStack()

    def sbuf(name, shape, dt=F32):
        return es.enter_context(nc.sbuf_tensor(name, shape, dt))

    c_ident = sbuf("c_ident", [128, 128])
    c_e0 = sbuf("c_e0", [128, 128])
    c_ones = sbuf("c_ones", [128, 128])
    c_onesb = sbuf("c_onesb", [128, 128], BF16)
    c_ltri = sbuf("c_ltri", [128, 128])
    c_maskTb = sbuf("c_maskTb", [128, 128])
    c_permR = sbuf("c_permR", [128, 128], BF16)
    c_bqa = sbuf("c_bqa", [128, 32])
    c_bkva = sbuf("c_bkva", [128, 16])
    c_bda = sbuf("c_bda", [128, 32])
    c_swg = sbuf("c_swg", [128, 16])
    c_swu = sbuf("c_swu", [128, 16])
    c_swd = sbuf("c_swd", [128, 32])
    c_dtb = sbuf("c_dtb", [128, 32])
    c_negA = sbuf("c_negA", [128, 32])
    c_gnw = sbuf("c_gnw", [128, HD])
    c_uses0 = sbuf("c_uses0", [128, 1])

    xst = [sbuf(f"xst{i}", [128, 512]) for i in range(3)]
    wb = [sbuf(f"wb{i}", [128, 512], BF16) for i in range(4)]
    yb = [sbuf(f"yb{i}", [128, 512], BF16) for i in range(2)]
    rowbuf = sbuf("rowbuf", [128, 512])
    statr = sbuf("statr", [128, 1024])
    rstd_rep = sbuf("rstd_rep", [128, 512])
    negm_rep = sbuf("negm_rep", [128, 512])
    cos_sb = sbuf("cos_sb", [128, 512])
    sin_sb = sbuf("sin_sb", [128, 512])
    red = sbuf("red", [128, 16])
    probs = sbuf("probs", [128, KEYS], BF16)
    probsT = sbuf("probsT", [128, KEYS], BF16)
    mbuf = sbuf("mbuf", [128, KEYS], BF16)
    qtile = sbuf("qtile", [128, 128], BF16)
    tile_a = sbuf("tile_a", [128, 512], BF16)
    tile_b = sbuf("tile_b", [128, 512], BF16)
    tile_c = sbuf("tile_cs", [128, 128], BF16)
    tile_d = sbuf("tile_d", [128, 128], BF16)
    tile_e = sbuf("tile_e", [128, 128], BF16)
    tile_f = sbuf("tile_f", [128, 128], BF16)
    qbf = sbuf("qbf", [128, 1024], BF16)
    wrk = [sbuf(f"wrk{i}", [128, 1024]) for i in range(3)]
    wrki = sbuf("wrki", [128, 1024], I32)
    invb = sbuf("invb", [128, 1024])
    state = sbuf("state", [128, NQH * HD])
    dtfm = sbuf("dtfm", [128, T])
    gcols = sbuf("gcols", [128, NCH * 96])
    grows = sbuf("grows", [128, NCH * 128])
    dtcols = sbuf("dtcols", [128, NCH * 32])
    egi = sbuf("egi", [128, NCH * 32])
    edec = sbuf("edec", [128, NCH * 32])
    ecd = sbuf("ecd", [128, NCH * 32])
    cumg = sbuf("cumg", [128, 32])
    sco = sbuf("sco", [128, 8])
    scq = sbuf("scq", [128, 16])
    scqr = sbuf("scqr", [128, 16])

    arA = sbuf("arA", [128, 32 * 512], BF16)
    arB = sbuf("arB", [128, 32 * 1024], BF16)
    arC = sbuf("arC", [128, 8 * 1024], BF16)

    psum = [es.enter_context(nc.psum_tensor(f"pb{i}", [128, 512], F32)) for i in range(8)]

    const_b = Buf()
    xst_b = [Buf() for _ in range(3)]
    wb_b = [Buf() for _ in range(4)]
    yb_b = [Buf() for _ in range(2)]
    rowbuf_b, statr_b, rep_b, cosb_b, red_b, sc_b = Buf(), Buf(), Buf(), Buf(), Buf(), Buf()
    probs_b, probsT_b, mbuf_b, qtile_b = Buf(), Buf(), Buf(), Buf()
    tab_b = [Buf() for _ in range(6)]
    qbf_b = Buf()
    wrk_b = [Buf() for _ in range(3)]
    wrki_b = Buf()
    invb_b = Buf()
    state_b, S0_b, dtfm_b = Buf(), Buf(), Buf()
    g_b = Buf()
    scq_b = Buf()
    arA_b = [Buf() for _ in range(32)]
    arB_b = [Buf() for _ in range(32)]
    arC_b = [Buf() for _ in range(8)]
    ps_b = [Buf() for _ in range(8)]
    srep = [rstd_rep, negm_rep]
    srep_b = [rep_b, rep_b]
    S0bf = arA[:][:, 0:NQH * HD // 4 * 0 + 4096]
    S0_bufs = [arA_b[i] for i in range(8)]

    vA = arA[:].rearrange("p (c t) -> p c t", t=512)

    def vBkT(kv):
        return arB[:].rearrange("p (kv t) -> p kv t", t=KVT)[:, kv, :]

    vBv = arB[:].rearrange("p (a ti r) -> p a ti r", a=2, r=1024)
    vBx = arB[:].rearrange("p (ti f) -> p ti f", f=D)
    vBq = arB[:].rearrange("p (a c t) -> p a c t", a=2, t=512)
    vBi = arB[:].rearrange("p (x c t) -> p x c t", x=4, t=512)
    vBit = arB[:].rearrange("p (y ti f) -> p y ti f", y=4, f=IFF)
    vBkt = arB[:].rearrange("p (ti r) -> p ti r", r=1024)
    vBpr = arB[:].rearrange("p (c t) -> p c t", t=128)
    vC = arC[:].rearrange("p (c t) -> p c t", t=1024)
    vC5 = arC[:].rearrange("p (c t) -> p c t", t=512)
    vC5 = arC[:].rearrange("p (c t) -> p c t", t=512)
    vD = arB[:].rearrange("p (c t) -> p c t", t=1024)

    wring = [0]

    def wtile(src_ap, width=512):
        i = wring[0] % 4
        wring[0] += 1
        kb.op(kb.gpq, lambda e, d=wb[i][:, 0:width], s=src_ap:
              e.dma_start(out=d, in_=s), writes=[wb_b[i]], dma=True, stream=f"wb{i}")
        return wb[i], wb_b[i]

    def dma(et, dst, src, reads=(), writes=(), stream="misc"):
        return kb.op(et, lambda e, d=dst, s=src: e.dma_start(out=d, in_=s),
                     reads=list(reads), writes=list(writes), dma=True, stream=stream)

    def dmaT(dst, src, reads=(), writes=(), stream="trA"):
        return kb.op(kb.act, lambda e, d=dst, s=src: e.dma_start_transpose(out=d, in_=s),
                     reads=list(reads), writes=list(writes), dma=True, stream=stream)

    for dst, src in [
        (c_ident[:], ident_in), (c_e0[:], e0_in), (c_ltri[:], ltri_in),
        (c_maskTb[:], maskTb_in), (c_bqa[:], bqa_in), (c_bkva[:], bkva_in),
        (c_bda[:], bda_in), (c_swg[:], swg_in), (c_swu[:], swu_in),
        (c_swd[:], swd_in), (c_dtb[:], dtb_in), (c_gnw[:], gnw_in),
        (c_uses0[:], uses0_in),
        (c_negA[:], alog_in),
    ]:
        dma(kb.gpq, dst, src, writes=[const_b], stream="const")
    kb.op(kb.dve, lambda e: e.memset(c_ones[:], 1.0), writes=[const_b])
    kb.op(kb.dve, lambda e: e.memset(c_onesb[:], 1.0), writes=[const_b])
    kb.op(kb.dve, lambda e: e.memset(rowbuf[:], 0.0), writes=[rowbuf_b])
    kb.op(kb.act, lambda e: e.activation(out=c_negA[:], in_=c_negA[:], func=AF.Exp),
          reads=[const_b], writes=[const_b])
    kb.op(kb.dve, lambda e: e.tensor_scalar_mul(out=c_negA[:], in0=c_negA[:], scalar1=-1.0),
          reads=[const_b], writes=[const_b])
    kb.op(kb.dve, lambda e: e.memset(c_permR[:], 0.0), writes=[const_b])
    kb.op(kb.dve, lambda e: e.tensor_scalar_mul(out=c_permR[64:128, 0:64],
                                                in0=c_ident[0:64, 0:64], scalar1=-1.0),
          reads=[const_b], writes=[const_b])
    kb.op(kb.dve, lambda e: e.tensor_copy(out=c_permR[0:64, 64:128], in_=c_ident[0:64, 0:64]),
          reads=[const_b], writes=[const_b])

    def row_broadcast(row_ap, row_buf, dst_ap, dst_buf, width=512, pbi=5):
        kb.op(kb.dve, lambda e: e.tensor_copy(out=rowbuf[0:1, 0:width], in_=row_ap),
              reads=[row_buf], writes=[rowbuf_b])
        kb.group([lambda e: e.matmul(psum[pbi][:, 0:width], c_e0[:], rowbuf[:, 0:width],
                                     start=True, stop=True)],
                 reads=[rowbuf_b, const_b], writes=[ps_b[pbi]])
        kb.op(kb.dve, lambda e: e.tensor_copy(out=dst_ap, in_=psum[pbi][:, 0:width]),
              reads=[ps_b[pbi]], writes=[dst_buf])

    def layernorm(src_fn, src_buf_fn):
        for c in range(32):
            st, stb = xst[c % 2], xst_b[c % 2]
            dma(kb.spq, st[:, 0:512], src_fn(c), reads=[src_buf_fn(c)], writes=[stb],
                stream=f"xst{c % 2}")
            kb.op(kb.dve, lambda e, c=c, st=st: e.tensor_copy(out=vA[:, c, :], in_=st[:, 0:512]),
                  reads=[stb], writes=[arA_b[c]])
            kb.op(kb.act, lambda e, st=st: e.activation(out=wrk[2][:, 0:512], in_=st[:, 0:512],
                                                        func=AF.Square),
                  reads=[stb], writes=[wrk_b[2]])
            kb.group([lambda e, c=c: e.matmul(psum[6][:, 0:512], c_onesb[:], vA[:, c, :],
                                              start=(c == 0), stop=(c == 31))],
                     reads=[arA_b[c], const_b], writes=[ps_b[6]])
            kb.group([lambda e, c=c: e.matmul(psum[7][:, 0:512], c_ones[:], wrk[2][:, 0:512],
                                              start=(c == 0), stop=(c == 31))],
                     reads=[wrk_b[2], const_b], writes=[ps_b[7]])
        r0 = lambda a: statr[0:1, a:a + 512]
        kb.op(kb.dve, lambda e: e.tensor_scalar_mul(out=r0(0), in0=psum[6][0:1, 0:512],
                                                    scalar1=1.0 / D),
              reads=[ps_b[6]], writes=[statr_b])
        kb.op(kb.dve, lambda e: e.tensor_scalar_mul(out=r0(512), in0=psum[7][0:1, 0:512],
                                                    scalar1=1.0 / D),
              reads=[ps_b[7], statr_b], writes=[statr_b])
        kb.op(kb.dve, lambda e: e.scalar_tensor_tensor(out=rowbuf[0:1, 0:512], in0=r0(0),
                                                       scalar=-1.0, in1=r0(0),
                                                       op0=ALU.mult, op1=ALU.mult),
              reads=[statr_b], writes=[rowbuf_b])
        kb.op(kb.dve, lambda e: e.tensor_tensor(out=r0(512), in0=r0(512),
                                                in1=rowbuf[0:1, 0:512], op=ALU.add),
              reads=[statr_b, rowbuf_b], writes=[statr_b])
        kb.op(kb.dve, lambda e: e.tensor_scalar_add(out=r0(512), in0=r0(512),
                                                    scalar1=EPS_LN),
              reads=[statr_b], writes=[statr_b])
        kb.op(kb.act, lambda e: e.activation(out=r0(512), in_=r0(512), func=AF.Ln),
              reads=[statr_b], writes=[statr_b])
        kb.op(kb.dve, lambda e: e.tensor_scalar_mul(out=r0(512), in0=r0(512),
                                                    scalar1=-0.5),
              reads=[statr_b], writes=[statr_b])
        kb.op(kb.act, lambda e: e.activation(out=r0(512), in_=r0(512), func=AF.Exp),
              reads=[statr_b], writes=[statr_b])
        kb.op(kb.dve, lambda e: e.tensor_tensor(out=r0(0), in0=r0(0), in1=r0(512),
                                                op=ALU.mult),
              reads=[statr_b], writes=[statr_b])
        kb.op(kb.dve, lambda e: e.tensor_scalar_mul(out=r0(0), in0=r0(0), scalar1=-1.0),
              reads=[statr_b], writes=[statr_b])
        row_broadcast(r0(512), statr_b, rstd_rep[:], rep_b)
        row_broadcast(r0(0), statr_b, negm_rep[:], rep_b)
        for c in range(32):
            st, stb = xst[c % 2], xst_b[c % 2]
            dma(kb.spq, st[:, 0:512], src_fn(c), reads=[src_buf_fn(c)], writes=[stb],
                stream=f"xst{c % 2}")
            kb.op(kb.dve, lambda e, st=st: e.tensor_tensor(out=wrk[0][:, 0:512],
                                                           in0=st[:, 0:512], in1=rstd_rep[:],
                                                           op=ALU.mult),
                  reads=[stb, rep_b], writes=[wrk_b[0]])
            kb.op(kb.dve, lambda e, c=c: e.tensor_tensor(out=vA[:, c, :], in0=wrk[0][:, 0:512],
                                                         in1=negm_rep[:], op=ALU.add),
                  reads=[wrk_b[0], rep_b], writes=[arA_b[c]])

    def gemm1(wT_ap_fn, nk, m_list, rhs_fn, evac_fn, nw=512):
        seen = []
        for g0 in range(0, len(m_list), 4):
            mg = m_list[g0:g0 + 4]
            for k in range(nk):
                wt, wtb = wtile(wT_ap_fn(k, mg[0], len(mg)), width=128 * len(mg))
                rhs, rhsb = rhs_fn(k)
                if rhsb not in seen:
                    seen.append(rhsb)
                builders = []
                for j in range(len(mg)):
                    builders.append(lambda e, j=j, k=k, wt=wt, rhs=rhs, nk=nk:
                                    e.matmul(psum[j][:, 0:nw], wt[:, 128 * j:128 * (j + 1)],
                                             rhs, start=(k == 0), stop=(k == nk - 1)))
                kb.group(builders, reads=[wtb, rhsb, const_b],
                         writes=[ps_b[j] for j in range(len(mg))])
            for j, m in enumerate(mg):
                evac_fn(m, j)
        kb.fence(kb.pe, seen)

    def rope_evac(ps_idx, nw, bias_col, cos_ap, sin_ap, dst_ap, dst_buf, tab_i=0):
        kw = dict(func=AF.Copy)
        if bias_col is not None:
            kw = dict(func=AF.Identity, bias=bias_col)
        kb.op(kb.act, lambda e, kw=kw: e.activation(out=tile_a[:, 0:nw],
                                                    in_=psum[ps_idx][:, 0:nw], **kw),
              reads=[ps_b[ps_idx], const_b], writes=[tab_b[0]])
        kb.group([lambda e: e.matmul(psum[4][:, 0:nw], c_permR[:], tile_a[:, 0:nw],
                                     start=True, stop=True)],
                 reads=[tab_b[0], const_b], writes=[ps_b[4]])
        kb.op(kb.dve, lambda e: e.tensor_tensor(out=wrk[0][:, 0:nw], in0=psum[4][:, 0:nw],
                                                in1=sin_ap, op=ALU.mult),
              reads=[ps_b[4], cosb_b, const_b], writes=[wrk_b[0]])
        kb.op(kb.dve, lambda e: e.tensor_tensor(out=wrk[1][:, 0:nw], in0=tile_a[:, 0:nw],
                                                in1=cos_ap, op=ALU.mult),
              reads=[tab_b[0], cosb_b, const_b], writes=[wrk_b[1]])
        kb.op(kb.dve, lambda e: e.tensor_tensor(out=dst_ap, in0=wrk[0][:, 0:nw],
                                                in1=wrk[1][:, 0:nw], op=ALU.add),
              reads=[wrk_b[0], wrk_b[1]], writes=[dst_buf])

    def quant_all(tok_fn, tok_bufs_fn, n_m, nfeat, inv_hbm, sc_base, qsink):
        nseg = nfeat // 1024
        for s in range(nseg):
            dma(kb.gpq, invb[:], inv_hbm[:, 1024 * s:1024 * (s + 1)], writes=[invb_b], stream="invb")
            for m in range(n_m):
                kb.op(kb.dve, lambda e, m=m, s=s: e.tensor_tensor(
                    out=wrk[0][:], in0=tok_fn(m)[:, 1024 * s:1024 * (s + 1)], in1=invb[:],
                    op=ALU.mult), reads=list(tok_bufs_fn(m, s)) + [invb_b],
                    writes=[wrk_b[0]])
                if s == 0:
                    kb.op(kb.dve, lambda e, m=m: e.reduce_max(
                        out=red[:, m:m + 1], in_=wrk[0][:], axis=AX.X,
                        apply_absolute_value=True), reads=[wrk_b[0]], writes=[red_b])
                else:
                    kb.op(kb.dve, lambda e: e.reduce_max(
                        out=red[:, 8:9], in_=wrk[0][:], axis=AX.X,
                        apply_absolute_value=True), reads=[wrk_b[0]], writes=[red_b])
                    kb.op(kb.dve, lambda e, m=m: e.tensor_tensor(
                        out=red[:, m:m + 1], in0=red[:, m:m + 1], in1=red[:, 8:9],
                        op=ALU.max), reads=[red_b], writes=[red_b])
        for m in range(n_m):
            kb.op(kb.dve, lambda e, m=m: e.tensor_scalar(
                out=scq[:, sc_base + m:sc_base + m + 1], in0=red[:, m:m + 1],
                scalar1=1.0 / 127.0, scalar2=1e-8, op0=ALU.mult, op1=ALU.max),
                reads=[red_b], writes=[scq_b])
            kb.op(kb.dve, lambda e, m=m: e.reciprocal(
                out=scqr[:, sc_base + m:sc_base + m + 1],
                in_=scq[:, sc_base + m:sc_base + m + 1]),
                reads=[scq_b], writes=[scq_b])
        for s in range(nseg):
            dma(kb.gpq, invb[:], inv_hbm[:, 1024 * s:1024 * (s + 1)], writes=[invb_b], stream="invb")
            for m in range(n_m):
                kb.op(kb.dve, lambda e, m=m, s=s: e.tensor_tensor(
                    out=wrk[0][:], in0=tok_fn(m)[:, 1024 * s:1024 * (s + 1)], in1=invb[:],
                    op=ALU.mult), reads=list(tok_bufs_fn(m, s)) + [invb_b],
                    writes=[wrk_b[0]])
                kb.op(kb.act, lambda e, m=m: e.activation(
                    out=wrk[0][:], in_=wrk[0][:], func=AF.Identity,
                    scale=scqr[:, sc_base + m:sc_base + m + 1]),
                    reads=[wrk_b[0], scq_b], writes=[wrk_b[0]])
                kb.op(kb.dve, lambda e: e.tensor_copy(out=wrki[:], in_=wrk[0][:]),
                      reads=[wrk_b[0]], writes=[wrki_b])
                kb.op(kb.dve, lambda e: e.tensor_copy(out=wrk[1][:], in_=wrki[:]),
                      reads=[wrki_b], writes=[wrk_b[1]])
                kb.op(kb.dve, lambda e: e.tensor_tensor(
                    out=wrk[2][:], in0=wrk[0][:], in1=wrk[1][:], op=ALU.subtract),
                    reads=[wrk_b[0], wrk_b[1]], writes=[wrk_b[2]])
                kb.op(kb.dve, lambda e: e.scalar_tensor_tensor(
                    out=wrk[1][:], in0=wrk[2][:], scalar=0.5, in1=wrk[1][:],
                    op0=ALU.is_ge, op1=ALU.add), reads=[wrk_b[2], wrk_b[1]],
                    writes=[wrk_b[1]])
                kb.op(kb.dve, lambda e: e.scalar_tensor_tensor(
                    out=wrk[1][:], in0=wrk[2][:], scalar=-0.5, in1=wrk[1][:],
                    op0=ALU.is_ge, op1=ALU.add), reads=[wrk_b[2], wrk_b[1]],
                    writes=[wrk_b[1]])
                kb.op(kb.dve, lambda e: e.tensor_scalar_add(
                    out=qbf[:], in0=wrk[1][:], scalar1=-1.0),
                    reads=[wrk_b[1]], writes=[qbf_b])
                qsink(m, s)

    def scale_row_rep(cols_ap, n_m, dst, dst_buf):
        kb.group([lambda e: e.transpose(psum[4][0:n_m, 0:128], cols_ap, c_ident[:])],
                 reads=[scq_b, const_b], writes=[ps_b[4]])
        kb.op(kb.dve, lambda e: e.tensor_copy(out=statr[0:n_m, 0:128],
                                              in_=psum[4][0:n_m, 0:128]),
              reads=[ps_b[4]], writes=[statr_b])
        for m in range(n_m):
            dma(kb.act, statr[0:1, 128 + 128 * m:128 + 128 * (m + 1)],
                statr[m:m + 1, 0:128], reads=[statr_b], writes=[statr_b],
                stream="srow")
        row_broadcast(statr[0:1, 128:128 + 128 * n_m], statr_b, dst[:, 0:128 * n_m],
                      dst_buf, width=128 * n_m)

    def attn_a(dst, dst_b, dst_stream):
        for qr in range(4):
            q0 = 512 * qr
            layernorm(lambda c, q0=q0: x_all[128 * c:128 * (c + 1), q0:q0 + 512],
                      lambda c: xall_b)
            dma(kb.spq, cos_sb[:], cosa_in[:, q0:q0 + 512], writes=[cosb_b], stream="cos")
            dma(kb.spq, sin_sb[:], sina_in[:, q0:q0 + 512], writes=[cosb_b], stream="cos")

            def kv_evac(m, j, q0=q0):
                half = 1 if q0 >= 1024 else 0
                toff = q0 % 1024
                if m < 8:
                    kv = m
                    slot = kv * 2 + half
                    rope_evac(j, 512, c_bkva[:, kv:kv + 1], cos_sb[:], sin_sb[:],
                              vBkT(kv)[:, q0:q0 + 512], arB_b[slot])
                else:
                    kv = m - 8
                    kb.op(kb.act, lambda e, j=j, kv=kv: e.activation(
                        out=tile_a[:, 0:512], in_=psum[j][:, 0:512], func=AF.Identity,
                        bias=c_bkva[:, 8 + kv:9 + kv]),
                        reads=[ps_b[j], const_b], writes=[tab_b[0]])
                    ti0 = q0 // 128
                    dmaT(vBv[:, 1, ti0:ti0 + 4, kv * 128:(kv + 1) * 128], tile_a[:, 0:512],
                         reads=[tab_b[0]], writes=[arB_b[16 + ti0 + i] for i in range(4)],
                         stream="trV")

            gemm1(lambda k, m0, mw: wkvaT[128 * k:128 * (k + 1), 128 * m0:128 * (m0 + mw)],
                  32, list(range(16)), lambda k: (vA[:, k, :], arA_b[k]), kv_evac)

            own_lo, own_hi = max(q0, WIN), min(q0 + 512, WIN + T)
            if own_hi > own_lo:
                nw = own_hi - own_lo
                toff = own_lo - q0

                def q_evac(m, j, nw=nw, toff=toff, oc=own_lo - WIN):
                    rope_evac(j, nw, c_bqa[:, m:m + 1], cos_sb[:, toff:toff + nw],
                              sin_sb[:, toff:toff + nw], tile_b[:, 0:nw], tab_b[1])
                    dma(kb.gpq, qa_hbm[128 * m:128 * (m + 1), oc:oc + nw], tile_b[:, 0:nw],
                        reads=[tab_b[1]], writes=[qa_b], stream="qa")

                gemm1(lambda k, m0, mw: wqaT[128 * k:128 * (k + 1), 128 * m0:128 * (m0 + mw)],
                      32, list(range(32)),
                      lambda k, toff=toff, nw=nw: (vA[:, k, toff:toff + nw], arA_b[k]),
                      q_evac, nw=nw)

        kb.barrier(kb.act, "trV", arB_b[16:32])
        for hh in range(2):
            for p4 in range(4):
                p = 4 * hh + p4
                dma(kb.spq, mbuf[:], maska_in[:, KEYS * p:KEYS * (p + 1)], writes=[mbuf_b], stream="mbuf")
                starts = _kchunk_starts(p)
                for h in range(NQH):
                    kv = h >> 2
                    dma(kb.spq, qtile[:], qa_hbm[128 * h:128 * (h + 1), 128 * p:128 * (p + 1)],
                        reads=[qa_b], writes=[qtile_b], stream="qtile")
                    segs = [(0, 0, starts[0], 512), (1, 0, starts[4], 512),
                            (2, 0, starts[8], 128), (2, 128, starts[9], 128),
                            (2, 256, starts[10], 128)]
                    builders, reads = [], [qtile_b]
                    for (pi, po, st, wdt) in segs:
                        builders.append(lambda e, pi=pi, po=po, st=st, wdt=wdt, kv=kv:
                                        e.matmul(psum[pi][:, po:po + wdt], qtile[:],
                                                 vBkT(kv)[:, st:st + wdt],
                                                 start=True, stop=True))
                        reads.append(arB_b[kv * 2 + (1 if st >= 1024 else 0)])
                        if (st % 1024) + wdt > 1024:
                            reads.append(arB_b[kv * 2 + 1])
                    kb.group(builders, reads=reads, writes=[ps_b[0], ps_b[1], ps_b[2]])
                    for (o, pi, po, wdt) in [(0, 0, 0, 512), (512, 1, 0, 512),
                                             (1024, 2, 0, 384)]:
                        kb.op(kb.dve, lambda e, o=o, pi=pi, po=po, wdt=wdt:
                              e.scalar_tensor_tensor(out=probsT[:, o:o + wdt],
                                                     in0=psum[pi][:, po:po + wdt],
                                                     scalar=SM_SCALE, in1=mbuf[:, o:o + wdt],
                                                     op0=ALU.mult, op1=ALU.add),
                              reads=[ps_b[pi], mbuf_b], writes=[probsT_b])
                    kb.op(kb.dve, lambda e: e.reduce_max(out=red[:, 0:1], in_=probsT[:],
                                                         axis=AX.X),
                          reads=[probsT_b], writes=[red_b])
                    kb.op(kb.dve, lambda e: e.tensor_scalar_mul(out=red[:, 1:2],
                                                                in0=red[:, 0:1],
                                                                scalar1=-1.0),
                          reads=[red_b], writes=[red_b])
                    kb.op(kb.act, lambda e: e.activation(out=probs[:], in_=probsT[:],
                                                         func=AF.Exp, bias=red[:, 1:2],
                                                         accum_out=red[:, 2:3]),
                          reads=[probsT_b, red_b], writes=[probs_b, red_b])
                    kb.op(kb.dve, lambda e: e.reciprocal(out=red[:, 3:4], in_=red[:, 2:3]),
                          reads=[red_b], writes=[red_b])
                    kb.op(kb.dve, lambda e: e.tensor_scalar_mul(out=probs[:], in0=probs[:],
                                                                scalar1=red[:, 3:4]),
                          reads=[probs_b, red_b], writes=[probs_b])
                    dmaT(probsT[:].rearrange("p (c t) -> p c t", t=128), probs[:],
                         reads=[probs_b], writes=[probsT_b], stream="trP")
                    builders, reads = [], [probsT_b]
                    for ci, st in enumerate(starts):
                        ti = st // 128
                        builders.append(lambda e, ci=ci, ti=ti, kv=kv:
                                        e.matmul(psum[3][:, 0:128],
                                                 vBv[:, 1, ti, kv * 128:(kv + 1) * 128],
                                                 probsT[:, 128 * ci:128 * (ci + 1)],
                                                 start=(ci == 0), stop=(ci == 10)))
                        reads.append(arB_b[16 + ti])
                    kb.group(builders, reads=reads, writes=[ps_b[3]])
                    kb.op(kb.act, lambda e, h=h, p4=p4: e.activation(
                        out=vA[:, h, 128 * p4:128 * (p4 + 1)], in_=psum[3][:, 0:128],
                        func=AF.Copy),
                        reads=[ps_b[3]], writes=[arA_b[h]])

            def d_evac(m, j, hh=hh):
                dma(kb.spq, xst[2][:, 0:512],
                    x_own[128 * m:128 * (m + 1), 512 * hh:512 * (hh + 1)],
                    reads=[xown_b], writes=[xst_b[2]], stream="xst2")
                kb.op(kb.dve, lambda e, m=m, j=j: e.scalar_tensor_tensor(
                    out=wrk[0][:, 0:512], in0=psum[j][:, 0:512], scalar=c_bda[:, m:m + 1],
                    op0=ALU.add, in1=xst[2][:, 0:512], op1=ALU.add),
                    reads=[ps_b[j], xst_b[2], const_b], writes=[wrk_b[0]])
                dma(kb.gpq, dst[128 * m:128 * (m + 1), 512 * hh:512 * (hh + 1)],
                    wrk[0][:, 0:512], reads=[wrk_b[0]], writes=[dst_b[m]],
                    stream=dst_stream)

            kb.fence(kb.pe, arB_b)
            gemm1(lambda k, m0, mw: wdaT[128 * k:128 * (k + 1), 128 * m0:128 * (m0 + mw)],
                  32, list(range(32)),
                  lambda k: (vA[:, k, :], arA_b[k]),
                  d_evac)

    def ffn(src, src_b, dst, dst_b, dst_stream):
        for hh in range(2):
            o5 = 512 * hh
            layernorm(lambda c, o5=o5: src[128 * c:128 * (c + 1), o5:o5 + 512],
                      lambda c: src_b[c])
            if debug == "xn":
                for c in range(32):
                    dma(kb.gpq, dst[128 * c:128 * (c + 1), o5:o5 + 512], vA[:, c, :],
                        reads=[arA_b[c]], writes=[dst_b[c]], stream="dump")
                continue
            for c in range(32):
                dmaT(vBx[:, 0:4, 128 * c:128 * (c + 1)], vA[:, c, :],
                     reads=[arA_b[c]],
                     writes=[arB_b[4 * ti + c // 8] for ti in range(4)], stream="trX")
            kb.barrier(kb.act, "trX", arB_b[0:16])
            for path in range(2):
                def qsink(m, s):
                    dmaT(vBq[:, 1, 8 * s:8 * s + 8, 128 * m:128 * (m + 1)],
                         qbf[:], reads=[qbf_b],
                         writes=[arB_b[16 + 4 * s + i] for i in range(4)], stream="trQ")
                quant_all(lambda m: vBx[:, m, :],
                          lambda m, s: [arB_b[4 * m + s]],
                          4, D, invg_in if path == 0 else invu_in, 4 * path, qsink)
                kb.fence(kb.dve, arB_b[0:16])
                kb.barrier(kb.act, "trQ", arB_b[16:32])
                scale_row_rep(scq[:, 4 * path:4 * path + 4], 4, srep[path], srep_b[path])
                if debug == "qg" and path == 0:
                    for k in range(32):
                        dma(kb.gpq, dst[128 * k:128 * (k + 1), o5:o5 + 512],
                            vBq[:, 1, k, :], reads=[arB_b[16 + k // 2]],
                            writes=[dst_b[k]], stream="dump")
                    break

                def gu_evac(m, j, path=path):
                    if debug == "gacc" and path == 0:
                        kb.op(kb.dve, lambda e, j=j: e.tensor_copy(
                            out=wrk[0][:, 0:512], in_=psum[j][:, 0:512]),
                            reads=[ps_b[j]], writes=[wrk_b[0]])
                        dma(kb.gpq, dst[128 * m:128 * (m + 1), o5:o5 + 512],
                            wrk[0][:, 0:512], reads=[wrk_b[0]], writes=[dst_b[m]],
                            stream="dump")
                        return
                    kb.op(kb.act, lambda e, m=m, j=j, path=path: e.activation(
                        out=wrk[0][:, 0:512], in_=psum[j][:, 0:512], func=AF.Identity,
                        scale=(c_swg if path == 0 else c_swu)[:, m:m + 1]),
                        reads=[ps_b[j], const_b], writes=[wrk_b[0]])
                    kb.op(kb.dve, lambda e, path=path: e.tensor_tensor(
                        out=wrk[1][:, 0:512], in0=wrk[0][:, 0:512], in1=srep[path][:],
                        op=ALU.mult),
                        reads=[wrk_b[0], srep_b[path]], writes=[wrk_b[1]])
                    if path == 0:
                        kb.op(kb.act, lambda e: e.activation(
                            out=wrk[0][:, 0:512], in_=wrk[1][:, 0:512], func=AF.Silu),
                            reads=[wrk_b[1]], writes=[wrk_b[0]])
                        kb.op(kb.dve, lambda e, m=m: e.tensor_copy(
                            out=vC5[:, m, :], in_=wrk[0][:, 0:512]),
                            reads=[wrk_b[0]], writes=[arC_b[m // 2]])
                        kb.op(kb.dve, lambda e, m=m: e.tensor_tensor(
                            out=vA[:, 16 + m, :], in0=wrk[0][:, 0:512], in1=vC5[:, m, :],
                            op=ALU.subtract),
                            reads=[wrk_b[0], arC_b[m // 2]], writes=[arA_b[16 + m]])
                    else:
                        kb.op(kb.dve, lambda e, m=m: e.tensor_tensor(
                            out=wrk[0][:, 0:512], in0=vC5[:, m, :],
                            in1=wrk[1][:, 0:512], op=ALU.mult),
                            reads=[arC_b[m // 2], wrk_b[1]], writes=[wrk_b[0]])
                        kb.op(kb.dve, lambda e, m=m: e.tensor_tensor(
                            out=wrk[2][:, 0:512], in0=vA[:, 16 + m, :],
                            in1=wrk[1][:, 0:512], op=ALU.mult),
                            reads=[arA_b[16 + m], wrk_b[1]], writes=[wrk_b[2]])
                        kb.op(kb.dve, lambda e: e.tensor_tensor(
                            out=wrk[0][:, 0:512], in0=wrk[0][:, 0:512],
                            in1=wrk[2][:, 0:512], op=ALU.add),
                            reads=[wrk_b[0], wrk_b[2]], writes=[wrk_b[0]])
                        kb.op(kb.dve, lambda e, m=m: e.tensor_scalar(
                            out=vC5[:, m, :], in0=wrk[0][:, 0:512], scalar1=10.0,
                            scalar2=-10.0, op0=ALU.min, op1=ALU.max),
                            reads=[wrk_b[0]], writes=[arC_b[m // 2]])

                gemm1(lambda k, m0, mw, path=path:
                      (wgT_in if path == 0 else wuT_in)[128 * k:128 * (k + 1),
                                                        128 * m0:128 * (m0 + mw)],
                      32, list(range(16)),
                      lambda k: (vBq[:, 1, k, :], arB_b[16 + k // 2]), gu_evac)
            if debug == "qg":
                continue
            if debug in ("gate", "gacc"):
                for m in range(16):
                    dma(kb.gpq, dst[128 * m:128 * (m + 1), o5:o5 + 512], vC5[:, m, :],
                        reads=[arC_b[m // 2]], writes=[dst_b[m]], stream="dump")
                continue
            kb.fence(kb.pe, arB_b[0:16])
            for c in range(16):
                dmaT(vBit[:, 0, 0:4, 128 * c:128 * (c + 1)], vC5[:, c, :],
                     reads=[arC_b[c // 2]],
                     writes=[arB_b[2 * ti + c // 8] for ti in range(4)], stream="trI")
            kb.barrier(kb.act, "trI", arB_b[0:8])
            def qsinki(m, s):
                dmaT(vBi[:, 1, 8 * s:8 * s + 8, 128 * m:128 * (m + 1)], qbf[:],
                     reads=[qbf_b], writes=[arB_b[8 + 4 * s + i] for i in range(4)],
                     stream="trJ")
            quant_all(lambda m: vBit[:, 0, m, :],
                      lambda m, s: [arB_b[2 * m + s]],
                      4, IFF, invi_in, 8, qsinki)
            kb.fence(kb.dve, arB_b[0:8])
            kb.barrier(kb.act, "trJ", arB_b[8:16])
            scale_row_rep(scq[:, 8:12], 4, srep[0], srep_b[0])
            if debug == "qi":
                for k in range(16):
                    dma(kb.gpq, dst[128 * k:128 * (k + 1), o5:o5 + 512],
                        vBi[:, 1, k, :], reads=[arB_b[8 + k // 2]], writes=[dst_b[k]],
                        stream="dump")
                continue

            def d_evac(m, j, hh=hh):
                kb.op(kb.act, lambda e, m=m, j=j: e.activation(
                    out=wrk[0][:, 0:512], in_=psum[j][:, 0:512], func=AF.Identity,
                    scale=c_swd[:, m:m + 1]),
                    reads=[ps_b[j], const_b], writes=[wrk_b[0]])
                dma(kb.spq, xst[2][:, 0:512], src[128 * m:128 * (m + 1), o5:o5 + 512],
                    reads=[src_b[m]], writes=[xst_b[2]], stream="xst2")
                kb.op(kb.dve, lambda e: e.tensor_tensor(
                    out=wrk[1][:, 0:512], in0=wrk[0][:, 0:512], in1=srep[0][:],
                    op=ALU.mult), reads=[wrk_b[0], srep_b[0]], writes=[wrk_b[1]])
                kb.op(kb.dve, lambda e: e.tensor_tensor(
                    out=wrk[0][:, 0:512], in0=wrk[1][:, 0:512], in1=xst[2][:, 0:512],
                    op=ALU.add), reads=[wrk_b[1], xst_b[2]], writes=[wrk_b[0]])
                dma(kb.gpq, dst[128 * m:128 * (m + 1), o5:o5 + 512], wrk[0][:, 0:512],
                    reads=[wrk_b[0]], writes=[dst_b[m]], stream=dst_stream)

            gemm1(lambda k, m0, mw: wdfT_in[128 * k:128 * (k + 1), 128 * m0:128 * (m0 + mw)],
                  16, list(range(32)), lambda k: (vBi[:, 1, k, :], arB_b[8 + k // 2]),
                  d_evac)

    def attn_b(src, src_b, dst, dst_b, dst_stream):
        for hh in range(2):
            o5 = 512 * hh
            layernorm(lambda c, o5=o5: src[128 * c:128 * (c + 1), o5:o5 + 512],
                      lambda c: src_b[c])
            dma(kb.spq, cos_sb[:], cosb_in[:, o5:o5 + 512], writes=[cosb_b], stream="cos")
            dma(kb.spq, sin_sb[:], sinb_in[:, o5:o5 + 512], writes=[cosb_b], stream="cos")

            def kvd_evac(m, j, hh=hh, o5=o5):
                if m < 8:
                    kv = m
                    rope_evac(j, 512, None, cos_sb[:], sin_sb[:],
                              vC[:, kv, o5:o5 + 512], arC_b[kv])
                elif m < 16:
                    kv = m - 8
                    kb.op(kb.act, lambda e, j=j: e.activation(
                        out=tile_a[:, 0:512], in_=psum[j][:, 0:512], func=AF.Copy),
                        reads=[ps_b[j]], writes=[tab_b[0]])
                    ti0 = 4 * hh
                    dmaT(vD[:, 12 + ti0:12 + ti0 + 4, kv * 128:(kv + 1) * 128], tile_a[:, 0:512],
                         reads=[tab_b[0]], writes=[arB_b[12 + ti0 + i] for i in range(4)],
                         stream="trV")
                else:
                    kb.op(kb.act, lambda e, j=j, o5=o5: e.activation(
                        out=dtfm[:, o5:o5 + 512], in_=psum[j][:, 0:512], func=AF.Copy),
                        reads=[ps_b[j]], writes=[dtfm_b])

            gemm1(lambda k, m0, mw: wkvdT[128 * k:128 * (k + 1), 128 * m0:128 * (m0 + mw)],
                  32, list(range(17)), lambda k: (vA[:, k, :], arA_b[k]), kvd_evac)

            def qb_evac(m, j, o5=o5):
                rope_evac(j, 512, None, cos_sb[:], sin_sb[:],
                          tile_b[:, 0:512], tab_b[1])
                dma(kb.gpq, qb_hbm[128 * m:128 * (m + 1), o5:o5 + 512], tile_b[:, 0:512],
                    reads=[tab_b[1]], writes=[qb_b], stream="qb")

            gemm1(lambda k, m0, mw: wqbT[128 * k:128 * (k + 1), 128 * m0:128 * (m0 + mw)],
                  32, list(range(32)), lambda k: (vA[:, k, :], arA_b[k]), qb_evac)

            def gb_evac(m, j, hh=hh, o5=o5):
                kb.op(kb.act, lambda e, j=j: e.activation(
                    out=tile_a[:, 0:512], in_=psum[j][:, 0:512], func=AF.Silu),
                    reads=[ps_b[j]], writes=[tab_b[0]])
                dmaT(tile_b[:, 0:512].rearrange("p (ti t) -> p ti t", t=128),
                     tile_a[:, 0:512], reads=[tab_b[0]], writes=[tab_b[1]], stream="trG")
                dma(kb.gpq,
                    gate_hbm[o5:o5 + 512, 128 * m:128 * (m + 1)].rearrange(
                        "(ti p) d -> p ti d", p=128),
                    tile_b[:, 0:512].rearrange("p (ti t) -> p ti t", t=128),
                    reads=[tab_b[1]], writes=[gate_b], stream="gate")

            gemm1(lambda k, m0, mw: wgbT[128 * k:128 * (k + 1), 128 * m0:128 * (m0 + mw)],
                  32, list(range(32)), lambda k: (vA[:, k, :], arA_b[k]), gb_evac)

        for kv in range(NKV):
            dmaT(vBkt[:, 0:8, kv * 128:(kv + 1) * 128], vC[:, kv, 0:1024],
                 reads=[arC_b[kv]], writes=[arB_b[i] for i in range(8)], stream="trK")
        kb.barrier(kb.act, "trK", arB_b[0:8])
        kb.barrier(kb.act, "trV", arB_b[12:20])

        for n in range(NCH):
            kb.group([lambda e, n=n: e.transpose(psum[4][:, 0:128],
                                                 dtfm[:, 128 * n:128 * (n + 1)], c_ident[:])],
                     reads=[dtfm_b, const_b], writes=[ps_b[4]])
            kb.op(kb.dve, lambda e: e.tensor_tensor(out=wrk[0][:, 0:32],
                                                    in0=psum[4][:, 0:32], in1=c_dtb[:, 0:32],
                                                    op=ALU.add),
                  reads=[ps_b[4], const_b], writes=[wrk_b[0]])
            kb.op(kb.act, lambda e: e.activation(out=wrk[1][:, 0:32], in_=wrk[0][:, 0:32],
                                                 func=AF.Exp),
                  reads=[wrk_b[0]], writes=[wrk_b[1]])
            kb.op(kb.dve, lambda e: e.tensor_scalar_add(out=wrk[1][:, 0:32],
                                                        in0=wrk[1][:, 0:32], scalar1=1.0),
                  reads=[wrk_b[1]], writes=[wrk_b[1]])
            kb.op(kb.act, lambda e, n=n: e.activation(out=dtcols[:, 32 * n:32 * (n + 1)],
                                                      in_=wrk[1][:, 0:32], func=AF.Ln),
                  reads=[wrk_b[1]], writes=[g_b])
            kb.op(kb.dve, lambda e, n=n: e.tensor_tensor(out=wrk[0][:, 0:32],
                                                         in0=dtcols[:, 32 * n:32 * (n + 1)],
                                                         in1=c_negA[:, 0:32], op=ALU.mult),
                  reads=[g_b, const_b], writes=[wrk_b[0]])
            kb.group([lambda e: e.matmul(psum[5][:, 0:32], c_ltri[:], wrk[0][:, 0:32],
                                         start=True, stop=True),
                      lambda e: e.matmul(psum[5][:, 64:96], c_ones[:], wrk[0][:, 0:32],
                                         start=True, stop=True)],
                     reads=[wrk_b[0], const_b], writes=[ps_b[5]])
            kb.op(kb.dve, lambda e, n=n: e.tensor_copy(out=gcols[:, 96 * n:96 * n + 32],
                                                       in_=psum[5][:, 0:32]),
                  reads=[ps_b[5]], writes=[g_b])
            kb.op(kb.dve, lambda e, n=n: e.tensor_copy(out=gcols[:, 96 * n + 32:96 * n + 64],
                                                       in_=psum[5][:, 64:96]),
                  reads=[ps_b[5]], writes=[g_b])
            kb.op(kb.dve, lambda e, n=n: e.tensor_tensor(
                out=gcols[:, 96 * n + 64:96 * n + 96], in0=gcols[:, 96 * n + 32:96 * n + 64],
                in1=gcols[:, 96 * n:96 * n + 32], op=ALU.subtract),
                reads=[g_b], writes=[g_b])
            kb.op(kb.act, lambda e, n=n: e.activation(out=egi[:, 32 * n:32 * (n + 1)],
                                                      in_=gcols[:, 96 * n:96 * n + 32],
                                                      func=AF.Exp),
                  reads=[g_b], writes=[g_b])
            kb.op(kb.act, lambda e, n=n: e.activation(out=edec[:, 32 * n:32 * (n + 1)],
                                                      in_=gcols[:, 96 * n + 64:96 * n + 96],
                                                      func=AF.Exp),
                  reads=[g_b], writes=[g_b])
            kb.op(kb.act, lambda e, n=n: e.activation(out=ecd[:, 32 * n:32 * (n + 1)],
                                                      in_=gcols[:, 96 * n + 32:96 * n + 64],
                                                      func=AF.Exp),
                  reads=[g_b], writes=[g_b])
            kb.group([lambda e, n=n: e.transpose(psum[4][0:32, 0:128],
                                                 gcols[:, 96 * n:96 * n + 32], c_ident[:])],
                     reads=[g_b, const_b], writes=[ps_b[4]])
            kb.op(kb.dve, lambda e, n=n: e.tensor_copy(out=grows[0:32, 128 * n:128 * (n + 1)],
                                                       in_=psum[4][0:32, 0:128]),
                  reads=[ps_b[4]], writes=[g_b])

        kb.op(kb.dve, lambda e: e.memset(state[:], 0.0), writes=[state_b])
        for h in range(NQH):
            kv = h >> 2
            for n in range(NCH):
                kb.op(kb.dve, lambda e, n=n, kv=kv, h=h: e.tensor_scalar_mul(
                    out=tile_a[:, 0:128], in0=vD[:, 12 + n, kv * 128:(kv + 1) * 128],
                    scalar1=dtcols[:, 32 * n + h:32 * n + h + 1]),
                    reads=[arB_b[12 + n], g_b], writes=[tab_b[0]])
                kb.op(kb.dve, lambda e, n=n, kv=kv, h=h: e.tensor_scalar_mul(
                    out=tile_b[:, 0:128], in0=vBkt[:, n, kv * 128:(kv + 1) * 128],
                    scalar1=edec[:, 32 * n + h:32 * n + h + 1]),
                    reads=[arB_b[n], g_b], writes=[tab_b[1]])
                kb.group([lambda e: e.matmul(psum[4][:, 0:128], tile_b[:, 0:128],
                                             tile_a[:, 0:128], start=True, stop=True)],
                         reads=[tab_b[0], tab_b[1]], writes=[ps_b[4]])
                kb.op(kb.dve, lambda e, n=n, h=h: e.scalar_tensor_tensor(
                    out=state[:, 128 * h:128 * (h + 1)], in0=state[:, 128 * h:128 * (h + 1)],
                    scalar=ecd[:, 32 * n + h:32 * n + h + 1], in1=psum[4][:, 0:128],
                    op0=ALU.mult, op1=ALU.add),
                    reads=[ps_b[4], g_b, state_b], writes=[state_b])
        dma(kb.gpq, cc_in.rearrange("(h d) e -> d h e", d=128),
            state[:].rearrange("p (h e) -> p h e", e=HD),
            reads=[state_b], writes=[ccin_b], stream="ccin")
        kb.op(kb.gpq, lambda e: e.collective_compute(
            "AllGather", ALU.bypass, replica_groups=[[0, 1], [2, 3], [4, 5], [6, 7]],
            ins=[cc_in], outs=[cc_out]), reads=[ccin_b], writes=[ccout_b])
        dma(kb.gpq, S0bf.rearrange("p (h e) -> p h e", e=HD),
            cc_out[0:D, :].rearrange("(h d) e -> d h e", d=128),
            reads=[ccout_b], writes=S0_bufs, stream="s0")
        kb.barrier(kb.gpq, "s0", S0_bufs)

        kb.op(kb.dve, lambda e: e.memset(cumg[:], 0.0), writes=[g_b])
        kb.op(kb.dve, lambda e: e.memset(state[:], 0.0), writes=[state_b])
        for n in range(NCH):
            for hg in range(4):
                dma(kb.act, statr[0:1, 0:1024],
                    grows[8 * hg:8 * hg + 8, 128 * n:128 * (n + 1)],
                    reads=[g_b], writes=[statr_b], stream="gath")
                for hl in range(8):
                    h = 8 * hg + hl
                    kv = h >> 2
                    gcol = gcols[:, 96 * n + h:96 * n + h + 1]
                    gtot = gcols[:, 96 * n + 32 + h:96 * n + 33 + h]
                    kb.op(kb.dve, lambda e, hl=hl: e.tensor_copy(
                        out=rowbuf[0:1, 0:128], in_=statr[0:1, 128 * hl:128 * (hl + 1)]),
                        reads=[statr_b], writes=[rowbuf_b])
                    kb.group([lambda e: e.matmul(psum[5][:, 0:128], c_e0[:],
                                                 rowbuf[:, 0:128], start=True, stop=True)],
                             reads=[rowbuf_b, const_b], writes=[ps_b[5]])
                    dma(kb.spq, qtile[:], qb_hbm[128 * h:128 * (h + 1),
                                                 128 * n:128 * (n + 1)],
                        reads=[qb_b], writes=[qtile_b], stream="qtile")
                    kb.op(kb.act, lambda e: e.activation(
                        out=wrk[2][:, 0:128], in_=psum[5][:, 0:128], func=AF.Exp),
                        reads=[ps_b[5]], writes=[wrk_b[2]])
                    kb.op(kb.dve, lambda e: e.tensor_tensor(
                        out=tile_c[:, 0:128], in0=qtile[:], in1=wrk[2][:, 0:128],
                        op=ALU.mult),
                        reads=[qtile_b, wrk_b[2]], writes=[tab_b[3]])
                    kb.op(kb.dve, lambda e, gcol=gcol: e.scalar_tensor_tensor(
                        out=wrk[0][:, 0:128], in0=psum[5][:, 0:128], scalar=gcol,
                        in1=c_maskTb[:], op0=ALU.subtract, op1=ALU.add),
                        reads=[ps_b[5], g_b, const_b], writes=[wrk_b[0]])
                    kb.op(kb.act, lambda e: e.activation(out=tile_d[:], in_=wrk[0][:, 0:128],
                                                         func=AF.Exp),
                          reads=[wrk_b[0]], writes=[tab_b[4]])
                    kb.group([lambda e, n=n, kv=kv: e.matmul(
                        psum[6][:, 0:128], vC[:, kv, 128 * n:128 * (n + 1)], qtile[:],
                        start=True, stop=True)],
                        reads=[arC_b[kv], qtile_b], writes=[ps_b[6]])
                    kb.op(kb.dve, lambda e: e.tensor_tensor(
                        out=tile_e[:], in0=psum[6][:, 0:128], in1=tile_d[:], op=ALU.mult),
                        reads=[ps_b[6], tab_b[4]], writes=[tab_b[5]])
                    kb.op(kb.dve, lambda e, n=n, kv=kv, h=h: e.tensor_scalar_mul(
                        out=tile_a[:, 0:128], in0=vD[:, 12 + n, kv * 128:(kv + 1) * 128],
                        scalar1=dtcols[:, 32 * n + h:32 * n + h + 1]),
                        reads=[arB_b[12 + n], g_b], writes=[tab_b[0]])
                    builders = [lambda e: e.matmul(psum[7][:, 0:128], tile_e[:],
                                                   tile_a[:, 0:128], start=True,
                                                   stop=(n == 0))]
                    yreads = [tab_b[5], tab_b[0]]
                    if n > 0:
                        builders.append(lambda e, h=h: e.matmul(
                            psum[7][:, 0:128], tile_c[:, 0:128], vBpr[:, 64 + h, :],
                            start=False, stop=True))
                        yreads += [tab_b[3], arB_b[8 + h // 8]]
                    kb.group(builders, reads=yreads, writes=[ps_b[7]])
                    kb.group([lambda e, h=h: e.matmul(psum[5][:, 128:256],
                                                      tile_c[:, 0:128],
                                                      S0bf[:, 128 * h:128 * (h + 1)],
                                                      start=True, stop=True)],
                             reads=[tab_b[3], S0_bufs[h // 4]], writes=[ps_b[5]])
                    kb.op(kb.dve, lambda e, n=n, kv=kv, h=h: e.tensor_scalar_mul(
                        out=tile_b[:, 0:128], in0=vBkt[:, n, kv * 128:(kv + 1) * 128],
                        scalar1=edec[:, 32 * n + h:32 * n + h + 1]),
                        reads=[arB_b[n], g_b], writes=[tab_b[1]])
                    kb.group([lambda e: e.matmul(psum[4][:, 0:128], tile_b[:, 0:128],
                                                 tile_a[:, 0:128], start=True, stop=True)],
                             reads=[tab_b[0], tab_b[1]], writes=[ps_b[4]])
                    kb.op(kb.act, lambda e, h=h: e.activation(out=sco[:, 1:2],
                                                              in_=cumg[:, h:h + 1],
                                                              func=AF.Exp),
                          reads=[g_b], writes=[g_b])
                    kb.op(kb.dve, lambda e: e.tensor_tensor(out=sco[:, 2:3], in0=sco[:, 1:2],
                                                            in1=c_uses0[:], op=ALU.mult),
                          reads=[g_b, const_b], writes=[g_b])
                    kb.op(kb.dve, lambda e: e.tensor_copy(out=wrk[1][:, 0:128],
                                                          in_=psum[7][:, 0:128]),
                          reads=[ps_b[7]], writes=[wrk_b[1]])
                    kb.op(kb.dve, lambda e: e.scalar_tensor_tensor(
                        out=wrk[1][:, 0:128], in0=psum[5][:, 128:256], scalar=sco[:, 2:3],
                        in1=wrk[1][:, 0:128], op0=ALU.mult, op1=ALU.add),
                        reads=[ps_b[5], g_b, wrk_b[1]], writes=[wrk_b[1]])
                    kb.op(kb.dve, lambda e, n=n, h=h: e.scalar_tensor_tensor(
                        out=state[:, 128 * h:128 * (h + 1)],
                        in0=state[:, 128 * h:128 * (h + 1)],
                        scalar=ecd[:, 32 * n + h:32 * n + h + 1], in1=psum[4][:, 0:128],
                        op0=ALU.mult, op1=ALU.add),
                        reads=[ps_b[4], g_b, state_b], writes=[state_b])
                    kb.op(kb.act, lambda e, h=h: e.activation(
                        out=vBpr[:, 64 + h, :], in_=state[:, 128 * h:128 * (h + 1)],
                        func=AF.Copy), reads=[state_b], writes=[arB_b[8 + h // 8]])
                    kb.op(kb.dve, lambda e, h=h, gtot=gtot: e.tensor_tensor(
                        out=cumg[:, h:h + 1], in0=cumg[:, h:h + 1], in1=gtot, op=ALU.add),
                        reads=[g_b], writes=[g_b])
                    kb.op(kb.act, lambda e: e.activation(out=wrk[2][:, 0:128],
                                                         in_=wrk[1][:, 0:128],
                                                         func=AF.Square),
                          reads=[wrk_b[1]], writes=[wrk_b[2]])
                    kb.op(kb.dve, lambda e: e.reduce_sum(out=sco[:, 3:4],
                                                         in_=wrk[2][:, 0:128], axis=AX.X),
                          reads=[wrk_b[2]], writes=[g_b])
                    kb.op(kb.dve, lambda e: e.tensor_scalar(
                        out=sco[:, 4:5], in0=sco[:, 3:4], scalar1=1.0 / HD,
                        scalar2=EPS_RMS, op0=ALU.mult, op1=ALU.add),
                        reads=[g_b], writes=[g_b])
                    kb.op(kb.act, lambda e: e.activation(out=sco[:, 4:5], in_=sco[:, 4:5],
                                                         func=AF.Ln),
                          reads=[g_b], writes=[g_b])
                    kb.op(kb.dve, lambda e: e.tensor_scalar_mul(out=sco[:, 4:5],
                                                                in0=sco[:, 4:5],
                                                                scalar1=-0.5),
                          reads=[g_b], writes=[g_b])
                    kb.op(kb.act, lambda e: e.activation(out=sco[:, 4:5], in_=sco[:, 4:5],
                                                         func=AF.Exp),
                          reads=[g_b], writes=[g_b])
                    kb.op(kb.dve, lambda e: e.scalar_tensor_tensor(
                        out=wrk[2][:, 0:128], in0=wrk[1][:, 0:128], scalar=sco[:, 4:5],
                        in1=c_gnw[:], op0=ALU.mult, op1=ALU.mult),
                        reads=[wrk_b[1], g_b, const_b], writes=[wrk_b[2]])
                    dma(kb.spq, tile_f[:],
                        gate_hbm[128 * n:128 * (n + 1), 128 * h:128 * (h + 1)],
                        reads=[gate_b], writes=[tab_b[2]], stream="gtile")
                    kb.op(kb.dve, lambda e: e.tensor_tensor(out=tile_e[:],
                                                            in0=wrk[2][:, 0:128],
                                                            in1=tile_f[:], op=ALU.mult),
                          reads=[wrk_b[2], tab_b[2]], writes=[tab_b[5]])
                    dmaT(tile_d[:], tile_e[:], reads=[tab_b[5]], writes=[tab_b[4]],
                         stream="trY")
                    dma(kb.gpq, y_hbm[128 * h:128 * (h + 1), 128 * n:128 * (n + 1)],
                        tile_d[:], reads=[tab_b[4]], writes=[y_b], stream="y")

        for hh in range(2):
            o5 = 512 * hh

            def wo_evac(m, j, hh=hh, o5=o5):
                dma(kb.spq, xst[2][:, 0:512], src[128 * m:128 * (m + 1), o5:o5 + 512],
                    reads=[src_b[m]], writes=[xst_b[2]], stream="xst2")
                kb.op(kb.dve, lambda e, j=j: e.tensor_tensor(
                    out=wrk[0][:, 0:512], in0=psum[j][:, 0:512], in1=xst[2][:, 0:512],
                    op=ALU.add), reads=[ps_b[j], xst_b[2]], writes=[wrk_b[0]])
                dma(kb.gpq, dst[128 * m:128 * (m + 1), o5:o5 + 512], wrk[0][:, 0:512],
                    reads=[wrk_b[0]], writes=[dst_b[m]], stream=dst_stream)

            def rhs_fn(k, hh=hh, o5=o5):
                i = k % 2
                dma(kb.spq, yb[i][:, 0:512], y_hbm[128 * k:128 * (k + 1), o5:o5 + 512],
                    reads=[y_b], writes=[yb_b[i]], stream=f"yb{i}")
                return (yb[i][:, 0:512], yb_b[i])

            gemm1(lambda k, m0, mw: wobT[128 * k:128 * (k + 1), 128 * m0:128 * (m0 + mw)],
                  32, list(range(32)), rhs_fn, wo_evac)


    order = []
    if nsub == -2:
        order = [("f", ffn)]
    if nsub == -3:
        order = [("b", attn_b)]
    if nsub >= 1:
        order.append(("a", attn_a))
    if nsub >= 2:
        order.append(("f", ffn))
    if nsub >= 3:
        order.append(("b", attn_b))
    if nsub >= 4:
        order.append(("f", ffn))

    cur_src, cur_src_b = x_own, [xown_b] * 32
    ping = 0
    for i, (kind, fn) in enumerate(order):
        last = (i == len(order) - 1)
        dst = out_fm if last else xs[ping]
        dst_b = out_b if last else xs_b[ping]
        dst_stream = "out" if last else f"xs{ping}"
        if kind == "a":
            attn_a(dst, dst_b, dst_stream)
        else:
            fn(cur_src, cur_src_b, dst, dst_b, dst_stream)
        kb.barrier(kb.gpq, dst_stream, dst_b)
        cur_src, cur_src_b = dst, dst_b
        ping ^= 1

    final_waits = sorted({b.w for b in out_b if b.w is not None})

    with nc.Block() as block:
        with ExitStack() as sems:
            kb.make_sems(nc, sems)
            kb.finalize(block, final_waits)

    es.close()
    return nc


_NC_CACHE = {}


def kernel(**inputs):
    in_maps = host_prep(inputs)
    key = NSUB
    if key not in _NC_CACHE:
        _NC_CACHE[key] = build_nc(NSUB)
    nc = _NC_CACHE[key]
    res = run_bass_kernel_spmd(nc, in_maps, core_ids=list(range(8)))
    out = np.zeros((B, S, D), np.float32)
    for c in range(8):
        b, h = c >> 1, c & 1
        out[b, h * T:(h + 1) * T] = np.asarray(res.results[c]["out_fm"], np.float32).T
    return out


# version 05
# speedup vs baseline: 1.1083x; 1.1083x over previous
import numpy as np
import ml_dtypes
from contextlib import ExitStack

import concourse.bass as bass
import concourse.mybir as mybir
from concourse.bass_utils import run_bass_kernel_spmd

F32 = mybir.dt.float32
BF16 = mybir.dt.bfloat16
I32 = mybir.dt.int32
AF = mybir.ActivationFunctionType
ALU = mybir.AluOpType
AX = mybir.AxisListType

B, S, D = 4, 2048, 4096
T = 1024
NQH, NKV, HD = 32, 8, 128
BLK = 64
KVT = 2048
WIN = 960
NKC = 11
KEYS = NKC * 128
IFF = 2048
NCH = 8
EPS_LN = 1e-5
EPS_RMS = 1e-5
SM_SCALE = 1.0 / HD
NEG = -1e30
NSUB = 4
DEBUG = None


def _emask():
    nb = S // BLK
    qb = np.arange(nb)[:, None]
    kb = np.arange(nb)[None, :]
    bm = (kb <= qb) & ((qb - kb < 16) | ((kb + 1) % 8 == 0))
    em = np.kron(bm, np.ones((BLK, BLK), dtype=bool))
    i = np.arange(S)
    em &= i[:, None] >= i[None, :]
    return em


def _kchunk_starts(p):
    return [128 * p + 128 * c for c in range(9)] + [384, 896]


def _blk(wT, nk):
    K, M = wT.shape
    ng = M // 512
    return np.ascontiguousarray(
        wT[:, :512 * ng].reshape(nk, 128, ng, 512)
          .transpose(2, 0, 1, 3).reshape(ng * nk * 128, 512))


def host_prep(inputs):
    f32 = np.float32
    x = np.asarray(inputs["L0_attn_a_hidden_states"], f32)

    wqkv = np.asarray(inputs["L0_attn_a_w_qkv"], f32)
    bqkv = np.asarray(inputs["L0_attn_a_b_qkv"], f32)
    rows = np.arange(6144).reshape(NKV, 6, HD)
    q_rows = rows[:, :4].reshape(-1)
    kv_rows = np.concatenate([rows[:, 4].reshape(-1), rows[:, 5].reshape(-1)])
    m0 = dict(
        wqaT=_blk(wqkv[q_rows].T, 32),
        wkvaT=_blk(wqkv[kv_rows].T, 32),
        bqa=np.ascontiguousarray(bqkv[q_rows].reshape(NQH, HD).T),
        bkva=np.ascontiguousarray(bqkv[kv_rows].reshape(16, HD).T),
        wdaT=_blk(np.asarray(inputs["L0_attn_a_w_dense"], f32).T, 32),
        bda=np.ascontiguousarray(np.asarray(inputs["L0_attn_a_b_dense"], f32).reshape(32, HD).T),
        wgT=_blk(np.asarray(inputs["L0_ffn_w_gate"], f32).T, 32),
        wuT=_blk(np.asarray(inputs["L0_ffn_w_up"], f32).T, 32),
        wdfT=_blk(np.asarray(inputs["L0_ffn_w_down"], f32).T, 16),
        s_wg_c=np.ascontiguousarray(np.asarray(inputs["L0_ffn_s_wgate"], f32).reshape(16, HD).T),
        s_wu_c=np.ascontiguousarray(np.asarray(inputs["L0_ffn_s_wup"], f32).reshape(16, HD).T),
        s_wd_c=np.ascontiguousarray(np.asarray(inputs["L0_ffn_s_wdown"], f32).reshape(32, HD).T),
        inv_g_rep=np.broadcast_to(np.asarray(inputs["L0_ffn_inv_gate"], f32), (128, D)).copy(),
        inv_u_rep=np.broadcast_to(np.asarray(inputs["L0_ffn_inv_up"], f32), (128, D)).copy(),
        inv_i_rep=np.broadcast_to(np.asarray(inputs["L0_ffn_inv_inter"], f32), (128, IFF)).copy(),
        wqbT=_blk(np.asarray(inputs["L0_attn_b_Wq"], f32).T, 32),
        wgbT=_blk(np.asarray(inputs["L0_attn_b_Wg"], f32).T, 32),
        wobT=_blk(np.asarray(inputs["L0_attn_b_Wo"], f32).T, 32),
        dtb_rep=np.broadcast_to(np.asarray(inputs["L0_attn_b_dt_bias"], f32), (128, 32)).copy(),
        alog_rep=np.broadcast_to(np.asarray(inputs["L0_attn_b_A_log_bias"], f32), (128, 32)).copy(),
        gnw_rep=np.broadcast_to(np.asarray(inputs["L0_attn_b_gnorm_w"], f32), (128, HD)).copy(),
        ident=np.eye(128, dtype=f32),
    )
    winp = np.zeros((128, D), f32)
    winp[:32] = np.asarray(inputs["L0_attn_b_Win"], f32)
    wkvd_full = np.concatenate(
        [np.asarray(inputs["L0_attn_b_Wk"], f32),
         np.asarray(inputs["L0_attn_b_Wv"], f32), winp], 0).T
    m0["wkvdT"] = _blk(wkvd_full, 32)
    m0["wkvdT_t"] = np.ascontiguousarray(wkvd_full[:, 2048:])
    e0 = np.zeros((128, 128), f32)
    e0[0, :] = 1.0
    m0["e0"] = e0
    m0["ltri"] = (np.arange(128)[:, None] <= np.arange(128)[None, :]).astype(f32)
    m0["maskTb"] = np.where(np.arange(128)[None, :] >= np.arange(128)[:, None],
                            0.0, NEG).astype(f32)

    ifa = 1.0 / (1e6 ** (np.arange(0, HD, 2, dtype=np.float64) / HD))
    ifb = 1.0 / (1e4 ** (np.arange(0, HD, 2, dtype=np.float64) / HD))
    em = _emask()

    in_maps = []
    for c in range(8):
        b, h = c >> 1, c & 1
        m = dict(m0)
        g0 = (16 * h - 15) * BLK
        xall = np.zeros((KVT, D), f32)
        lo = max(0, -g0)
        hi = min(WIN + T, S - g0)
        xall[lo:hi] = x[b, g0 + lo:g0 + hi]
        m["x_all"] = _blk(np.ascontiguousarray(xall.T), 32)
        m["x_own"] = _blk(np.ascontiguousarray(x[b, h * T:(h + 1) * T].T), 32)

        pos = np.maximum(g0 + np.arange(KVT, dtype=np.float64), 0.0)
        ang = pos[None, :] * np.concatenate([ifa, ifa])[:, None]
        m["cosa"] = np.cos(ang).astype(f32)
        m["sina"] = np.sin(ang).astype(f32)

        masks = np.full((8, 128, KEYS), NEG, f32)
        for p in range(8):
            qg = h * T + 128 * p + np.arange(128)
            seen = set()
            for ci, st in enumerate(_kchunk_starts(p)):
                for kj in range(128):
                    kv = st + kj
                    g = g0 + kv
                    if g < 0 or g >= S or kv >= WIN + T or kv in seen:
                        continue
                    seen.add(kv)
                    masks[p, :, ci * 128 + kj] = np.where(em[qg, g], 0.0, NEG)
        m["maska"] = np.ascontiguousarray(
            masks.transpose(1, 0, 2).reshape(128, 8 * KEYS)).astype(ml_dtypes.bfloat16)

        posb = h * T + np.arange(T, dtype=np.float64)
        angb = posb[None, :] * np.concatenate([ifb, ifb])[:, None]
        m["cosb"] = np.cos(angb).astype(f32)
        m["sinb"] = np.sin(angb).astype(f32)
        m["use_s0"] = np.full((128, 1), float(h), f32)
        in_maps.append(m)
    return in_maps


class Buf:
    __slots__ = ("w", "r")

    def __init__(self):
        self.w = None
        self.r = []


class ET:
    def __init__(self, name, attr):
        self.name = name
        self.attr = attr
        self.cval = 0
        self.dval = {}
        self.hw = {}
        self.ops = []
        self.csem = None
        self.dsems = {}


class KB:
    def __init__(self):
        self.pe = ET("pe", "tensor")
        self.dve = ET("dve", "vector")
        self.act = ET("act", "scalar")
        self.gpq = ET("gpq", "gpsimd")
        self.spq = ET("spq", "sync")
        self.ets = {e.name: e for e in (self.pe, self.dve, self.act, self.gpq, self.spq)}

    def _sem(self, key):
        e = self.ets[key[0]]
        return e.csem if key[1] == "c" else e.dsems[key[2]]

    def _collect(self, et, reads, writes, extra):
        deps = list(extra)
        for b in reads:
            if b.w is not None:
                deps.append(b.w)
        for b in writes:
            deps.extend(b.r)
            if b.w is not None:
                deps.append(b.w)
        need = {}
        for key, v in deps:
            if et.hw.get(key, 0) >= v:
                continue
            need[key] = max(need.get(key, 0), v)
        for key, v in need.items():
            et.hw[key] = v
        return list(need.items())

    @staticmethod
    def _finish(tok, reads, writes):
        for b in reads:
            b.r.append(tok)
        for b in writes:
            b.w = tok
            b.r = []
        return tok

    def op(self, et, build, reads=(), writes=(), extra=(), dma=False, fuse=True,
           stream=None):
        waits = self._collect(et, reads, writes, extra)
        if dma:
            assert stream is not None
            et.dval[stream] = et.dval.get(stream, 0) + 16
            et.dsems.setdefault(stream, None)
            tok = ((et.name, "d", stream), et.dval[stream])
            fuse = False
        else:
            et.cval += 1
            tok = ((et.name, "c"), et.cval)
        kb = self

        def closure(eng, waits=waits, build=build, et=et, tok=tok, fuse=fuse,
                    dma=dma, stream=stream):
            std = waits[:-1] if (fuse and waits) else waits
            for key, v in std:
                eng.wait_ge(kb._sem(key), v)
            inst = build(eng)
            if fuse and waits:
                key, v = waits[-1]
                inst._wait_ge(kb._sem(key), v)
            inst.then_inc(et.csem if not dma else et.dsems[stream], 1 if not dma else 16)

        et.ops.append(closure)
        return self._finish(tok, reads, writes)

    def fence(self, et, bufs):
        tok = ((et.name, "c"), et.cval)
        for b in bufs:
            b.r.append(tok)

    def barrier(self, et, stream, bufs):
        v = et.dval.get(stream, 0)
        if v == 0:
            return
        tok = ((et.name, "d", stream), v)
        for b in bufs:
            if b.w is not None and b.w[0] == tok[0]:
                b.w = tok

    def group(self, builders, reads=(), writes=(), extra=()):
        et = self.pe
        waits = self._collect(et, reads, writes, extra)
        et.cval += 1
        tok = ((et.name, "c"), et.cval)
        kb = self

        def closure(eng, waits=waits, builders=builders, et=et):
            for key, v in waits:
                eng.wait_ge(kb._sem(key), v)
            inst = None
            for b in builders:
                inst = b(eng)
            inst.then_inc(et.csem, 1)

        et.ops.append(closure)
        return self._finish(tok, reads, writes)

    def finalize(self, block, final_waits):
        kb = self
        for et in self.ets.values():
            def run(eng, ops=et.ops):
                for f in ops:
                    f(eng)
            getattr(block, et.attr)(run)

        def tail(eng):
            for key, v in final_waits:
                eng.wait_ge(kb._sem(key), v)
        block.sync(tail)

    def make_sems(self, nc, sems):
        for et in self.ets.values():
            et.csem = sems.enter_context(nc.semaphore(f"s_{et.name}_c"))
            for st in list(et.dsems):
                et.dsems[st] = sems.enter_context(nc.semaphore(f"s_{et.name}_{st}"))


def build_nc(nsub=None, debug=None):
    if nsub is None:
        nsub = NSUB
    if debug is None:
        debug = DEBUG
    nc = bass.Bass("TRN2", target_bir_lowering=False)
    kb = KB()

    def din(name, shape, dt=F32):
        return nc.dram_tensor(name, shape, dt, kind="ExternalInput").ap()

    x_all = din("x_all", [4 * D, 512])
    x_own = din("x_own", [2 * D, 512])
    wqaT = din("wqaT", [8 * D, 512])
    wkvaT = din("wkvaT", [4 * D, 512])
    bqa_in = din("bqa", [128, 32])
    bkva_in = din("bkva", [128, 16])
    wdaT = din("wdaT", [8 * D, 512])
    bda_in = din("bda", [128, 32])
    cosa_in = din("cosa", [128, KVT])
    sina_in = din("sina", [128, KVT])
    maska_in = din("maska", [128, 8 * KEYS], BF16)
    wgT_in = din("wgT", [4 * D, 512])
    wuT_in = din("wuT", [4 * D, 512])
    wdfT_in = din("wdfT", [4 * D, 512])
    swg_in = din("s_wg_c", [128, 16])
    swu_in = din("s_wu_c", [128, 16])
    swd_in = din("s_wd_c", [128, 32])
    invg_in = din("inv_g_rep", [128, D])
    invu_in = din("inv_u_rep", [128, D])
    invi_in = din("inv_i_rep", [128, IFF])
    wqbT = din("wqbT", [8 * D, 512])
    wkvdT = din("wkvdT", [4 * D, 512])
    wkvdT_t = din("wkvdT_t", [D, 128])
    wgbT = din("wgbT", [8 * D, 512])
    wobT = din("wobT", [8 * D, 512])
    dtb_in = din("dtb_rep", [128, 32])
    alog_in = din("alog_rep", [128, 32])
    gnw_in = din("gnw_rep", [128, HD])
    ident_in = din("ident", [128, 128])
    e0_in = din("e0", [128, 128])
    ltri_in = din("ltri", [128, 128])
    maskTb_in = din("maskTb", [128, 128])
    cosb_in = din("cosb", [128, T])
    sinb_in = din("sinb", [128, T])
    uses0_in = din("use_s0", [128, 1])

    out_fm = nc.dram_tensor("out_fm", [2 * D, 512], F32, kind="ExternalOutput").ap()

    xs = [nc.dram_tensor(f"xs{i}", [2 * D, 512], F32).ap() for i in range(2)]
    qa_hbm = nc.dram_tensor("qa_hbm", [D, T], BF16).ap()
    qb_hbm = nc.dram_tensor("qb_hbm", [D, T], BF16).ap()
    gate_hbm = nc.dram_tensor("gate_hbm", [T, D], BF16).ap()
    y_hbm = nc.dram_tensor("y_hbm", [D, T], BF16).ap()
    cc_in = nc.dram_tensor("cc_in", [D, HD], F32).ap()
    cc_out = nc.dram_tensor("cc_out", [2 * D, HD], F32).ap()

    xall_b, xown_b = Buf(), Buf()
    xs_b = [[Buf() for _ in range(32)] for _ in range(2)]
    qa_b, qb_b, gate_b, y_b = Buf(), Buf(), Buf(), Buf()
    ccin_b, ccout_b = Buf(), Buf()
    out_b = [Buf() for _ in range(32)]

    es = ExitStack()

    def sbuf(name, shape, dt=F32):
        return es.enter_context(nc.sbuf_tensor(name, shape, dt))

    c_ident = sbuf("c_ident", [128, 128])
    c_e0 = sbuf("c_e0", [128, 128])
    c_ones = sbuf("c_ones", [128, 128])
    c_onesb = sbuf("c_onesb", [128, 128], BF16)
    c_ltri = sbuf("c_ltri", [128, 128])
    c_maskTb = sbuf("c_maskTb", [128, 128])
    c_permR = sbuf("c_permR", [128, 128], BF16)
    c_bqa = sbuf("c_bqa", [128, 32])
    c_bkva = sbuf("c_bkva", [128, 16])
    c_bda = sbuf("c_bda", [128, 32])
    c_swg = sbuf("c_swg", [128, 16])
    c_swu = sbuf("c_swu", [128, 16])
    c_swd = sbuf("c_swd", [128, 32])
    c_dtb = sbuf("c_dtb", [128, 32])
    c_negA = sbuf("c_negA", [128, 32])
    c_gnw = sbuf("c_gnw", [128, HD])
    c_uses0 = sbuf("c_uses0", [128, 1])

    xst = [sbuf(f"xst{i}", [128, 512]) for i in range(3)]
    wb = [sbuf(f"wb{i}", [128, 512], BF16) for i in range(4)]
    yb = [sbuf(f"yb{i}", [128, 512], BF16) for i in range(2)]
    rowbuf = sbuf("rowbuf", [128, 512])
    statr = sbuf("statr", [128, 1024])
    rstd_rep = sbuf("rstd_rep", [128, 512])
    negm_rep = sbuf("negm_rep", [128, 512])
    cos_sb = sbuf("cos_sb", [128, 512])
    sin_sb = sbuf("sin_sb", [128, 512])
    red = sbuf("red", [128, 16])
    probs = sbuf("probs", [128, KEYS], BF16)
    probsT = sbuf("probsT", [128, KEYS], BF16)
    mbuf = sbuf("mbuf", [128, KEYS], BF16)
    qtile = sbuf("qtile", [128, 128], BF16)
    qtile2 = sbuf("qtile2", [128, 128], BF16)
    reda = sbuf("reda", [128, 32])
    c_identb = sbuf("c_identb", [128, 128], BF16)
    qtileX = [qtile, qtile2]
    tile_a = sbuf("tile_a", [128, 512], BF16)
    tile_b = sbuf("tile_b", [128, 512], BF16)
    tile_c = sbuf("tile_cs", [128, 128], BF16)
    tile_d = sbuf("tile_d", [128, 128], BF16)
    tile_e = sbuf("tile_e", [128, 128], BF16)
    tile_f = sbuf("tile_f", [128, 128], BF16)
    qbf = sbuf("qbf", [128, 1024], BF16)
    wrk = [sbuf(f"wrk{i}", [128, 1024]) for i in range(3)]
    wrki = sbuf("wrki", [128, 1024], I32)
    invb = sbuf("invb", [128, 1024])
    state = sbuf("state", [128, NQH * HD])
    dtfm = sbuf("dtfm", [128, T])
    gcols = sbuf("gcols", [128, NCH * 96])
    grows = sbuf("grows", [128, NCH * 128])
    dtcols = sbuf("dtcols", [128, NCH * 32])
    egi = sbuf("egi", [128, NCH * 32])
    edec = sbuf("edec", [128, NCH * 32])
    ecd = sbuf("ecd", [128, NCH * 32])
    cumg = sbuf("cumg", [128, 32])
    sco = sbuf("sco", [128, 8])
    scq = sbuf("scq", [128, 16])
    scqr = sbuf("scqr", [128, 16])

    arA = sbuf("arA", [128, 32 * 512], BF16)
    arB = sbuf("arB", [128, 32 * 1024], BF16)
    arC = sbuf("arC", [128, 8 * 1024], BF16)

    psum = [es.enter_context(nc.psum_tensor(f"pb{i}", [128, 512], F32)) for i in range(8)]

    const_b = Buf()
    xst_b = [Buf() for _ in range(3)]
    wb_b = [Buf() for _ in range(4)]
    yb_b = [Buf() for _ in range(2)]
    rowbuf_b, statr_b, rep_b, cosb_b, red_b, sc_b = Buf(), Buf(), Buf(), Buf(), Buf(), Buf()
    probs_b, probsT_b, mbuf_b, qtile_b = Buf(), Buf(), Buf(), Buf()
    probsX_bL = [Buf(), Buf()]
    probsX_bH = [Buf(), Buf()]
    probsTX_bL = [Buf(), Buf()]
    probsTX_bH = [Buf(), Buf()]
    qtileX_b = [qtile_b, Buf()]
    reda_b = [Buf(), Buf()]
    tab_b = [Buf() for _ in range(6)]
    qbf_b = Buf()
    wrk_b = [Buf() for _ in range(3)]
    wrki_b = Buf()
    invb_b = Buf()
    state_b, S0_b, dtfm_b = Buf(), Buf(), Buf()
    g_b = Buf()
    scq_b = Buf()
    arA_b = [Buf() for _ in range(32)]
    arB_b = [Buf() for _ in range(32)]
    arC_b = [Buf() for _ in range(8)]
    ps_b = [Buf() for _ in range(8)]
    srep = [rstd_rep, negm_rep]
    srep_b = [rep_b, rep_b]
    S0bf = arA[:][:, 0:NQH * HD // 4 * 0 + 4096]
    S0_bufs = [arA_b[i] for i in range(8)]

    vA = arA[:].rearrange("p (c t) -> p c t", t=512)

    def vBkT(kv):
        return arB[:].rearrange("p (kv t) -> p kv t", t=KVT)[:, kv, :]

    vBv = arB[:].rearrange("p (a ti r) -> p a ti r", a=2, r=1024)
    vBx = arB[:].rearrange("p (ti f) -> p ti f", f=D)
    vBq = arB[:].rearrange("p (a c t) -> p a c t", a=2, t=512)
    vBi = arB[:].rearrange("p (x c t) -> p x c t", x=4, t=512)
    vBit = arB[:].rearrange("p (y ti f) -> p y ti f", y=4, f=IFF)
    vBkt = arB[:].rearrange("p (ti r) -> p ti r", r=1024)
    vBpr = arB[:].rearrange("p (c t) -> p c t", t=128)
    vC = arC[:].rearrange("p (c t) -> p c t", t=1024)
    vC5 = arC[:].rearrange("p (c t) -> p c t", t=512)
    vC5 = arC[:].rearrange("p (c t) -> p c t", t=512)
    vD = arB[:].rearrange("p (c t) -> p c t", t=1024)

    probsA = [probs[:], arC[:, 0:KEYS]]
    probsTA = [probsT[:], arC[:, KEYS:2 * KEYS]]
    probsTC = [probsT[:].rearrange("p (c t) -> p c t", t=128),
               arC[:, KEYS:2 * KEYS].rearrange("p (c t) -> p c t", t=128)]

    wring = [0]

    def wtile(src_ap, width=512):
        i = wring[0] % 4
        wring[0] += 1
        kb.op(kb.gpq, lambda e, d=wb[i][:, 0:width], s=src_ap:
              e.dma_start(out=d, in_=s), writes=[wb_b[i]], dma=True, stream=f"wb{i}")
        return wb[i], wb_b[i]

    def wblk(W, nk):
        def fn(k, m0, mw, W=W, nk=nk):
            r = ((m0 >> 2) * nk + k) * 128
            return W[r:r + 128, 0:128 * mw]
        return fn

    def xtile(tsr, slab, c):
        r = (slab * 32 + c) * 128
        return tsr[r:r + 128, 0:512]

    def dma(et, dst, src, reads=(), writes=(), stream="misc"):
        return kb.op(et, lambda e, d=dst, s=src: e.dma_start(out=d, in_=s),
                     reads=list(reads), writes=list(writes), dma=True, stream=stream)

    def dmaT(dst, src, reads=(), writes=(), stream="trA"):
        return kb.op(kb.act, lambda e, d=dst, s=src: e.dma_start_transpose(out=d, in_=s),
                     reads=list(reads), writes=list(writes), dma=True, stream=stream)

    for dst, src in [
        (c_ident[:], ident_in), (c_e0[:], e0_in), (c_ltri[:], ltri_in),
        (c_maskTb[:], maskTb_in), (c_bqa[:], bqa_in), (c_bkva[:], bkva_in),
        (c_bda[:], bda_in), (c_swg[:], swg_in), (c_swu[:], swu_in),
        (c_swd[:], swd_in), (c_dtb[:], dtb_in), (c_gnw[:], gnw_in),
        (c_uses0[:], uses0_in),
        (c_negA[:], alog_in),
    ]:
        dma(kb.gpq, dst, src, writes=[const_b], stream="const")
    dma(kb.gpq, c_identb[:], ident_in, writes=[const_b], stream="const")
    kb.op(kb.dve, lambda e: e.memset(c_ones[:], 1.0), writes=[const_b])
    kb.op(kb.dve, lambda e: e.memset(c_onesb[:], 1.0), writes=[const_b])
    kb.op(kb.dve, lambda e: e.memset(rowbuf[:], 0.0), writes=[rowbuf_b])
    kb.op(kb.act, lambda e: e.activation(out=c_negA[:], in_=c_negA[:], func=AF.Exp),
          reads=[const_b], writes=[const_b])
    kb.op(kb.dve, lambda e: e.tensor_scalar_mul(out=c_negA[:], in0=c_negA[:], scalar1=-1.0),
          reads=[const_b], writes=[const_b])
    kb.op(kb.dve, lambda e: e.memset(c_permR[:], 0.0), writes=[const_b])
    kb.op(kb.dve, lambda e: e.tensor_scalar_mul(out=c_permR[64:128, 0:64],
                                                in0=c_ident[0:64, 0:64], scalar1=-1.0),
          reads=[const_b], writes=[const_b])
    kb.op(kb.dve, lambda e: e.tensor_copy(out=c_permR[0:64, 64:128], in_=c_ident[0:64, 0:64]),
          reads=[const_b], writes=[const_b])

    def row_broadcast(row_ap, row_buf, dst_ap, dst_buf, width=512, pbi=5):
        kb.op(kb.dve, lambda e: e.tensor_copy(out=rowbuf[0:1, 0:width], in_=row_ap),
              reads=[row_buf], writes=[rowbuf_b])
        kb.group([lambda e: e.matmul(psum[pbi][:, 0:width], c_e0[:], rowbuf[:, 0:width],
                                     start=True, stop=True)],
                 reads=[rowbuf_b, const_b], writes=[ps_b[pbi]])
        kb.op(kb.dve, lambda e: e.tensor_copy(out=dst_ap, in_=psum[pbi][:, 0:width]),
              reads=[ps_b[pbi]], writes=[dst_buf])

    def layernorm(src_fn, src_buf_fn):
        for c in range(32):
            st, stb = xst[c % 2], xst_b[c % 2]
            dma(kb.spq, st[:, 0:512], src_fn(c), reads=[src_buf_fn(c)], writes=[stb],
                stream=f"xst{c % 2}")
            kb.op(kb.dve, lambda e, c=c, st=st: e.tensor_copy(out=vA[:, c, :], in_=st[:, 0:512]),
                  reads=[stb], writes=[arA_b[c]])
            kb.op(kb.act, lambda e, st=st: e.activation(out=wrk[2][:, 0:512], in_=st[:, 0:512],
                                                        func=AF.Square),
                  reads=[stb], writes=[wrk_b[2]])
            kb.group([lambda e, c=c: e.matmul(psum[6][:, 0:512], c_onesb[:], vA[:, c, :],
                                              start=(c == 0), stop=(c == 31))],
                     reads=[arA_b[c], const_b], writes=[ps_b[6]])
            kb.group([lambda e, c=c: e.matmul(psum[7][:, 0:512], c_ones[:], wrk[2][:, 0:512],
                                              start=(c == 0), stop=(c == 31))],
                     reads=[wrk_b[2], const_b], writes=[ps_b[7]])
        r0 = lambda a: statr[0:1, a:a + 512]
        kb.op(kb.dve, lambda e: e.tensor_scalar_mul(out=r0(0), in0=psum[6][0:1, 0:512],
                                                    scalar1=1.0 / D),
              reads=[ps_b[6]], writes=[statr_b])
        kb.op(kb.dve, lambda e: e.tensor_scalar_mul(out=r0(512), in0=psum[7][0:1, 0:512],
                                                    scalar1=1.0 / D),
              reads=[ps_b[7], statr_b], writes=[statr_b])
        kb.op(kb.dve, lambda e: e.scalar_tensor_tensor(out=rowbuf[0:1, 0:512], in0=r0(0),
                                                       scalar=-1.0, in1=r0(0),
                                                       op0=ALU.mult, op1=ALU.mult),
              reads=[statr_b], writes=[rowbuf_b])
        kb.op(kb.dve, lambda e: e.tensor_tensor(out=r0(512), in0=r0(512),
                                                in1=rowbuf[0:1, 0:512], op=ALU.add),
              reads=[statr_b, rowbuf_b], writes=[statr_b])
        kb.op(kb.dve, lambda e: e.tensor_scalar_add(out=r0(512), in0=r0(512),
                                                    scalar1=EPS_LN),
              reads=[statr_b], writes=[statr_b])
        kb.op(kb.act, lambda e: e.activation(out=r0(512), in_=r0(512), func=AF.Ln),
              reads=[statr_b], writes=[statr_b])
        kb.op(kb.dve, lambda e: e.tensor_scalar_mul(out=r0(512), in0=r0(512),
                                                    scalar1=-0.5),
              reads=[statr_b], writes=[statr_b])
        kb.op(kb.act, lambda e: e.activation(out=r0(512), in_=r0(512), func=AF.Exp),
              reads=[statr_b], writes=[statr_b])
        kb.op(kb.dve, lambda e: e.tensor_tensor(out=r0(0), in0=r0(0), in1=r0(512),
                                                op=ALU.mult),
              reads=[statr_b], writes=[statr_b])
        kb.op(kb.dve, lambda e: e.tensor_scalar_mul(out=r0(0), in0=r0(0), scalar1=-1.0),
              reads=[statr_b], writes=[statr_b])
        row_broadcast(r0(512), statr_b, rstd_rep[:], rep_b)
        row_broadcast(r0(0), statr_b, negm_rep[:], rep_b)
        for c in range(32):
            w, wbuf = wrk[c % 2], wrk_b[c % 2]
            kb.op(kb.dve, lambda e, c=c, w=w: e.tensor_tensor(out=w[:, 0:512],
                                                              in0=vA[:, c, :], in1=rstd_rep[:],
                                                              op=ALU.mult),
                  reads=[arA_b[c], rep_b], writes=[wbuf])
            kb.op(kb.dve, lambda e, c=c, w=w: e.tensor_tensor(out=vA[:, c, :], in0=w[:, 0:512],
                                                              in1=negm_rep[:], op=ALU.add),
                  reads=[wbuf, rep_b], writes=[arA_b[c]])

    def gemm1(wT_ap_fn, nk, m_list, rhs_fn, evac_fn, nw=512):
        seen = []
        for g0 in range(0, len(m_list), 4):
            mg = m_list[g0:g0 + 4]
            for k in range(nk):
                wt, wtb = wtile(wT_ap_fn(k, mg[0], len(mg)), width=128 * len(mg))
                rhs, rhsb = rhs_fn(k)
                if rhsb not in seen:
                    seen.append(rhsb)
                builders = []
                for j in range(len(mg)):
                    builders.append(lambda e, j=j, k=k, wt=wt, rhs=rhs, nk=nk:
                                    e.matmul(psum[j][:, 0:nw], wt[:, 128 * j:128 * (j + 1)],
                                             rhs, start=(k == 0), stop=(k == nk - 1)))
                kb.group(builders, reads=[wtb, rhsb, const_b],
                         writes=[ps_b[j] for j in range(len(mg))])
            for j, m in enumerate(mg):
                evac_fn(m, j)
        kb.fence(kb.pe, seen)

    def rope_evac(ps_idx, nw, bias_col, cos_ap, sin_ap, dst_ap, dst_buf, tab_i=0):
        kw = dict(func=AF.Copy)
        if bias_col is not None:
            kw = dict(func=AF.Identity, bias=bias_col)
        kb.op(kb.act, lambda e, kw=kw: e.activation(out=tile_a[:, 0:nw],
                                                    in_=psum[ps_idx][:, 0:nw], **kw),
              reads=[ps_b[ps_idx], const_b], writes=[tab_b[0]])
        kb.group([lambda e: e.matmul(psum[4][:, 0:nw], c_permR[:], tile_a[:, 0:nw],
                                     start=True, stop=True)],
                 reads=[tab_b[0], const_b], writes=[ps_b[4]])
        kb.op(kb.dve, lambda e: e.tensor_tensor(out=wrk[0][:, 0:nw], in0=psum[4][:, 0:nw],
                                                in1=sin_ap, op=ALU.mult),
              reads=[ps_b[4], cosb_b, const_b], writes=[wrk_b[0]])
        kb.op(kb.dve, lambda e: e.tensor_tensor(out=wrk[1][:, 0:nw], in0=tile_a[:, 0:nw],
                                                in1=cos_ap, op=ALU.mult),
              reads=[tab_b[0], cosb_b, const_b], writes=[wrk_b[1]])
        kb.op(kb.dve, lambda e: e.tensor_tensor(out=dst_ap, in0=wrk[0][:, 0:nw],
                                                in1=wrk[1][:, 0:nw], op=ALU.add),
              reads=[wrk_b[0], wrk_b[1]], writes=[dst_buf])

    def quant_all(tok_fn, tok_bufs_fn, n_m, nfeat, inv_hbm, sc_base, qsink):
        nseg = nfeat // 1024
        for s in range(nseg):
            dma(kb.gpq, invb[:], inv_hbm[:, 1024 * s:1024 * (s + 1)], writes=[invb_b], stream="invb")
            for m in range(n_m):
                kb.op(kb.dve, lambda e, m=m, s=s: e.tensor_tensor(
                    out=wrk[0][:], in0=tok_fn(m)[:, 1024 * s:1024 * (s + 1)], in1=invb[:],
                    op=ALU.mult), reads=list(tok_bufs_fn(m, s)) + [invb_b],
                    writes=[wrk_b[0]])
                if s == 0:
                    kb.op(kb.dve, lambda e, m=m: e.reduce_max(
                        out=red[:, m:m + 1], in_=wrk[0][:], axis=AX.X,
                        apply_absolute_value=True), reads=[wrk_b[0]], writes=[red_b])
                else:
                    kb.op(kb.dve, lambda e: e.reduce_max(
                        out=red[:, 8:9], in_=wrk[0][:], axis=AX.X,
                        apply_absolute_value=True), reads=[wrk_b[0]], writes=[red_b])
                    kb.op(kb.dve, lambda e, m=m: e.tensor_tensor(
                        out=red[:, m:m + 1], in0=red[:, m:m + 1], in1=red[:, 8:9],
                        op=ALU.max), reads=[red_b], writes=[red_b])
        for m in range(n_m):
            kb.op(kb.dve, lambda e, m=m: e.tensor_scalar(
                out=scq[:, sc_base + m:sc_base + m + 1], in0=red[:, m:m + 1],
                scalar1=1.0 / 127.0, scalar2=1e-8, op0=ALU.mult, op1=ALU.max),
                reads=[red_b], writes=[scq_b])
            kb.op(kb.dve, lambda e, m=m: e.reciprocal(
                out=scqr[:, sc_base + m:sc_base + m + 1],
                in_=scq[:, sc_base + m:sc_base + m + 1]),
                reads=[scq_b], writes=[scq_b])
        for s in range(nseg):
            dma(kb.gpq, invb[:], inv_hbm[:, 1024 * s:1024 * (s + 1)], writes=[invb_b], stream="invb")
            for m in range(n_m):
                kb.op(kb.dve, lambda e, m=m, s=s: e.tensor_tensor(
                    out=wrk[0][:], in0=tok_fn(m)[:, 1024 * s:1024 * (s + 1)], in1=invb[:],
                    op=ALU.mult), reads=list(tok_bufs_fn(m, s)) + [invb_b],
                    writes=[wrk_b[0]])
                kb.op(kb.act, lambda e, m=m: e.activation(
                    out=wrk[0][:], in_=wrk[0][:], func=AF.Identity,
                    scale=scqr[:, sc_base + m:sc_base + m + 1]),
                    reads=[wrk_b[0], scq_b], writes=[wrk_b[0]])
                kb.op(kb.dve, lambda e: e.tensor_copy(out=wrki[:], in_=wrk[0][:]),
                      reads=[wrk_b[0]], writes=[wrki_b])
                kb.op(kb.dve, lambda e: e.tensor_copy(out=wrk[1][:], in_=wrki[:]),
                      reads=[wrki_b], writes=[wrk_b[1]])
                kb.op(kb.dve, lambda e: e.tensor_tensor(
                    out=wrk[2][:], in0=wrk[0][:], in1=wrk[1][:], op=ALU.subtract),
                    reads=[wrk_b[0], wrk_b[1]], writes=[wrk_b[2]])
                kb.op(kb.dve, lambda e: e.scalar_tensor_tensor(
                    out=wrk[1][:], in0=wrk[2][:], scalar=0.5, in1=wrk[1][:],
                    op0=ALU.is_ge, op1=ALU.add), reads=[wrk_b[2], wrk_b[1]],
                    writes=[wrk_b[1]])
                kb.op(kb.dve, lambda e: e.scalar_tensor_tensor(
                    out=wrk[1][:], in0=wrk[2][:], scalar=-0.5, in1=wrk[1][:],
                    op0=ALU.is_ge, op1=ALU.add), reads=[wrk_b[2], wrk_b[1]],
                    writes=[wrk_b[1]])
                kb.op(kb.dve, lambda e: e.tensor_scalar_add(
                    out=qbf[:], in0=wrk[1][:], scalar1=-1.0),
                    reads=[wrk_b[1]], writes=[qbf_b])
                qsink(m, s)

    def scale_row_rep(cols_ap, n_m, dst, dst_buf):
        kb.group([lambda e: e.transpose(psum[4][0:n_m, 0:128], cols_ap, c_ident[:])],
                 reads=[scq_b, const_b], writes=[ps_b[4]])
        kb.op(kb.dve, lambda e: e.tensor_copy(out=statr[0:n_m, 0:128],
                                              in_=psum[4][0:n_m, 0:128]),
              reads=[ps_b[4]], writes=[statr_b])
        for m in range(n_m):
            dma(kb.act, statr[0:1, 128 + 128 * m:128 + 128 * (m + 1)],
                statr[m:m + 1, 0:128], reads=[statr_b], writes=[statr_b],
                stream="srow")
        row_broadcast(statr[0:1, 128:128 + 128 * n_m], statr_b, dst[:, 0:128 * n_m],
                      dst_buf, width=128 * n_m)

    def attn_a(dst, dst_b, dst_stream):
        for qr in range(4):
            q0 = 512 * qr
            layernorm(lambda c, qr=qr: xtile(x_all, qr, c),
                      lambda c: xall_b)
            dma(kb.spq, cos_sb[:], cosa_in[:, q0:q0 + 512], writes=[cosb_b], stream="cos")
            dma(kb.spq, sin_sb[:], sina_in[:, q0:q0 + 512], writes=[cosb_b], stream="cos")

            def kv_evac(m, j, q0=q0):
                half = 1 if q0 >= 1024 else 0
                toff = q0 % 1024
                if m < 8:
                    kv = m
                    slot = kv * 2 + half
                    rope_evac(j, 512, c_bkva[:, kv:kv + 1], cos_sb[:], sin_sb[:],
                              vBkT(kv)[:, q0:q0 + 512], arB_b[slot])
                else:
                    kv = m - 8
                    kb.op(kb.act, lambda e, j=j, kv=kv: e.activation(
                        out=tile_a[:, 0:512], in_=psum[j][:, 0:512], func=AF.Identity,
                        bias=c_bkva[:, 8 + kv:9 + kv]),
                        reads=[ps_b[j], const_b], writes=[tab_b[0]])
                    ti0 = q0 // 128
                    dmaT(vBv[:, 1, ti0:ti0 + 4, kv * 128:(kv + 1) * 128], tile_a[:, 0:512],
                         reads=[tab_b[0]], writes=[arB_b[16 + ti0 + i] for i in range(4)],
                         stream="trV")

            gemm1(wblk(wkvaT, 32),
                  32, list(range(16)), lambda k: (vA[:, k, :], arA_b[k]), kv_evac)

            own_lo, own_hi = max(q0, WIN), min(q0 + 512, WIN + T)
            if own_hi > own_lo:
                nw = own_hi - own_lo
                toff = own_lo - q0
                kb.op(kb.dve, lambda e: e.tensor_scalar_mul(out=cos_sb[:], in0=cos_sb[:],
                                                            scalar1=SM_SCALE),
                      reads=[cosb_b], writes=[cosb_b])
                kb.op(kb.dve, lambda e: e.tensor_scalar_mul(out=sin_sb[:], in0=sin_sb[:],
                                                            scalar1=SM_SCALE),
                      reads=[cosb_b], writes=[cosb_b])

                def q_evac(m, j, nw=nw, toff=toff, oc=own_lo - WIN):
                    rope_evac(j, nw, c_bqa[:, m:m + 1], cos_sb[:, toff:toff + nw],
                              sin_sb[:, toff:toff + nw], tile_b[:, 0:nw], tab_b[1])
                    dma(kb.gpq, qa_hbm[128 * m:128 * (m + 1), oc:oc + nw], tile_b[:, 0:nw],
                        reads=[tab_b[1]], writes=[qa_b], stream="qa")

                gemm1(wblk(wqaT, 32),
                      32, list(range(32)),
                      lambda k, toff=toff, nw=nw: (vA[:, k, toff:toff + nw], arA_b[k]),
                      q_evac, nw=nw)

        kb.barrier(kb.act, "trV", arB_b[16:32])
        for hh in range(2):
            iters = [(4 * hh + p4, h) for p4 in range(4) for h in range(NQH)]
            n_it = len(iters)

            def scores_stage(t):
                p, h = iters[t]
                u = t % 2
                kv = h >> 2
                if h == 0:
                    dma(kb.spq, mbuf[:], maska_in[:, KEYS * p:KEYS * (p + 1)],
                        writes=[mbuf_b], stream="mbuf")
                dma(kb.spq, qtileX[u][:], qa_hbm[128 * h:128 * (h + 1), 128 * p:128 * (p + 1)],
                    reads=[qa_b], writes=[qtileX_b[u]], stream="qtile")
                starts = _kchunk_starts(p)
                segs = [(0, 0, starts[0], 512), (1, 0, starts[4], 512),
                        (2, 0, starts[8], 128), (2, 128, starts[9], 128),
                        (2, 256, starts[10], 128)]
                builders, reads = [], [qtileX_b[u], mbuf_b, const_b]
                for (pi, po, st, wdt) in segs:
                    builders.append(lambda e, pi=pi, po=po, st=st, wdt=wdt, kv=kv, u=u:
                                    e.matmul(psum[4 * u + pi][:, po:po + wdt], qtileX[u][:],
                                             vBkT(kv)[:, st:st + wdt],
                                             start=True, stop=False))
                    reads.append(arB_b[kv * 2 + (1 if st >= 1024 else 0)])
                    if (st % 1024) + wdt > 1024:
                        reads.append(arB_b[kv * 2 + 1])
                for (pi, wdt, mo) in [(0, 512, 0), (1, 512, 512), (2, 384, 1024)]:
                    builders.append(lambda e, pi=pi, wdt=wdt, mo=mo, u=u:
                                    e.matmul(psum[4 * u + pi][:, 0:wdt], c_identb[:],
                                             mbuf[:, mo:mo + wdt],
                                             start=False, stop=True))
                kb.group(builders, reads=reads,
                         writes=[ps_b[4 * u], ps_b[4 * u + 1], ps_b[4 * u + 2]])

            def softmax_stage(t):
                u = t % 2
                ro = 16 * u
                for s, (pi, wdt) in enumerate([(0, 512), (1, 512), (2, 384)]):
                    kb.op(kb.dve, lambda e, pi=pi, wdt=wdt, s=s, u=u, ro=ro: e.reduce_max(
                        out=reda[:, ro + s:ro + s + 1], in_=psum[4 * u + pi][:, 0:wdt],
                        axis=AX.X), reads=[ps_b[4 * u + pi]], writes=[reda_b[u]])
                kb.op(kb.dve, lambda e, ro=ro: e.reduce_max(
                    out=reda[:, ro + 3:ro + 4], in_=reda[:, ro:ro + 3], axis=AX.X),
                    reads=[reda_b[u]], writes=[reda_b[u]])
                kb.op(kb.dve, lambda e, ro=ro: e.tensor_scalar_mul(
                    out=reda[:, ro + 4:ro + 5], in0=reda[:, ro + 3:ro + 4], scalar1=-1.0),
                    reads=[reda_b[u]], writes=[reda_b[u]])
                for s, (pi, wdt, o, hi) in enumerate([(0, 512, 0, 0), (1, 512, 512, 0),
                                                      (2, 384, 1024, 1)]):
                    kb.op(kb.act, lambda e, pi=pi, wdt=wdt, o=o, s=s, u=u, ro=ro:
                          e.activation(out=probsA[u][:, o:o + wdt],
                                       in_=psum[4 * u + pi][:, 0:wdt],
                                       func=AF.Exp, bias=reda[:, ro + 4:ro + 5],
                                       accum_out=reda[:, ro + 5 + s:ro + 6 + s]),
                          reads=[ps_b[4 * u + pi], reda_b[u]],
                          writes=[probsX_bH[u] if hi else probsX_bL[u], reda_b[u]])
                kb.op(kb.dve, lambda e, ro=ro: e.tensor_tensor(
                    out=reda[:, ro + 8:ro + 9], in0=reda[:, ro + 5:ro + 6],
                    in1=reda[:, ro + 6:ro + 7], op=ALU.add),
                    reads=[reda_b[u]], writes=[reda_b[u]])
                kb.op(kb.dve, lambda e, ro=ro: e.tensor_tensor(
                    out=reda[:, ro + 9:ro + 10], in0=reda[:, ro + 8:ro + 9],
                    in1=reda[:, ro + 7:ro + 8], op=ALU.add),
                    reads=[reda_b[u]], writes=[reda_b[u]])
                kb.op(kb.dve, lambda e, ro=ro: e.reciprocal(
                    out=reda[:, ro + 10:ro + 11], in_=reda[:, ro + 9:ro + 10]),
                    reads=[reda_b[u]], writes=[reda_b[u]])
                kb.op(kb.act, lambda e, u=u, ro=ro: e.activation(
                    out=probsA[u][:, 0:1024], in_=probsA[u][:, 0:1024], func=AF.Identity,
                    scale=reda[:, ro + 10:ro + 11]),
                    reads=[probsX_bL[u], reda_b[u]], writes=[probsX_bL[u]])
                dmaT(probsTC[u][:, 0:8, :],
                     probsA[u][:, 0:1024], reads=[probsX_bL[u]], writes=[probsTX_bL[u]],
                     stream="trP")
                kb.op(kb.dve, lambda e, u=u, ro=ro: e.tensor_scalar_mul(
                    out=probsA[u][:, 1024:KEYS], in0=probsA[u][:, 1024:KEYS],
                    scalar1=reda[:, ro + 10:ro + 11]),
                    reads=[probsX_bH[u], reda_b[u]], writes=[probsX_bH[u]])
                dmaT(probsTC[u][:, 8:11, :],
                     probsA[u][:, 1024:KEYS], reads=[probsX_bH[u]], writes=[probsTX_bH[u]],
                     stream="trP")

            def ctx_stage(t):
                p, h = iters[t]
                u = t % 2
                p4 = p % 4
                kv = h >> 2
                starts = _kchunk_starts(p)
                builders, reads = [], [probsTX_bL[u], probsTX_bH[u]]
                for ci, st in enumerate(starts):
                    ti = st // 128
                    builders.append(lambda e, ci=ci, ti=ti, kv=kv, u=u:
                                    e.matmul(psum[4 * u + 3][:, 0:128],
                                             vBv[:, 1, ti, kv * 128:(kv + 1) * 128],
                                             probsTA[u][:, 128 * ci:128 * (ci + 1)],
                                             start=(ci == 0), stop=(ci == 10)))
                    reads.append(arB_b[16 + ti])
                kb.group(builders, reads=reads, writes=[ps_b[4 * u + 3]])
                kb.op(kb.act, lambda e, h=h, p4=p4, u=u: e.activation(
                    out=vA[:, h, 128 * p4:128 * (p4 + 1)], in_=psum[4 * u + 3][:, 0:128],
                    func=AF.Copy),
                    reads=[ps_b[4 * u + 3]], writes=[arA_b[h]])

            for t in range(n_it + 2):
                if t < n_it:
                    scores_stage(t)
                if 1 <= t <= n_it:
                    softmax_stage(t - 1)
                if t >= 2:
                    ctx_stage(t - 2)

            def d_evac(m, j, hh=hh):
                dma(kb.spq, xst[2][:, 0:512], xtile(x_own, hh, m),
                    reads=[xown_b], writes=[xst_b[2]], stream="xst2")
                kb.op(kb.dve, lambda e, m=m, j=j: e.scalar_tensor_tensor(
                    out=wrk[0][:, 0:512], in0=psum[j][:, 0:512], scalar=c_bda[:, m:m + 1],
                    op0=ALU.add, in1=xst[2][:, 0:512], op1=ALU.add),
                    reads=[ps_b[j], xst_b[2], const_b], writes=[wrk_b[0]])
                dma(kb.gpq, xtile(dst, hh, m),
                    wrk[0][:, 0:512], reads=[wrk_b[0]], writes=[dst_b[m]],
                    stream=dst_stream)

            kb.fence(kb.pe, arB_b)
            gemm1(wblk(wdaT, 32),
                  32, list(range(32)),
                  lambda k: (vA[:, k, :], arA_b[k]),
                  d_evac)

    def ffn(src, src_b, dst, dst_b, dst_stream):
        for hh in range(2):
            o5 = 512 * hh
            layernorm(lambda c, hh=hh: xtile(src, hh, c),
                      lambda c: src_b[c])
            if debug == "xn":
                for c in range(32):
                    dma(kb.gpq, xtile(dst, hh, c), vA[:, c, :],
                        reads=[arA_b[c]], writes=[dst_b[c]], stream="dump")
                continue
            for c in range(32):
                dmaT(vBx[:, 0:4, 128 * c:128 * (c + 1)], vA[:, c, :],
                     reads=[arA_b[c]],
                     writes=[arB_b[4 * ti + c // 8] for ti in range(4)], stream="trX")
            kb.barrier(kb.act, "trX", arB_b[0:16])
            for path in range(2):
                def qsink(m, s):
                    dmaT(vBq[:, 1, 8 * s:8 * s + 8, 128 * m:128 * (m + 1)],
                         qbf[:], reads=[qbf_b],
                         writes=[arB_b[16 + 4 * s + i] for i in range(4)], stream="trQ")
                quant_all(lambda m: vBx[:, m, :],
                          lambda m, s: [arB_b[4 * m + s]],
                          4, D, invg_in if path == 0 else invu_in, 4 * path, qsink)
                kb.fence(kb.dve, arB_b[0:16])
                kb.barrier(kb.act, "trQ", arB_b[16:32])
                scale_row_rep(scq[:, 4 * path:4 * path + 4], 4, srep[path], srep_b[path])
                if debug == "qg" and path == 0:
                    for k in range(32):
                        dma(kb.gpq, xtile(dst, hh, k),
                            vBq[:, 1, k, :], reads=[arB_b[16 + k // 2]],
                            writes=[dst_b[k]], stream="dump")
                    break

                def gu_evac(m, j, path=path):
                    if debug == "gacc" and path == 0:
                        kb.op(kb.dve, lambda e, j=j: e.tensor_copy(
                            out=wrk[0][:, 0:512], in_=psum[j][:, 0:512]),
                            reads=[ps_b[j]], writes=[wrk_b[0]])
                        dma(kb.gpq, xtile(dst, hh, m),
                            wrk[0][:, 0:512], reads=[wrk_b[0]], writes=[dst_b[m]],
                            stream="dump")
                        return
                    kb.op(kb.act, lambda e, m=m, j=j, path=path: e.activation(
                        out=wrk[0][:, 0:512], in_=psum[j][:, 0:512], func=AF.Identity,
                        scale=(c_swg if path == 0 else c_swu)[:, m:m + 1]),
                        reads=[ps_b[j], const_b], writes=[wrk_b[0]])
                    kb.op(kb.dve, lambda e, path=path: e.tensor_tensor(
                        out=wrk[1][:, 0:512], in0=wrk[0][:, 0:512], in1=srep[path][:],
                        op=ALU.mult),
                        reads=[wrk_b[0], srep_b[path]], writes=[wrk_b[1]])
                    if path == 0:
                        kb.op(kb.act, lambda e: e.activation(
                            out=wrk[0][:, 0:512], in_=wrk[1][:, 0:512], func=AF.Silu),
                            reads=[wrk_b[1]], writes=[wrk_b[0]])
                        kb.op(kb.dve, lambda e, m=m: e.tensor_copy(
                            out=vC5[:, m, :], in_=wrk[0][:, 0:512]),
                            reads=[wrk_b[0]], writes=[arC_b[m // 2]])
                        kb.op(kb.dve, lambda e, m=m: e.tensor_tensor(
                            out=vA[:, 16 + m, :], in0=wrk[0][:, 0:512], in1=vC5[:, m, :],
                            op=ALU.subtract),
                            reads=[wrk_b[0], arC_b[m // 2]], writes=[arA_b[16 + m]])
                    else:
                        kb.op(kb.dve, lambda e, m=m: e.tensor_tensor(
                            out=wrk[0][:, 0:512], in0=vC5[:, m, :],
                            in1=wrk[1][:, 0:512], op=ALU.mult),
                            reads=[arC_b[m // 2], wrk_b[1]], writes=[wrk_b[0]])
                        kb.op(kb.dve, lambda e, m=m: e.tensor_tensor(
                            out=wrk[2][:, 0:512], in0=vA[:, 16 + m, :],
                            in1=wrk[1][:, 0:512], op=ALU.mult),
                            reads=[arA_b[16 + m], wrk_b[1]], writes=[wrk_b[2]])
                        kb.op(kb.dve, lambda e: e.tensor_tensor(
                            out=wrk[0][:, 0:512], in0=wrk[0][:, 0:512],
                            in1=wrk[2][:, 0:512], op=ALU.add),
                            reads=[wrk_b[0], wrk_b[2]], writes=[wrk_b[0]])
                        kb.op(kb.dve, lambda e, m=m: e.tensor_scalar(
                            out=vC5[:, m, :], in0=wrk[0][:, 0:512], scalar1=10.0,
                            scalar2=-10.0, op0=ALU.min, op1=ALU.max),
                            reads=[wrk_b[0]], writes=[arC_b[m // 2]])

                gemm1(wblk(wgT_in if path == 0 else wuT_in, 32),
                      32, list(range(16)),
                      lambda k: (vBq[:, 1, k, :], arB_b[16 + k // 2]), gu_evac)
            if debug == "qg":
                continue
            if debug in ("gate", "gacc"):
                for m in range(16):
                    dma(kb.gpq, xtile(dst, hh, m), vC5[:, m, :],
                        reads=[arC_b[m // 2]], writes=[dst_b[m]], stream="dump")
                continue
            kb.fence(kb.pe, arB_b[0:16])
            for c in range(16):
                dmaT(vBit[:, 0, 0:4, 128 * c:128 * (c + 1)], vC5[:, c, :],
                     reads=[arC_b[c // 2]],
                     writes=[arB_b[2 * ti + c // 8] for ti in range(4)], stream="trI")
            kb.barrier(kb.act, "trI", arB_b[0:8])
            def qsinki(m, s):
                dmaT(vBi[:, 1, 8 * s:8 * s + 8, 128 * m:128 * (m + 1)], qbf[:],
                     reads=[qbf_b], writes=[arB_b[8 + 4 * s + i] for i in range(4)],
                     stream="trJ")
            quant_all(lambda m: vBit[:, 0, m, :],
                      lambda m, s: [arB_b[2 * m + s]],
                      4, IFF, invi_in, 8, qsinki)
            kb.fence(kb.dve, arB_b[0:8])
            kb.barrier(kb.act, "trJ", arB_b[8:16])
            scale_row_rep(scq[:, 8:12], 4, srep[0], srep_b[0])
            if debug == "qi":
                for k in range(16):
                    dma(kb.gpq, xtile(dst, hh, k),
                        vBi[:, 1, k, :], reads=[arB_b[8 + k // 2]], writes=[dst_b[k]],
                        stream="dump")
                continue

            def d_evac(m, j, hh=hh):
                kb.op(kb.act, lambda e, m=m, j=j: e.activation(
                    out=wrk[0][:, 0:512], in_=psum[j][:, 0:512], func=AF.Identity,
                    scale=c_swd[:, m:m + 1]),
                    reads=[ps_b[j], const_b], writes=[wrk_b[0]])
                dma(kb.spq, xst[2][:, 0:512], xtile(src, hh, m),
                    reads=[src_b[m]], writes=[xst_b[2]], stream="xst2")
                kb.op(kb.dve, lambda e: e.tensor_tensor(
                    out=wrk[1][:, 0:512], in0=wrk[0][:, 0:512], in1=srep[0][:],
                    op=ALU.mult), reads=[wrk_b[0], srep_b[0]], writes=[wrk_b[1]])
                kb.op(kb.dve, lambda e: e.tensor_tensor(
                    out=wrk[0][:, 0:512], in0=wrk[1][:, 0:512], in1=xst[2][:, 0:512],
                    op=ALU.add), reads=[wrk_b[1], xst_b[2]], writes=[wrk_b[0]])
                dma(kb.gpq, xtile(dst, hh, m), wrk[0][:, 0:512],
                    reads=[wrk_b[0]], writes=[dst_b[m]], stream=dst_stream)

            gemm1(wblk(wdfT_in, 16),
                  16, list(range(32)), lambda k: (vBi[:, 1, k, :], arB_b[8 + k // 2]),
                  d_evac)

    def attn_b(src, src_b, dst, dst_b, dst_stream):
        for hh in range(2):
            o5 = 512 * hh
            layernorm(lambda c, hh=hh: xtile(src, hh, c),
                      lambda c: src_b[c])
            dma(kb.spq, cos_sb[:], cosb_in[:, o5:o5 + 512], writes=[cosb_b], stream="cos")
            dma(kb.spq, sin_sb[:], sinb_in[:, o5:o5 + 512], writes=[cosb_b], stream="cos")

            def kvd_evac(m, j, hh=hh, o5=o5):
                if m < 8:
                    kv = m
                    rope_evac(j, 512, None, cos_sb[:], sin_sb[:],
                              vC[:, kv, o5:o5 + 512], arC_b[kv])
                elif m < 16:
                    kv = m - 8
                    kb.op(kb.act, lambda e, j=j: e.activation(
                        out=tile_a[:, 0:512], in_=psum[j][:, 0:512], func=AF.Copy),
                        reads=[ps_b[j]], writes=[tab_b[0]])
                    ti0 = 4 * hh
                    dmaT(vD[:, 12 + ti0:12 + ti0 + 4, kv * 128:(kv + 1) * 128], tile_a[:, 0:512],
                         reads=[tab_b[0]], writes=[arB_b[12 + ti0 + i] for i in range(4)],
                         stream="trV")
                else:
                    kb.op(kb.act, lambda e, j=j, o5=o5: e.activation(
                        out=dtfm[:, o5:o5 + 512], in_=psum[j][:, 0:512], func=AF.Copy),
                        reads=[ps_b[j]], writes=[dtfm_b])

            gemm1(lambda k, m0, mw: (wkvdT[((m0 >> 2) * 32 + k) * 128:
                                           ((m0 >> 2) * 32 + k) * 128 + 128, 0:512]
                                     if mw == 4 else
                                     wkvdT_t[128 * k:128 * (k + 1), 0:128]),
                  32, list(range(17)), lambda k: (vA[:, k, :], arA_b[k]), kvd_evac)

            def qb_evac(m, j, o5=o5):
                rope_evac(j, 512, None, cos_sb[:], sin_sb[:],
                          tile_b[:, 0:512], tab_b[1])
                dma(kb.gpq, qb_hbm[128 * m:128 * (m + 1), o5:o5 + 512], tile_b[:, 0:512],
                    reads=[tab_b[1]], writes=[qb_b], stream="qb")

            gemm1(wblk(wqbT, 32),
                  32, list(range(32)), lambda k: (vA[:, k, :], arA_b[k]), qb_evac)

            def gb_evac(m, j, hh=hh, o5=o5):
                kb.op(kb.act, lambda e, j=j: e.activation(
                    out=tile_a[:, 0:512], in_=psum[j][:, 0:512], func=AF.Silu),
                    reads=[ps_b[j]], writes=[tab_b[0]])
                dmaT(tile_b[:, 0:512].rearrange("p (ti t) -> p ti t", t=128),
                     tile_a[:, 0:512], reads=[tab_b[0]], writes=[tab_b[1]], stream="trG")
                dma(kb.gpq,
                    gate_hbm[o5:o5 + 512, 128 * m:128 * (m + 1)].rearrange(
                        "(ti p) d -> p ti d", p=128),
                    tile_b[:, 0:512].rearrange("p (ti t) -> p ti t", t=128),
                    reads=[tab_b[1]], writes=[gate_b], stream="gate")

            gemm1(wblk(wgbT, 32),
                  32, list(range(32)), lambda k: (vA[:, k, :], arA_b[k]), gb_evac)

        for kv in range(NKV):
            dmaT(vBkt[:, 0:8, kv * 128:(kv + 1) * 128], vC[:, kv, 0:1024],
                 reads=[arC_b[kv]], writes=[arB_b[i] for i in range(8)], stream="trK")
        kb.barrier(kb.act, "trK", arB_b[0:8])
        kb.barrier(kb.act, "trV", arB_b[12:20])

        for n in range(NCH):
            kb.group([lambda e, n=n: e.transpose(psum[4][:, 0:128],
                                                 dtfm[:, 128 * n:128 * (n + 1)], c_ident[:])],
                     reads=[dtfm_b, const_b], writes=[ps_b[4]])
            kb.op(kb.dve, lambda e: e.tensor_tensor(out=wrk[0][:, 0:32],
                                                    in0=psum[4][:, 0:32], in1=c_dtb[:, 0:32],
                                                    op=ALU.add),
                  reads=[ps_b[4], const_b], writes=[wrk_b[0]])
            kb.op(kb.act, lambda e: e.activation(out=wrk[1][:, 0:32], in_=wrk[0][:, 0:32],
                                                 func=AF.Exp),
                  reads=[wrk_b[0]], writes=[wrk_b[1]])
            kb.op(kb.dve, lambda e: e.tensor_scalar_add(out=wrk[1][:, 0:32],
                                                        in0=wrk[1][:, 0:32], scalar1=1.0),
                  reads=[wrk_b[1]], writes=[wrk_b[1]])
            kb.op(kb.act, lambda e, n=n: e.activation(out=dtcols[:, 32 * n:32 * (n + 1)],
                                                      in_=wrk[1][:, 0:32], func=AF.Ln),
                  reads=[wrk_b[1]], writes=[g_b])
            kb.op(kb.dve, lambda e, n=n: e.tensor_tensor(out=wrk[0][:, 0:32],
                                                         in0=dtcols[:, 32 * n:32 * (n + 1)],
                                                         in1=c_negA[:, 0:32], op=ALU.mult),
                  reads=[g_b, const_b], writes=[wrk_b[0]])
            kb.group([lambda e: e.matmul(psum[5][:, 0:32], c_ltri[:], wrk[0][:, 0:32],
                                         start=True, stop=True),
                      lambda e: e.matmul(psum[5][:, 64:96], c_ones[:], wrk[0][:, 0:32],
                                         start=True, stop=True)],
                     reads=[wrk_b[0], const_b], writes=[ps_b[5]])
            kb.op(kb.dve, lambda e, n=n: e.tensor_copy(out=gcols[:, 96 * n:96 * n + 32],
                                                       in_=psum[5][:, 0:32]),
                  reads=[ps_b[5]], writes=[g_b])
            kb.op(kb.dve, lambda e, n=n: e.tensor_copy(out=gcols[:, 96 * n + 32:96 * n + 64],
                                                       in_=psum[5][:, 64:96]),
                  reads=[ps_b[5]], writes=[g_b])
            kb.op(kb.dve, lambda e, n=n: e.tensor_tensor(
                out=gcols[:, 96 * n + 64:96 * n + 96], in0=gcols[:, 96 * n + 32:96 * n + 64],
                in1=gcols[:, 96 * n:96 * n + 32], op=ALU.subtract),
                reads=[g_b], writes=[g_b])
            kb.op(kb.act, lambda e, n=n: e.activation(out=egi[:, 32 * n:32 * (n + 1)],
                                                      in_=gcols[:, 96 * n:96 * n + 32],
                                                      func=AF.Exp),
                  reads=[g_b], writes=[g_b])
            kb.op(kb.act, lambda e, n=n: e.activation(out=edec[:, 32 * n:32 * (n + 1)],
                                                      in_=gcols[:, 96 * n + 64:96 * n + 96],
                                                      func=AF.Exp),
                  reads=[g_b], writes=[g_b])
            kb.op(kb.act, lambda e, n=n: e.activation(out=ecd[:, 32 * n:32 * (n + 1)],
                                                      in_=gcols[:, 96 * n + 32:96 * n + 64],
                                                      func=AF.Exp),
                  reads=[g_b], writes=[g_b])
            kb.group([lambda e, n=n: e.transpose(psum[4][0:32, 0:128],
                                                 gcols[:, 96 * n:96 * n + 32], c_ident[:])],
                     reads=[g_b, const_b], writes=[ps_b[4]])
            kb.op(kb.dve, lambda e, n=n: e.tensor_copy(out=grows[0:32, 128 * n:128 * (n + 1)],
                                                       in_=psum[4][0:32, 0:128]),
                  reads=[ps_b[4]], writes=[g_b])

        kb.op(kb.dve, lambda e: e.memset(state[:], 0.0), writes=[state_b])
        for h in range(NQH):
            kv = h >> 2
            for n in range(NCH):
                kb.op(kb.dve, lambda e, n=n, kv=kv, h=h: e.tensor_scalar_mul(
                    out=tile_a[:, 0:128], in0=vD[:, 12 + n, kv * 128:(kv + 1) * 128],
                    scalar1=dtcols[:, 32 * n + h:32 * n + h + 1]),
                    reads=[arB_b[12 + n], g_b], writes=[tab_b[0]])
                kb.op(kb.dve, lambda e, n=n, kv=kv, h=h: e.tensor_scalar_mul(
                    out=tile_b[:, 0:128], in0=vBkt[:, n, kv * 128:(kv + 1) * 128],
                    scalar1=edec[:, 32 * n + h:32 * n + h + 1]),
                    reads=[arB_b[n], g_b], writes=[tab_b[1]])
                kb.group([lambda e: e.matmul(psum[4][:, 0:128], tile_b[:, 0:128],
                                             tile_a[:, 0:128], start=True, stop=True)],
                         reads=[tab_b[0], tab_b[1]], writes=[ps_b[4]])
                kb.op(kb.dve, lambda e, n=n, h=h: e.scalar_tensor_tensor(
                    out=state[:, 128 * h:128 * (h + 1)], in0=state[:, 128 * h:128 * (h + 1)],
                    scalar=ecd[:, 32 * n + h:32 * n + h + 1], in1=psum[4][:, 0:128],
                    op0=ALU.mult, op1=ALU.add),
                    reads=[ps_b[4], g_b, state_b], writes=[state_b])
        dma(kb.gpq, cc_in.rearrange("(h d) e -> d h e", d=128),
            state[:].rearrange("p (h e) -> p h e", e=HD),
            reads=[state_b], writes=[ccin_b], stream="ccin")
        kb.op(kb.gpq, lambda e: e.collective_compute(
            "AllGather", ALU.bypass, replica_groups=[[0, 1], [2, 3], [4, 5], [6, 7]],
            ins=[cc_in], outs=[cc_out]), reads=[ccin_b], writes=[ccout_b])
        dma(kb.gpq, S0bf.rearrange("p (h e) -> p h e", e=HD),
            cc_out[0:D, :].rearrange("(h d) e -> d h e", d=128),
            reads=[ccout_b], writes=S0_bufs, stream="s0")
        kb.barrier(kb.gpq, "s0", S0_bufs)

        kb.op(kb.dve, lambda e: e.memset(cumg[:], 0.0), writes=[g_b])
        kb.op(kb.dve, lambda e: e.memset(state[:], 0.0), writes=[state_b])
        for n in range(NCH):
            for hg in range(4):
                dma(kb.act, statr[0:1, 0:1024],
                    grows[8 * hg:8 * hg + 8, 128 * n:128 * (n + 1)],
                    reads=[g_b], writes=[statr_b], stream="gath")
                for hl in range(8):
                    h = 8 * hg + hl
                    kv = h >> 2
                    gcol = gcols[:, 96 * n + h:96 * n + h + 1]
                    gtot = gcols[:, 96 * n + 32 + h:96 * n + 33 + h]
                    kb.op(kb.dve, lambda e, hl=hl: e.tensor_copy(
                        out=rowbuf[0:1, 0:128], in_=statr[0:1, 128 * hl:128 * (hl + 1)]),
                        reads=[statr_b], writes=[rowbuf_b])
                    kb.group([lambda e: e.matmul(psum[5][:, 0:128], c_e0[:],
                                                 rowbuf[:, 0:128], start=True, stop=True)],
                             reads=[rowbuf_b, const_b], writes=[ps_b[5]])
                    dma(kb.spq, qtile[:], qb_hbm[128 * h:128 * (h + 1),
                                                 128 * n:128 * (n + 1)],
                        reads=[qb_b], writes=[qtile_b], stream="qtile")
                    kb.op(kb.act, lambda e: e.activation(
                        out=wrk[2][:, 0:128], in_=psum[5][:, 0:128], func=AF.Exp),
                        reads=[ps_b[5]], writes=[wrk_b[2]])
                    kb.op(kb.dve, lambda e: e.tensor_tensor(
                        out=tile_c[:, 0:128], in0=qtile[:], in1=wrk[2][:, 0:128],
                        op=ALU.mult),
                        reads=[qtile_b, wrk_b[2]], writes=[tab_b[3]])
                    kb.op(kb.dve, lambda e, gcol=gcol: e.scalar_tensor_tensor(
                        out=wrk[0][:, 0:128], in0=psum[5][:, 0:128], scalar=gcol,
                        in1=c_maskTb[:], op0=ALU.subtract, op1=ALU.add),
                        reads=[ps_b[5], g_b, const_b], writes=[wrk_b[0]])
                    kb.op(kb.act, lambda e: e.activation(out=tile_d[:], in_=wrk[0][:, 0:128],
                                                         func=AF.Exp),
                          reads=[wrk_b[0]], writes=[tab_b[4]])
                    kb.group([lambda e, n=n, kv=kv: e.matmul(
                        psum[6][:, 0:128], vC[:, kv, 128 * n:128 * (n + 1)], qtile[:],
                        start=True, stop=True)],
                        reads=[arC_b[kv], qtile_b], writes=[ps_b[6]])
                    kb.op(kb.dve, lambda e: e.tensor_tensor(
                        out=tile_e[:], in0=psum[6][:, 0:128], in1=tile_d[:], op=ALU.mult),
                        reads=[ps_b[6], tab_b[4]], writes=[tab_b[5]])
                    kb.op(kb.dve, lambda e, n=n, kv=kv, h=h: e.tensor_scalar_mul(
                        out=tile_a[:, 0:128], in0=vD[:, 12 + n, kv * 128:(kv + 1) * 128],
                        scalar1=dtcols[:, 32 * n + h:32 * n + h + 1]),
                        reads=[arB_b[12 + n], g_b], writes=[tab_b[0]])
                    builders = [lambda e: e.matmul(psum[7][:, 0:128], tile_e[:],
                                                   tile_a[:, 0:128], start=True,
                                                   stop=(n == 0))]
                    yreads = [tab_b[5], tab_b[0]]
                    if n > 0:
                        builders.append(lambda e, h=h: e.matmul(
                            psum[7][:, 0:128], tile_c[:, 0:128], vBpr[:, 64 + h, :],
                            start=False, stop=True))
                        yreads += [tab_b[3], arB_b[8 + h // 8]]
                    kb.group(builders, reads=yreads, writes=[ps_b[7]])
                    kb.group([lambda e, h=h: e.matmul(psum[5][:, 128:256],
                                                      tile_c[:, 0:128],
                                                      S0bf[:, 128 * h:128 * (h + 1)],
                                                      start=True, stop=True)],
                             reads=[tab_b[3], S0_bufs[h // 4]], writes=[ps_b[5]])
                    kb.op(kb.dve, lambda e, n=n, kv=kv, h=h: e.tensor_scalar_mul(
                        out=tile_b[:, 0:128], in0=vBkt[:, n, kv * 128:(kv + 1) * 128],
                        scalar1=edec[:, 32 * n + h:32 * n + h + 1]),
                        reads=[arB_b[n], g_b], writes=[tab_b[1]])
                    kb.group([lambda e: e.matmul(psum[4][:, 0:128], tile_b[:, 0:128],
                                                 tile_a[:, 0:128], start=True, stop=True)],
                             reads=[tab_b[0], tab_b[1]], writes=[ps_b[4]])
                    kb.op(kb.act, lambda e, h=h: e.activation(out=sco[:, 1:2],
                                                              in_=cumg[:, h:h + 1],
                                                              func=AF.Exp),
                          reads=[g_b], writes=[g_b])
                    kb.op(kb.dve, lambda e: e.tensor_tensor(out=sco[:, 2:3], in0=sco[:, 1:2],
                                                            in1=c_uses0[:], op=ALU.mult),
                          reads=[g_b, const_b], writes=[g_b])
                    kb.op(kb.dve, lambda e: e.tensor_copy(out=wrk[1][:, 0:128],
                                                          in_=psum[7][:, 0:128]),
                          reads=[ps_b[7]], writes=[wrk_b[1]])
                    kb.op(kb.dve, lambda e: e.scalar_tensor_tensor(
                        out=wrk[1][:, 0:128], in0=psum[5][:, 128:256], scalar=sco[:, 2:3],
                        in1=wrk[1][:, 0:128], op0=ALU.mult, op1=ALU.add),
                        reads=[ps_b[5], g_b, wrk_b[1]], writes=[wrk_b[1]])
                    kb.op(kb.dve, lambda e, n=n, h=h: e.scalar_tensor_tensor(
                        out=state[:, 128 * h:128 * (h + 1)],
                        in0=state[:, 128 * h:128 * (h + 1)],
                        scalar=ecd[:, 32 * n + h:32 * n + h + 1], in1=psum[4][:, 0:128],
                        op0=ALU.mult, op1=ALU.add),
                        reads=[ps_b[4], g_b, state_b], writes=[state_b])
                    kb.op(kb.act, lambda e, h=h: e.activation(
                        out=vBpr[:, 64 + h, :], in_=state[:, 128 * h:128 * (h + 1)],
                        func=AF.Copy), reads=[state_b], writes=[arB_b[8 + h // 8]])
                    kb.op(kb.dve, lambda e, h=h, gtot=gtot: e.tensor_tensor(
                        out=cumg[:, h:h + 1], in0=cumg[:, h:h + 1], in1=gtot, op=ALU.add),
                        reads=[g_b], writes=[g_b])
                    kb.op(kb.act, lambda e: e.activation(out=wrk[2][:, 0:128],
                                                         in_=wrk[1][:, 0:128],
                                                         func=AF.Square),
                          reads=[wrk_b[1]], writes=[wrk_b[2]])
                    kb.op(kb.dve, lambda e: e.reduce_sum(out=sco[:, 3:4],
                                                         in_=wrk[2][:, 0:128], axis=AX.X),
                          reads=[wrk_b[2]], writes=[g_b])
                    kb.op(kb.dve, lambda e: e.tensor_scalar(
                        out=sco[:, 4:5], in0=sco[:, 3:4], scalar1=1.0 / HD,
                        scalar2=EPS_RMS, op0=ALU.mult, op1=ALU.add),
                        reads=[g_b], writes=[g_b])
                    kb.op(kb.act, lambda e: e.activation(out=sco[:, 4:5], in_=sco[:, 4:5],
                                                         func=AF.Ln),
                          reads=[g_b], writes=[g_b])
                    kb.op(kb.dve, lambda e: e.tensor_scalar_mul(out=sco[:, 4:5],
                                                                in0=sco[:, 4:5],
                                                                scalar1=-0.5),
                          reads=[g_b], writes=[g_b])
                    kb.op(kb.act, lambda e: e.activation(out=sco[:, 4:5], in_=sco[:, 4:5],
                                                         func=AF.Exp),
                          reads=[g_b], writes=[g_b])
                    kb.op(kb.dve, lambda e: e.scalar_tensor_tensor(
                        out=wrk[2][:, 0:128], in0=wrk[1][:, 0:128], scalar=sco[:, 4:5],
                        in1=c_gnw[:], op0=ALU.mult, op1=ALU.mult),
                        reads=[wrk_b[1], g_b, const_b], writes=[wrk_b[2]])
                    dma(kb.spq, tile_f[:],
                        gate_hbm[128 * n:128 * (n + 1), 128 * h:128 * (h + 1)],
                        reads=[gate_b], writes=[tab_b[2]], stream="gtile")
                    kb.op(kb.dve, lambda e: e.tensor_tensor(out=tile_e[:],
                                                            in0=wrk[2][:, 0:128],
                                                            in1=tile_f[:], op=ALU.mult),
                          reads=[wrk_b[2], tab_b[2]], writes=[tab_b[5]])
                    dmaT(tile_d[:], tile_e[:], reads=[tab_b[5]], writes=[tab_b[4]],
                         stream="trY")
                    dma(kb.gpq, y_hbm[128 * h:128 * (h + 1), 128 * n:128 * (n + 1)],
                        tile_d[:], reads=[tab_b[4]], writes=[y_b], stream="y")

        for hh in range(2):
            o5 = 512 * hh

            def wo_evac(m, j, hh=hh, o5=o5):
                dma(kb.spq, xst[2][:, 0:512], xtile(src, hh, m),
                    reads=[src_b[m]], writes=[xst_b[2]], stream="xst2")
                kb.op(kb.dve, lambda e, j=j: e.tensor_tensor(
                    out=wrk[0][:, 0:512], in0=psum[j][:, 0:512], in1=xst[2][:, 0:512],
                    op=ALU.add), reads=[ps_b[j], xst_b[2]], writes=[wrk_b[0]])
                dma(kb.gpq, xtile(dst, hh, m), wrk[0][:, 0:512],
                    reads=[wrk_b[0]], writes=[dst_b[m]], stream=dst_stream)

            def rhs_fn(k, hh=hh, o5=o5):
                i = k % 2
                dma(kb.spq, yb[i][:, 0:512], y_hbm[128 * k:128 * (k + 1), o5:o5 + 512],
                    reads=[y_b], writes=[yb_b[i]], stream=f"yb{i}")
                return (yb[i][:, 0:512], yb_b[i])

            gemm1(wblk(wobT, 32),
                  32, list(range(32)), rhs_fn, wo_evac)


    order = []
    if nsub == -2:
        order = [("f", ffn)]
    if nsub == -3:
        order = [("b", attn_b)]
    if nsub >= 1:
        order.append(("a", attn_a))
    if nsub >= 2:
        order.append(("f", ffn))
    if nsub >= 3:
        order.append(("b", attn_b))
    if nsub >= 4:
        order.append(("f", ffn))

    cur_src, cur_src_b = x_own, [xown_b] * 32
    ping = 0
    for i, (kind, fn) in enumerate(order):
        last = (i == len(order) - 1)
        dst = out_fm if last else xs[ping]
        dst_b = out_b if last else xs_b[ping]
        dst_stream = "out" if last else f"xs{ping}"
        if kind == "a":
            attn_a(dst, dst_b, dst_stream)
        else:
            fn(cur_src, cur_src_b, dst, dst_b, dst_stream)
        kb.barrier(kb.gpq, dst_stream, dst_b)
        cur_src, cur_src_b = dst, dst_b
        ping ^= 1

    final_waits = sorted({b.w for b in out_b if b.w is not None})

    with nc.Block() as block:
        with ExitStack() as sems:
            kb.make_sems(nc, sems)
            kb.finalize(block, final_waits)

    es.close()
    return nc


_NC_CACHE = {}


def kernel(**inputs):
    in_maps = host_prep(inputs)
    key = NSUB
    if key not in _NC_CACHE:
        _NC_CACHE[key] = build_nc(NSUB)
    nc = _NC_CACHE[key]
    res = run_bass_kernel_spmd(nc, in_maps, core_ids=list(range(8)))
    out = np.zeros((B, S, D), np.float32)
    for c in range(8):
        b, h = c >> 1, c & 1
        fm = np.asarray(res.results[c]["out_fm"], np.float32).reshape(2, 32, 128, 512)
        out[b, h * T:(h + 1) * T] = fm.transpose(0, 3, 1, 2).reshape(T, D)
    return out


# version 08 (final)
# speedup vs baseline: 1.1149x; 1.0059x over previous
import numpy as np
import ml_dtypes
from contextlib import ExitStack

import concourse.bass as bass
import concourse.mybir as mybir
from concourse.bass_utils import run_bass_kernel_spmd

F32 = mybir.dt.float32
BF16 = mybir.dt.bfloat16
I32 = mybir.dt.int32
AF = mybir.ActivationFunctionType
ALU = mybir.AluOpType
AX = mybir.AxisListType

B, S, D = 4, 2048, 4096
T = 1024
NQH, NKV, HD = 32, 8, 128
BLK = 64
KVT = 2048
WIN = 960
NKC = 11
KEYS = NKC * 128
IFF = 2048
NCH = 8
EPS_LN = 1e-5
EPS_RMS = 1e-5
SM_SCALE = 1.0 / HD
NEG = -1e30
NSUB = 4
DEBUG = None


def _emask():
    nb = S // BLK
    qb = np.arange(nb)[:, None]
    kb = np.arange(nb)[None, :]
    bm = (kb <= qb) & ((qb - kb < 16) | ((kb + 1) % 8 == 0))
    em = np.kron(bm, np.ones((BLK, BLK), dtype=bool))
    i = np.arange(S)
    em &= i[:, None] >= i[None, :]
    return em


def _kchunk_starts(p):
    return [128 * p + 128 * c for c in range(9)] + [384, 896]


def _blk(wT, nk):
    K, M = wT.shape
    ng = M // 512
    return np.ascontiguousarray(
        wT[:, :512 * ng].reshape(nk, 128, ng, 512)
          .transpose(2, 0, 1, 3).reshape(ng * nk * 128, 512))


def host_prep(inputs):
    f32 = np.float32
    x = np.asarray(inputs["L0_attn_a_hidden_states"], f32)

    wqkv = np.asarray(inputs["L0_attn_a_w_qkv"], f32)
    bqkv = np.asarray(inputs["L0_attn_a_b_qkv"], f32)
    rows = np.arange(6144).reshape(NKV, 6, HD)
    q_rows = rows[:, :4].reshape(-1)
    kv_rows = np.concatenate([rows[:, 4].reshape(-1), rows[:, 5].reshape(-1)])
    m0 = dict(
        wqaT=_blk(wqkv[q_rows].T, 32),
        wkvaT=_blk(wqkv[kv_rows].T, 32),
        bqa=np.ascontiguousarray(bqkv[q_rows].reshape(NQH, HD).T),
        bkva=np.ascontiguousarray(bqkv[kv_rows].reshape(16, HD).T),
        wdaT=_blk(np.asarray(inputs["L0_attn_a_w_dense"], f32).T, 32),
        bda=np.ascontiguousarray(np.asarray(inputs["L0_attn_a_b_dense"], f32).reshape(32, HD).T),
        wgT=_blk(np.asarray(inputs["L0_ffn_w_gate"], f32).T, 32),
        wuT=_blk(np.asarray(inputs["L0_ffn_w_up"], f32).T, 32),
        wdfT=_blk(np.asarray(inputs["L0_ffn_w_down"], f32).T, 16),
        s_wg_c=np.ascontiguousarray(np.asarray(inputs["L0_ffn_s_wgate"], f32).reshape(16, HD).T),
        s_wu_c=np.ascontiguousarray(np.asarray(inputs["L0_ffn_s_wup"], f32).reshape(16, HD).T),
        s_wd_c=np.ascontiguousarray(np.asarray(inputs["L0_ffn_s_wdown"], f32).reshape(32, HD).T),
        inv_g_rep=np.broadcast_to(np.asarray(inputs["L0_ffn_inv_gate"], f32), (128, D)).copy(),
        inv_u_rep=np.broadcast_to(np.asarray(inputs["L0_ffn_inv_up"], f32), (128, D)).copy(),
        inv_i_rep=np.broadcast_to(np.asarray(inputs["L0_ffn_inv_inter"], f32), (128, IFF)).copy(),
        wqbT=_blk(np.asarray(inputs["L0_attn_b_Wq"], f32).T, 32),
        wgbT=_blk(np.asarray(inputs["L0_attn_b_Wg"], f32).T, 32),
        wobT=_blk(np.asarray(inputs["L0_attn_b_Wo"], f32).T, 32),
        dtb_rep=np.broadcast_to(np.asarray(inputs["L0_attn_b_dt_bias"], f32), (128, 32)).copy(),
        alog_rep=np.broadcast_to(np.asarray(inputs["L0_attn_b_A_log_bias"], f32), (128, 32)).copy(),
        gnw_rep=np.broadcast_to(np.asarray(inputs["L0_attn_b_gnorm_w"], f32), (128, HD)).copy(),
        ident=np.eye(128, dtype=f32),
    )
    winp = np.zeros((128, D), f32)
    winp[:32] = np.asarray(inputs["L0_attn_b_Win"], f32)
    wkvd_full = np.concatenate(
        [np.asarray(inputs["L0_attn_b_Wk"], f32),
         np.asarray(inputs["L0_attn_b_Wv"], f32), winp], 0).T
    m0["wkvdT"] = _blk(wkvd_full, 32)
    m0["wkvdT_t"] = np.ascontiguousarray(wkvd_full[:, 2048:])
    e0 = np.zeros((128, 128), f32)
    e0[0, :] = 1.0
    m0["e0"] = e0
    m0["ltri"] = (np.arange(128)[:, None] <= np.arange(128)[None, :]).astype(f32)
    m0["maskTb"] = np.where(np.arange(128)[None, :] >= np.arange(128)[:, None],
                            0.0, NEG).astype(f32)

    ifa = 1.0 / (1e6 ** (np.arange(0, HD, 2, dtype=np.float64) / HD))
    ifb = 1.0 / (1e4 ** (np.arange(0, HD, 2, dtype=np.float64) / HD))
    em = _emask()

    in_maps = []
    for c in range(8):
        b, h = c >> 1, c & 1
        m = dict(m0)
        g0 = (16 * h - 15) * BLK
        xall = np.zeros((KVT, D), f32)
        lo = max(0, -g0)
        hi = min(WIN + T, S - g0)
        xall[lo:hi] = x[b, g0 + lo:g0 + hi]
        m["x_all"] = _blk(np.ascontiguousarray(xall.T), 32)
        m["x_own"] = _blk(np.ascontiguousarray(x[b, h * T:(h + 1) * T].T), 32)

        pos = np.maximum(g0 + np.arange(KVT, dtype=np.float64), 0.0)
        ang = pos[None, :] * np.concatenate([ifa, ifa])[:, None]
        m["cosa"] = np.cos(ang).astype(f32)
        m["sina"] = np.sin(ang).astype(f32)

        masks = np.full((8, 128, KEYS), NEG, f32)
        for p in range(8):
            qg = h * T + 128 * p + np.arange(128)
            seen = set()
            for ci, st in enumerate(_kchunk_starts(p)):
                for kj in range(128):
                    kv = st + kj
                    g = g0 + kv
                    if g < 0 or g >= S or kv >= WIN + T or kv in seen:
                        continue
                    seen.add(kv)
                    masks[p, :, ci * 128 + kj] = np.where(em[qg, g], 0.0, NEG)
        m["maska"] = np.ascontiguousarray(
            masks.transpose(1, 0, 2).reshape(128, 8 * KEYS)).astype(ml_dtypes.bfloat16)

        posb = h * T + np.arange(T, dtype=np.float64)
        angb = posb[None, :] * np.concatenate([ifb, ifb])[:, None]
        m["cosb"] = np.cos(angb).astype(f32)
        m["sinb"] = np.sin(angb).astype(f32)
        m["use_s0"] = np.full((128, 1), float(h), f32)
        in_maps.append(m)
    return in_maps


class Buf:
    __slots__ = ("w", "r")

    def __init__(self):
        self.w = None
        self.r = []


class ET:
    def __init__(self, name, attr):
        self.name = name
        self.attr = attr
        self.cval = 0
        self.dval = {}
        self.hw = {}
        self.ops = []
        self.csem = None
        self.dsems = {}


class KB:
    def __init__(self):
        self.pe = ET("pe", "tensor")
        self.dve = ET("dve", "vector")
        self.act = ET("act", "scalar")
        self.gpq = ET("gpq", "gpsimd")
        self.spq = ET("spq", "sync")
        self.ets = {e.name: e for e in (self.pe, self.dve, self.act, self.gpq, self.spq)}

    def _sem(self, key):
        e = self.ets[key[0]]
        return e.csem if key[1] == "c" else e.dsems[key[2]]

    def _collect(self, et, reads, writes, extra):
        deps = list(extra)
        for b in reads:
            if b.w is not None:
                deps.append(b.w)
        for b in writes:
            deps.extend(b.r)
            if b.w is not None:
                deps.append(b.w)
        need = {}
        for key, v in deps:
            if et.hw.get(key, 0) >= v:
                continue
            need[key] = max(need.get(key, 0), v)
        for key, v in need.items():
            et.hw[key] = v
        return list(need.items())

    @staticmethod
    def _finish(tok, reads, writes):
        for b in reads:
            b.r.append(tok)
        for b in writes:
            b.w = tok
            b.r = []
        return tok

    def op(self, et, build, reads=(), writes=(), extra=(), dma=False, fuse=True,
           stream=None):
        waits = self._collect(et, reads, writes, extra)
        if dma:
            assert stream is not None
            et.dval[stream] = et.dval.get(stream, 0) + 16
            et.dsems.setdefault(stream, None)
            tok = ((et.name, "d", stream), et.dval[stream])
            fuse = False
        else:
            et.cval += 1
            tok = ((et.name, "c"), et.cval)
        kb = self

        def closure(eng, waits=waits, build=build, et=et, tok=tok, fuse=fuse,
                    dma=dma, stream=stream):
            std = waits[:-1] if (fuse and waits) else waits
            for key, v in std:
                eng.wait_ge(kb._sem(key), v)
            inst = build(eng)
            if fuse and waits:
                key, v = waits[-1]
                inst._wait_ge(kb._sem(key), v)
            inst.then_inc(et.csem if not dma else et.dsems[stream], 1 if not dma else 16)

        et.ops.append(closure)
        return self._finish(tok, reads, writes)

    def fence(self, et, bufs):
        tok = ((et.name, "c"), et.cval)
        for b in bufs:
            b.r.append(tok)

    def barrier(self, et, stream, bufs):
        v = et.dval.get(stream, 0)
        if v == 0:
            return
        tok = ((et.name, "d", stream), v)
        for b in bufs:
            if b.w is not None and b.w[0] == tok[0]:
                b.w = tok

    def group(self, builders, reads=(), writes=(), extra=()):
        et = self.pe
        waits = self._collect(et, reads, writes, extra)
        et.cval += 1
        tok = ((et.name, "c"), et.cval)
        kb = self

        def closure(eng, waits=waits, builders=builders, et=et):
            for key, v in waits:
                eng.wait_ge(kb._sem(key), v)
            inst = None
            for b in builders:
                inst = b(eng)
            inst.then_inc(et.csem, 1)

        et.ops.append(closure)
        return self._finish(tok, reads, writes)

    def finalize(self, block, final_waits):
        kb = self
        for et in self.ets.values():
            def run(eng, ops=et.ops):
                for f in ops:
                    f(eng)
            getattr(block, et.attr)(run)

        def tail(eng):
            for key, v in final_waits:
                eng.wait_ge(kb._sem(key), v)
        block.sync(tail)

    def make_sems(self, nc, sems):
        for et in self.ets.values():
            et.csem = sems.enter_context(nc.semaphore(f"s_{et.name}_c"))
            for st in list(et.dsems):
                et.dsems[st] = sems.enter_context(nc.semaphore(f"s_{et.name}_{st}"))


def build_nc(nsub=None, debug=None):
    if nsub is None:
        nsub = NSUB
    if debug is None:
        debug = DEBUG
    nc = bass.Bass("TRN2", target_bir_lowering=False)
    kb = KB()

    def din(name, shape, dt=F32):
        return nc.dram_tensor(name, shape, dt, kind="ExternalInput").ap()

    x_all = din("x_all", [4 * D, 512])
    x_own = din("x_own", [2 * D, 512])
    wqaT = din("wqaT", [8 * D, 512])
    wkvaT = din("wkvaT", [4 * D, 512])
    bqa_in = din("bqa", [128, 32])
    bkva_in = din("bkva", [128, 16])
    wdaT = din("wdaT", [8 * D, 512])
    bda_in = din("bda", [128, 32])
    cosa_in = din("cosa", [128, KVT])
    sina_in = din("sina", [128, KVT])
    maska_in = din("maska", [128, 8 * KEYS], BF16)
    wgT_in = din("wgT", [4 * D, 512])
    wuT_in = din("wuT", [4 * D, 512])
    wdfT_in = din("wdfT", [4 * D, 512])
    swg_in = din("s_wg_c", [128, 16])
    swu_in = din("s_wu_c", [128, 16])
    swd_in = din("s_wd_c", [128, 32])
    invg_in = din("inv_g_rep", [128, D])
    invu_in = din("inv_u_rep", [128, D])
    invi_in = din("inv_i_rep", [128, IFF])
    wqbT = din("wqbT", [8 * D, 512])
    wkvdT = din("wkvdT", [4 * D, 512])
    wkvdT_t = din("wkvdT_t", [D, 128])
    wgbT = din("wgbT", [8 * D, 512])
    wobT = din("wobT", [8 * D, 512])
    dtb_in = din("dtb_rep", [128, 32])
    alog_in = din("alog_rep", [128, 32])
    gnw_in = din("gnw_rep", [128, HD])
    ident_in = din("ident", [128, 128])
    e0_in = din("e0", [128, 128])
    ltri_in = din("ltri", [128, 128])
    maskTb_in = din("maskTb", [128, 128])
    cosb_in = din("cosb", [128, T])
    sinb_in = din("sinb", [128, T])
    uses0_in = din("use_s0", [128, 1])

    out_fm = nc.dram_tensor("out_fm", [2 * D, 512], F32, kind="ExternalOutput").ap()

    xs = [nc.dram_tensor(f"xs{i}", [2 * D, 512], F32).ap() for i in range(2)]
    qa_hbm = nc.dram_tensor("qa_hbm", [D, T], BF16).ap()
    qb_hbm = nc.dram_tensor("qb_hbm", [D, T], BF16).ap()
    gate_hbm = nc.dram_tensor("gate_hbm", [T, D], BF16).ap()
    y_hbm = nc.dram_tensor("y_hbm", [D, T], BF16).ap()
    cc_in = nc.dram_tensor("cc_in", [D, HD], F32).ap()
    cc_out = nc.dram_tensor("cc_out", [2 * D, HD], F32).ap()

    xall_b, xown_b = Buf(), Buf()
    xs_b = [[Buf() for _ in range(32)] for _ in range(2)]
    qa_b, qb_b, gate_b, y_b = Buf(), Buf(), Buf(), Buf()
    ccin_b, ccout_b = Buf(), Buf()
    out_b = [Buf() for _ in range(32)]

    es = ExitStack()

    def sbuf(name, shape, dt=F32):
        return es.enter_context(nc.sbuf_tensor(name, shape, dt))

    c_ident = sbuf("c_ident", [128, 128])
    c_e0 = sbuf("c_e0", [128, 128])
    c_ones = sbuf("c_ones", [128, 128])
    c_onesb = sbuf("c_onesb", [128, 128], BF16)
    c_ltri = sbuf("c_ltri", [128, 128])
    c_maskTb = sbuf("c_maskTb", [128, 128])
    c_permR = sbuf("c_permR", [128, 128], BF16)
    c_bqa = sbuf("c_bqa", [128, 32])
    c_bkva = sbuf("c_bkva", [128, 16])
    c_bda = sbuf("c_bda", [128, 32])
    c_swg = sbuf("c_swg", [128, 16])
    c_swu = sbuf("c_swu", [128, 16])
    c_swd = sbuf("c_swd", [128, 32])
    c_dtb = sbuf("c_dtb", [128, 32])
    c_negA = sbuf("c_negA", [128, 32])
    c_gnw = sbuf("c_gnw", [128, HD])
    c_uses0 = sbuf("c_uses0", [128, 1])

    xst = [sbuf(f"xst{i}", [128, 512]) for i in range(3)]
    wb = [sbuf(f"wb{i}", [128, 512], BF16) for i in range(4)]
    yb = [sbuf(f"yb{i}", [128, 512], BF16) for i in range(2)]
    rowbuf = sbuf("rowbuf", [128, 512])
    statr = sbuf("statr", [128, 1024])
    rstd_rep = sbuf("rstd_rep", [128, 512])
    negm_rep = sbuf("negm_rep", [128, 512])
    cos_sb = sbuf("cos_sb", [128, 512])
    sin_sb = sbuf("sin_sb", [128, 512])
    red = sbuf("red", [128, 16])
    probs = sbuf("probs", [128, KEYS], BF16)
    probsT = sbuf("probsT", [128, KEYS], BF16)
    mbuf = sbuf("mbuf", [128, KEYS], BF16)
    qtile = sbuf("qtile", [128, 128], BF16)
    qtile2 = sbuf("qtile2", [128, 128], BF16)
    reda = sbuf("reda", [128, 32])
    c_identb = sbuf("c_identb", [128, 128], BF16)
    qtileX = [qtile, qtile2]
    tile_a = sbuf("tile_a", [128, 512], BF16)
    tile_b = sbuf("tile_b", [128, 512], BF16)
    tile_c = sbuf("tile_cs", [128, 128], BF16)
    tile_d = sbuf("tile_d", [128, 128], BF16)
    tile_e = sbuf("tile_e", [128, 128], BF16)
    tile_f = sbuf("tile_f", [128, 128], BF16)
    wrk = [sbuf(f"wrk{i}", [128, 1024]) for i in range(3)]
    wrki = sbuf("wrki", [128, 1024], I32)
    invb = sbuf("invb", [128, 1024])
    state = sbuf("state", [128, NQH * HD])
    dtfm = sbuf("dtfm", [128, T])
    gcols = sbuf("gcols", [128, NCH * 96])
    grows = sbuf("grows", [128, NCH * 128])
    dtcols = sbuf("dtcols", [128, NCH * 32])
    egi = sbuf("egi", [128, NCH * 32])
    edec = sbuf("edec", [128, NCH * 32])
    ecd = sbuf("ecd", [128, NCH * 32])
    cumg = sbuf("cumg", [128, 32])
    sco = sbuf("sco", [128, 8])
    scq = sbuf("scq", [128, 16])
    scqr = sbuf("scqr", [128, 16])

    arA = sbuf("arA", [128, 32 * 512], BF16)
    arB = sbuf("arB", [128, 32 * 1024], BF16)
    arC = sbuf("arC", [128, 8 * 1024], BF16)

    psum = [es.enter_context(nc.psum_tensor(f"pb{i}", [128, 512], F32)) for i in range(8)]

    const_b = Buf()
    xst_b = [Buf() for _ in range(3)]
    wb_b = [Buf() for _ in range(4)]
    yb_b = [Buf() for _ in range(2)]
    rowbuf_b, statr_b, rep_b, cosb_b, red_b, sc_b = Buf(), Buf(), Buf(), Buf(), Buf(), Buf()
    probs_b, probsT_b, mbuf_b, qtile_b = Buf(), Buf(), Buf(), Buf()
    probsX_bL = [Buf(), Buf()]
    probsX_bH = [Buf(), Buf()]
    probsTX_bL = [Buf(), Buf()]
    probsTX_bH = [Buf(), Buf()]
    qtileX_b = [qtile_b, Buf()]
    reda_b = [Buf(), Buf()]
    tab_b = [Buf() for _ in range(6)]
    wrk_b = [Buf() for _ in range(3)]
    wrki_b = Buf()
    invb_b = Buf()
    state_b, S0_b, dtfm_b = Buf(), Buf(), Buf()
    g_b = Buf()
    scq_b = Buf()
    arA_b = [Buf() for _ in range(32)]
    arB_b = [Buf() for _ in range(32)]
    arC_b = [Buf() for _ in range(8)]
    ps_b = [Buf() for _ in range(8)]
    srep = [rstd_rep, negm_rep]
    srep_b = [rep_b, rep_b]
    S0bf = arA[:][:, 0:NQH * HD // 4 * 0 + 4096]
    S0_bufs = [arA_b[i] for i in range(8)]

    vA = arA[:].rearrange("p (c t) -> p c t", t=512)

    qbf = [arA[:, 0:1024], arA[:, 1024:2048]]
    qbf_bufs = [[arA_b[0], arA_b[1]], [arA_b[2], arA_b[3]]]

    def vBkT(kv):
        return arB[:].rearrange("p (kv t) -> p kv t", t=KVT)[:, kv, :]

    vBv = arB[:].rearrange("p (a ti r) -> p a ti r", a=2, r=1024)
    vBx = arB[:].rearrange("p (ti f) -> p ti f", f=D)
    vBq = arB[:].rearrange("p (a c t) -> p a c t", a=2, t=512)
    vBi = arB[:].rearrange("p (x c t) -> p x c t", x=4, t=512)
    vBit = arB[:].rearrange("p (y ti f) -> p y ti f", y=4, f=IFF)
    vBkt = arB[:].rearrange("p (ti r) -> p ti r", r=1024)
    vBpr = arB[:].rearrange("p (c t) -> p c t", t=128)
    vC = arC[:].rearrange("p (c t) -> p c t", t=1024)
    vC5 = arC[:].rearrange("p (c t) -> p c t", t=512)
    vC5 = arC[:].rearrange("p (c t) -> p c t", t=512)
    vD = arB[:].rearrange("p (c t) -> p c t", t=1024)

    probsA = [probs[:], arC[:, 0:KEYS]]
    probsTA = [probsT[:], arC[:, KEYS:2 * KEYS]]
    probsTC = [probsT[:].rearrange("p (c t) -> p c t", t=128),
               arC[:, KEYS:2 * KEYS].rearrange("p (c t) -> p c t", t=128)]

    wring = [0]

    def wtile(src_ap, width=512):
        i = wring[0] % 4
        wring[0] += 1
        kb.op(kb.gpq, lambda e, d=wb[i][:, 0:width], s=src_ap:
              e.dma_start(out=d, in_=s), writes=[wb_b[i]], dma=True, stream=f"wb{i}")
        return wb[i], wb_b[i]

    def wblk(W, nk):
        def fn(k, m0, mw, W=W, nk=nk):
            r = ((m0 >> 2) * nk + k) * 128
            return W[r:r + 128, 0:128 * mw]
        return fn

    def xtile(tsr, slab, c):
        r = (slab * 32 + c) * 128
        return tsr[r:r + 128, 0:512]

    def dma(et, dst, src, reads=(), writes=(), stream="misc"):
        return kb.op(et, lambda e, d=dst, s=src: e.dma_start(out=d, in_=s),
                     reads=list(reads), writes=list(writes), dma=True, stream=stream)

    def dmaT(dst, src, reads=(), writes=(), stream="trA"):
        return kb.op(kb.act, lambda e, d=dst, s=src: e.dma_start_transpose(out=d, in_=s),
                     reads=list(reads), writes=list(writes), dma=True, stream=stream)

    for dst, src in [
        (c_ident[:], ident_in), (c_e0[:], e0_in), (c_ltri[:], ltri_in),
        (c_maskTb[:], maskTb_in), (c_bqa[:], bqa_in), (c_bkva[:], bkva_in),
        (c_bda[:], bda_in), (c_swg[:], swg_in), (c_swu[:], swu_in),
        (c_swd[:], swd_in), (c_dtb[:], dtb_in), (c_gnw[:], gnw_in),
        (c_uses0[:], uses0_in),
        (c_negA[:], alog_in),
    ]:
        dma(kb.gpq, dst, src, writes=[const_b], stream="const")
    dma(kb.gpq, c_identb[:], ident_in, writes=[const_b], stream="const")
    kb.op(kb.dve, lambda e: e.memset(c_ones[:], 1.0), writes=[const_b])
    kb.op(kb.dve, lambda e: e.memset(c_onesb[:], 1.0), writes=[const_b])
    kb.op(kb.dve, lambda e: e.memset(rowbuf[:], 0.0), writes=[rowbuf_b])
    kb.op(kb.act, lambda e: e.activation(out=c_negA[:], in_=c_negA[:], func=AF.Exp),
          reads=[const_b], writes=[const_b])
    kb.op(kb.dve, lambda e: e.tensor_scalar_mul(out=c_negA[:], in0=c_negA[:], scalar1=-1.0),
          reads=[const_b], writes=[const_b])
    kb.op(kb.dve, lambda e: e.memset(c_permR[:], 0.0), writes=[const_b])
    kb.op(kb.dve, lambda e: e.tensor_scalar_mul(out=c_permR[64:128, 0:64],
                                                in0=c_ident[0:64, 0:64], scalar1=-1.0),
          reads=[const_b], writes=[const_b])
    kb.op(kb.dve, lambda e: e.tensor_copy(out=c_permR[0:64, 64:128], in_=c_ident[0:64, 0:64]),
          reads=[const_b], writes=[const_b])

    def row_broadcast(row_ap, row_buf, dst_ap, dst_buf, width=512, pbi=5):
        kb.op(kb.dve, lambda e: e.tensor_copy(out=rowbuf[0:1, 0:width], in_=row_ap),
              reads=[row_buf], writes=[rowbuf_b])
        kb.group([lambda e: e.matmul(psum[pbi][:, 0:width], c_e0[:], rowbuf[:, 0:width],
                                     start=True, stop=True)],
                 reads=[rowbuf_b, const_b], writes=[ps_b[pbi]])
        kb.op(kb.dve, lambda e: e.tensor_copy(out=dst_ap, in_=psum[pbi][:, 0:width]),
              reads=[ps_b[pbi]], writes=[dst_buf])

    def layernorm(src_fn, src_buf_fn):
        for c in range(32):
            st, stb = xst[c % 2], xst_b[c % 2]
            dma(kb.spq, st[:, 0:512], src_fn(c), reads=[src_buf_fn(c)], writes=[stb],
                stream=f"xst{c % 2}")
            kb.op(kb.dve, lambda e, c=c, st=st: e.tensor_copy(out=vA[:, c, :], in_=st[:, 0:512]),
                  reads=[stb], writes=[arA_b[c]])
            kb.op(kb.act, lambda e, st=st: e.activation(out=wrk[2][:, 0:512], in_=st[:, 0:512],
                                                        func=AF.Square),
                  reads=[stb], writes=[wrk_b[2]])
            kb.group([lambda e, c=c: e.matmul(psum[6][:, 0:512], c_onesb[:], vA[:, c, :],
                                              start=(c == 0), stop=(c == 31))],
                     reads=[arA_b[c], const_b], writes=[ps_b[6]])
            kb.group([lambda e, c=c: e.matmul(psum[7][:, 0:512], c_ones[:], wrk[2][:, 0:512],
                                              start=(c == 0), stop=(c == 31))],
                     reads=[wrk_b[2], const_b], writes=[ps_b[7]])
        r0 = lambda a: statr[0:1, a:a + 512]
        kb.op(kb.dve, lambda e: e.tensor_scalar_mul(out=r0(0), in0=psum[6][0:1, 0:512],
                                                    scalar1=1.0 / D),
              reads=[ps_b[6]], writes=[statr_b])
        kb.op(kb.dve, lambda e: e.tensor_scalar_mul(out=r0(512), in0=psum[7][0:1, 0:512],
                                                    scalar1=1.0 / D),
              reads=[ps_b[7], statr_b], writes=[statr_b])
        kb.op(kb.dve, lambda e: e.scalar_tensor_tensor(out=rowbuf[0:1, 0:512], in0=r0(0),
                                                       scalar=-1.0, in1=r0(0),
                                                       op0=ALU.mult, op1=ALU.mult),
              reads=[statr_b], writes=[rowbuf_b])
        kb.op(kb.dve, lambda e: e.tensor_tensor(out=r0(512), in0=r0(512),
                                                in1=rowbuf[0:1, 0:512], op=ALU.add),
              reads=[statr_b, rowbuf_b], writes=[statr_b])
        kb.op(kb.dve, lambda e: e.tensor_scalar_add(out=r0(512), in0=r0(512),
                                                    scalar1=EPS_LN),
              reads=[statr_b], writes=[statr_b])
        kb.op(kb.act, lambda e: e.activation(out=r0(512), in_=r0(512), func=AF.Ln),
              reads=[statr_b], writes=[statr_b])
        kb.op(kb.dve, lambda e: e.tensor_scalar_mul(out=r0(512), in0=r0(512),
                                                    scalar1=-0.5),
              reads=[statr_b], writes=[statr_b])
        kb.op(kb.act, lambda e: e.activation(out=r0(512), in_=r0(512), func=AF.Exp),
              reads=[statr_b], writes=[statr_b])
        kb.op(kb.dve, lambda e: e.tensor_tensor(out=r0(0), in0=r0(0), in1=r0(512),
                                                op=ALU.mult),
              reads=[statr_b], writes=[statr_b])
        kb.op(kb.dve, lambda e: e.tensor_scalar_mul(out=r0(0), in0=r0(0), scalar1=-1.0),
              reads=[statr_b], writes=[statr_b])
        row_broadcast(r0(512), statr_b, rstd_rep[:], rep_b)
        row_broadcast(r0(0), statr_b, negm_rep[:], rep_b)
        for c in range(32):
            w, wbuf = wrk[c % 2], wrk_b[c % 2]
            kb.op(kb.dve, lambda e, c=c, w=w: e.tensor_tensor(out=w[:, 0:512],
                                                              in0=vA[:, c, :], in1=rstd_rep[:],
                                                              op=ALU.mult),
                  reads=[arA_b[c], rep_b], writes=[wbuf])
            kb.op(kb.dve, lambda e, c=c, w=w: e.tensor_tensor(out=vA[:, c, :], in0=w[:, 0:512],
                                                              in1=negm_rep[:], op=ALU.add),
                  reads=[wbuf, rep_b], writes=[arA_b[c]])

    def gemm1(wT_ap_fn, nk, m_list, rhs_fn, evac_fn, nw=512):
        seen = []
        for g0 in range(0, len(m_list), 4):
            mg = m_list[g0:g0 + 4]
            for k in range(nk):
                wt, wtb = wtile(wT_ap_fn(k, mg[0], len(mg)), width=128 * len(mg))
                rhs, rhsb = rhs_fn(k)
                if rhsb not in seen:
                    seen.append(rhsb)
                builders = []
                for j in range(len(mg)):
                    builders.append(lambda e, j=j, k=k, wt=wt, rhs=rhs, nk=nk:
                                    e.matmul(psum[j][:, 0:nw], wt[:, 128 * j:128 * (j + 1)],
                                             rhs, start=(k == 0), stop=(k == nk - 1)))
                kb.group(builders, reads=[wtb, rhsb, const_b],
                         writes=[ps_b[j] for j in range(len(mg))])
            for j, m in enumerate(mg):
                evac_fn(m, j)
        kb.fence(kb.pe, seen)

    def rope_evac(ps_idx, nw, bias_col, cos_ap, sin_ap, dst_ap, dst_buf, tab_i=0):
        kw = dict(func=AF.Copy)
        if bias_col is not None:
            kw = dict(func=AF.Identity, bias=bias_col)
        kb.op(kb.act, lambda e, kw=kw: e.activation(out=tile_a[:, 0:nw],
                                                    in_=psum[ps_idx][:, 0:nw], **kw),
              reads=[ps_b[ps_idx], const_b], writes=[tab_b[0]])
        kb.group([lambda e: e.matmul(psum[4][:, 0:nw], c_permR[:], tile_a[:, 0:nw],
                                     start=True, stop=True)],
                 reads=[tab_b[0], const_b], writes=[ps_b[4]])
        kb.op(kb.dve, lambda e: e.tensor_tensor(out=wrk[0][:, 0:nw], in0=psum[4][:, 0:nw],
                                                in1=sin_ap, op=ALU.mult),
              reads=[ps_b[4], cosb_b, const_b], writes=[wrk_b[0]])
        kb.op(kb.dve, lambda e: e.tensor_tensor(out=wrk[1][:, 0:nw], in0=tile_a[:, 0:nw],
                                                in1=cos_ap, op=ALU.mult),
              reads=[tab_b[0], cosb_b, const_b], writes=[wrk_b[1]])
        kb.op(kb.dve, lambda e: e.tensor_tensor(out=dst_ap, in0=wrk[0][:, 0:nw],
                                                in1=wrk[1][:, 0:nw], op=ALU.add),
              reads=[wrk_b[0], wrk_b[1]], writes=[dst_buf])

    def quant_all(tok_fn, tok_bufs_fn, n_m, nfeat, inv_hbm, sc_base, qsink):
        nseg = nfeat // 1024
        for s in range(nseg):
            iv, ivb = invb, invb_b
            dma(kb.gpq, iv[:], inv_hbm[:, 1024 * s:1024 * (s + 1)], writes=[ivb],
                stream="invb")
            for m in range(n_m):
                kb.op(kb.dve, lambda e, m=m, s=s, iv=iv: e.tensor_tensor(
                    out=wrk[0][:], in0=tok_fn(m)[:, 1024 * s:1024 * (s + 1)], in1=iv[:],
                    op=ALU.mult), reads=list(tok_bufs_fn(m, s)) + [ivb],
                    writes=[wrk_b[0]])
                if s == 0:
                    kb.op(kb.dve, lambda e, m=m: e.reduce_max(
                        out=red[:, m:m + 1], in_=wrk[0][:], axis=AX.X,
                        apply_absolute_value=True), reads=[wrk_b[0]], writes=[red_b])
                else:
                    kb.op(kb.dve, lambda e: e.reduce_max(
                        out=red[:, 8:9], in_=wrk[0][:], axis=AX.X,
                        apply_absolute_value=True), reads=[wrk_b[0]], writes=[red_b])
                    kb.op(kb.dve, lambda e, m=m: e.tensor_tensor(
                        out=red[:, m:m + 1], in0=red[:, m:m + 1], in1=red[:, 8:9],
                        op=ALU.max), reads=[red_b], writes=[red_b])
        for m in range(n_m):
            kb.op(kb.dve, lambda e, m=m: e.tensor_scalar(
                out=scq[:, sc_base + m:sc_base + m + 1], in0=red[:, m:m + 1],
                scalar1=1.0 / 127.0, scalar2=1e-8, op0=ALU.mult, op1=ALU.max),
                reads=[red_b], writes=[scq_b])
            kb.op(kb.dve, lambda e, m=m: e.reciprocal(
                out=scqr[:, sc_base + m:sc_base + m + 1],
                in_=scq[:, sc_base + m:sc_base + m + 1]),
                reads=[scq_b], writes=[scq_b])
        t = 0
        for s in range(nseg):
            iv, ivb = invb, invb_b
            dma(kb.gpq, iv[:], inv_hbm[:, 1024 * s:1024 * (s + 1)], writes=[ivb],
                stream="invb")
            for m in range(n_m):
                kb.op(kb.dve, lambda e, m=m, s=s, iv=iv: e.tensor_tensor(
                    out=wrk[0][:], in0=tok_fn(m)[:, 1024 * s:1024 * (s + 1)], in1=iv[:],
                    op=ALU.mult), reads=list(tok_bufs_fn(m, s)) + [ivb],
                    writes=[wrk_b[0]])
                kb.op(kb.dve, lambda e, m=m: e.tensor_scalar_mul(
                    out=wrk[0][:], in0=wrk[0][:],
                    scalar1=scqr[:, sc_base + m:sc_base + m + 1]),
                    reads=[wrk_b[0], scq_b], writes=[wrk_b[0]])
                kb.op(kb.dve, lambda e: e.tensor_copy(out=wrki[:], in_=wrk[0][:]),
                      reads=[wrk_b[0]], writes=[wrki_b])
                kb.op(kb.dve, lambda e: e.tensor_copy(out=wrk[1][:], in_=wrki[:]),
                      reads=[wrki_b], writes=[wrk_b[1]])
                kb.op(kb.dve, lambda e: e.tensor_tensor(
                    out=wrk[2][:], in0=wrk[0][:], in1=wrk[1][:], op=ALU.subtract),
                    reads=[wrk_b[0], wrk_b[1]], writes=[wrk_b[2]])
                kb.op(kb.dve, lambda e: e.scalar_tensor_tensor(
                    out=wrk[1][:], in0=wrk[2][:], scalar=0.5, in1=wrk[1][:],
                    op0=ALU.is_ge, op1=ALU.add), reads=[wrk_b[2], wrk_b[1]],
                    writes=[wrk_b[1]])
                kb.op(kb.dve, lambda e: e.scalar_tensor_tensor(
                    out=wrk[1][:], in0=wrk[2][:], scalar=-0.5, in1=wrk[1][:],
                    op0=ALU.is_ge, op1=ALU.add), reads=[wrk_b[2], wrk_b[1]],
                    writes=[wrk_b[1]])
                q, qbs = qbf[t % 2], qbf_bufs[t % 2]
                kb.op(kb.dve, lambda e, q=q: e.tensor_scalar_add(
                    out=q, in0=wrk[1][:], scalar1=-1.0),
                    reads=[wrk_b[1]], writes=list(qbs))
                qsink(m, s, q, qbs)
                t += 1

    def scale_row_rep(cols_ap, n_m, dst, dst_buf):
        kb.group([lambda e: e.transpose(psum[4][0:n_m, 0:128], cols_ap, c_ident[:])],
                 reads=[scq_b, const_b], writes=[ps_b[4]])
        kb.op(kb.dve, lambda e: e.tensor_copy(out=statr[0:n_m, 0:128],
                                              in_=psum[4][0:n_m, 0:128]),
              reads=[ps_b[4]], writes=[statr_b])
        for m in range(n_m):
            dma(kb.act, statr[0:1, 128 + 128 * m:128 + 128 * (m + 1)],
                statr[m:m + 1, 0:128], reads=[statr_b], writes=[statr_b],
                stream="srow")
        row_broadcast(statr[0:1, 128:128 + 128 * n_m], statr_b, dst[:, 0:128 * n_m],
                      dst_buf, width=128 * n_m)

    def attn_a(dst, dst_b, dst_stream):
        for qr in range(4):
            q0 = 512 * qr
            layernorm(lambda c, qr=qr: xtile(x_all, qr, c),
                      lambda c: xall_b)
            dma(kb.spq, cos_sb[:], cosa_in[:, q0:q0 + 512], writes=[cosb_b], stream="cos")
            dma(kb.spq, sin_sb[:], sina_in[:, q0:q0 + 512], writes=[cosb_b], stream="cos")

            def kv_evac(m, j, q0=q0):
                half = 1 if q0 >= 1024 else 0
                toff = q0 % 1024
                if m < 8:
                    kv = m
                    slot = kv * 2 + half
                    rope_evac(j, 512, c_bkva[:, kv:kv + 1], cos_sb[:], sin_sb[:],
                              vBkT(kv)[:, q0:q0 + 512], arB_b[slot])
                else:
                    kv = m - 8
                    kb.op(kb.act, lambda e, j=j, kv=kv: e.activation(
                        out=tile_a[:, 0:512], in_=psum[j][:, 0:512], func=AF.Identity,
                        bias=c_bkva[:, 8 + kv:9 + kv]),
                        reads=[ps_b[j], const_b], writes=[tab_b[0]])
                    ti0 = q0 // 128
                    dmaT(vBv[:, 1, ti0:ti0 + 4, kv * 128:(kv + 1) * 128], tile_a[:, 0:512],
                         reads=[tab_b[0]], writes=[arB_b[16 + ti0 + i] for i in range(4)],
                         stream="trV")

            gemm1(wblk(wkvaT, 32),
                  32, list(range(16)), lambda k: (vA[:, k, :], arA_b[k]), kv_evac)

            own_lo, own_hi = max(q0, WIN), min(q0 + 512, WIN + T)
            if own_hi > own_lo:
                nw = own_hi - own_lo
                toff = own_lo - q0
                kb.op(kb.dve, lambda e: e.tensor_scalar_mul(out=cos_sb[:], in0=cos_sb[:],
                                                            scalar1=SM_SCALE),
                      reads=[cosb_b], writes=[cosb_b])
                kb.op(kb.dve, lambda e: e.tensor_scalar_mul(out=sin_sb[:], in0=sin_sb[:],
                                                            scalar1=SM_SCALE),
                      reads=[cosb_b], writes=[cosb_b])

                def q_evac(m, j, nw=nw, toff=toff, oc=own_lo - WIN):
                    rope_evac(j, nw, c_bqa[:, m:m + 1], cos_sb[:, toff:toff + nw],
                              sin_sb[:, toff:toff + nw], tile_b[:, 0:nw], tab_b[1])
                    dma(kb.gpq, qa_hbm[128 * m:128 * (m + 1), oc:oc + nw], tile_b[:, 0:nw],
                        reads=[tab_b[1]], writes=[qa_b], stream="qa")

                gemm1(wblk(wqaT, 32),
                      32, list(range(32)),
                      lambda k, toff=toff, nw=nw: (vA[:, k, toff:toff + nw], arA_b[k]),
                      q_evac, nw=nw)

        kb.barrier(kb.act, "trV", arB_b[16:32])
        for hh in range(2):
            iters = [(4 * hh + p4, h) for p4 in range(4) for h in range(NQH)]
            n_it = len(iters)

            def scores_stage(t):
                p, h = iters[t]
                u = t % 2
                kv = h >> 2
                if h == 0:
                    dma(kb.spq, mbuf[:], maska_in[:, KEYS * p:KEYS * (p + 1)],
                        writes=[mbuf_b], stream="mbuf")
                dma(kb.spq, qtileX[u][:], qa_hbm[128 * h:128 * (h + 1), 128 * p:128 * (p + 1)],
                    reads=[qa_b], writes=[qtileX_b[u]], stream="qtile")
                starts = _kchunk_starts(p)
                segs = [(0, 0, starts[0], 512), (1, 0, starts[4], 512),
                        (2, 0, starts[8], 128), (2, 128, starts[9], 128),
                        (2, 256, starts[10], 128)]
                builders, reads = [], [qtileX_b[u], mbuf_b, const_b]
                for (pi, po, st, wdt) in segs:
                    builders.append(lambda e, pi=pi, po=po, st=st, wdt=wdt, kv=kv, u=u:
                                    e.matmul(psum[4 * u + pi][:, po:po + wdt], qtileX[u][:],
                                             vBkT(kv)[:, st:st + wdt],
                                             start=True, stop=False))
                    reads.append(arB_b[kv * 2 + (1 if st >= 1024 else 0)])
                    if (st % 1024) + wdt > 1024:
                        reads.append(arB_b[kv * 2 + 1])
                for (pi, wdt, mo) in [(0, 512, 0), (1, 512, 512), (2, 384, 1024)]:
                    builders.append(lambda e, pi=pi, wdt=wdt, mo=mo, u=u:
                                    e.matmul(psum[4 * u + pi][:, 0:wdt], c_identb[:],
                                             mbuf[:, mo:mo + wdt],
                                             start=False, stop=True))
                kb.group(builders, reads=reads,
                         writes=[ps_b[4 * u], ps_b[4 * u + 1], ps_b[4 * u + 2]])

            def softmax_stage(t):
                u = t % 2
                ro = 16 * u
                for s, (pi, wdt) in enumerate([(0, 512), (1, 512), (2, 384)]):
                    kb.op(kb.dve, lambda e, pi=pi, wdt=wdt, s=s, u=u, ro=ro: e.reduce_max(
                        out=reda[:, ro + s:ro + s + 1], in_=psum[4 * u + pi][:, 0:wdt],
                        axis=AX.X), reads=[ps_b[4 * u + pi]], writes=[reda_b[u]])
                kb.op(kb.dve, lambda e, ro=ro: e.reduce_max(
                    out=reda[:, ro + 3:ro + 4], in_=reda[:, ro:ro + 3], axis=AX.X),
                    reads=[reda_b[u]], writes=[reda_b[u]])
                kb.op(kb.dve, lambda e, ro=ro: e.tensor_scalar_mul(
                    out=reda[:, ro + 4:ro + 5], in0=reda[:, ro + 3:ro + 4], scalar1=-1.0),
                    reads=[reda_b[u]], writes=[reda_b[u]])
                for s, (pi, wdt, o, hi) in enumerate([(0, 512, 0, 0), (1, 512, 512, 0),
                                                      (2, 384, 1024, 1)]):
                    kb.op(kb.act, lambda e, pi=pi, wdt=wdt, o=o, s=s, u=u, ro=ro:
                          e.activation(out=probsA[u][:, o:o + wdt],
                                       in_=psum[4 * u + pi][:, 0:wdt],
                                       func=AF.Exp, bias=reda[:, ro + 4:ro + 5],
                                       accum_out=reda[:, ro + 5 + s:ro + 6 + s]),
                          reads=[ps_b[4 * u + pi], reda_b[u]],
                          writes=[probsX_bH[u] if hi else probsX_bL[u], reda_b[u]])
                kb.op(kb.dve, lambda e, ro=ro: e.tensor_tensor(
                    out=reda[:, ro + 8:ro + 9], in0=reda[:, ro + 5:ro + 6],
                    in1=reda[:, ro + 6:ro + 7], op=ALU.add),
                    reads=[reda_b[u]], writes=[reda_b[u]])
                kb.op(kb.dve, lambda e, ro=ro: e.tensor_tensor(
                    out=reda[:, ro + 9:ro + 10], in0=reda[:, ro + 8:ro + 9],
                    in1=reda[:, ro + 7:ro + 8], op=ALU.add),
                    reads=[reda_b[u]], writes=[reda_b[u]])
                kb.op(kb.dve, lambda e, ro=ro: e.reciprocal(
                    out=reda[:, ro + 10:ro + 11], in_=reda[:, ro + 9:ro + 10]),
                    reads=[reda_b[u]], writes=[reda_b[u]])
                kb.op(kb.act, lambda e, u=u, ro=ro: e.activation(
                    out=probsA[u][:, 0:1024], in_=probsA[u][:, 0:1024], func=AF.Identity,
                    scale=reda[:, ro + 10:ro + 11]),
                    reads=[probsX_bL[u], reda_b[u]], writes=[probsX_bL[u]])
                dmaT(probsTC[u][:, 0:8, :],
                     probsA[u][:, 0:1024], reads=[probsX_bL[u]], writes=[probsTX_bL[u]],
                     stream="trP")
                kb.op(kb.dve, lambda e, u=u, ro=ro: e.tensor_scalar_mul(
                    out=probsA[u][:, 1024:KEYS], in0=probsA[u][:, 1024:KEYS],
                    scalar1=reda[:, ro + 10:ro + 11]),
                    reads=[probsX_bH[u], reda_b[u]], writes=[probsX_bH[u]])
                dmaT(probsTC[u][:, 8:11, :],
                     probsA[u][:, 1024:KEYS], reads=[probsX_bH[u]], writes=[probsTX_bH[u]],
                     stream="trP")

            def ctx_stage(t):
                p, h = iters[t]
                u = t % 2
                p4 = p % 4
                kv = h >> 2
                starts = _kchunk_starts(p)
                builders, reads = [], [probsTX_bL[u], probsTX_bH[u]]
                for ci, st in enumerate(starts):
                    ti = st // 128
                    builders.append(lambda e, ci=ci, ti=ti, kv=kv, u=u:
                                    e.matmul(psum[4 * u + 3][:, 0:128],
                                             vBv[:, 1, ti, kv * 128:(kv + 1) * 128],
                                             probsTA[u][:, 128 * ci:128 * (ci + 1)],
                                             start=(ci == 0), stop=(ci == 10)))
                    reads.append(arB_b[16 + ti])
                kb.group(builders, reads=reads, writes=[ps_b[4 * u + 3]])
                kb.op(kb.act, lambda e, h=h, p4=p4, u=u: e.activation(
                    out=vA[:, h, 128 * p4:128 * (p4 + 1)], in_=psum[4 * u + 3][:, 0:128],
                    func=AF.Copy),
                    reads=[ps_b[4 * u + 3]], writes=[arA_b[h]])

            for t in range(n_it + 2):
                if t < n_it:
                    scores_stage(t)
                if 1 <= t <= n_it:
                    softmax_stage(t - 1)
                if t >= 2:
                    ctx_stage(t - 2)

            def d_evac(m, j, hh=hh):
                dma(kb.spq, xst[2][:, 0:512], xtile(x_own, hh, m),
                    reads=[xown_b], writes=[xst_b[2]], stream="xst2")
                kb.op(kb.dve, lambda e, m=m, j=j: e.scalar_tensor_tensor(
                    out=wrk[0][:, 0:512], in0=psum[j][:, 0:512], scalar=c_bda[:, m:m + 1],
                    op0=ALU.add, in1=xst[2][:, 0:512], op1=ALU.add),
                    reads=[ps_b[j], xst_b[2], const_b], writes=[wrk_b[0]])
                dma(kb.gpq, xtile(dst, hh, m),
                    wrk[0][:, 0:512], reads=[wrk_b[0]], writes=[dst_b[m]],
                    stream=dst_stream)

            kb.fence(kb.pe, arB_b)
            gemm1(wblk(wdaT, 32),
                  32, list(range(32)),
                  lambda k: (vA[:, k, :], arA_b[k]),
                  d_evac)

    def ffn(src, src_b, dst, dst_b, dst_stream):
        for hh in range(2):
            o5 = 512 * hh
            layernorm(lambda c, hh=hh: xtile(src, hh, c),
                      lambda c: src_b[c])
            if debug == "xn":
                for c in range(32):
                    dma(kb.gpq, xtile(dst, hh, c), vA[:, c, :],
                        reads=[arA_b[c]], writes=[dst_b[c]], stream="dump")
                continue
            for c in range(32):
                dmaT(vBx[:, 0:4, 128 * c:128 * (c + 1)], vA[:, c, :],
                     reads=[arA_b[c]],
                     writes=[arB_b[4 * ti + c // 8] for ti in range(4)], stream="trX")
            kb.barrier(kb.act, "trX", arB_b[0:16])
            for path in range(2):
                def qsink(m, s, q, qbs):
                    dmaT(vBq[:, 1, 8 * s:8 * s + 8, 128 * m:128 * (m + 1)],
                         q, reads=list(qbs),
                         writes=[arB_b[16 + 4 * s + i] for i in range(4)], stream="trQ")
                quant_all(lambda m: vBx[:, m, :],
                          lambda m, s: [arB_b[4 * m + s]],
                          4, D, invg_in if path == 0 else invu_in, 4 * path, qsink)
                kb.fence(kb.dve, arB_b[0:16])
                kb.barrier(kb.act, "trQ", arB_b[16:32])
                scale_row_rep(scq[:, 4 * path:4 * path + 4], 4, srep[path], srep_b[path])
                if debug == "qg" and path == 0:
                    for k in range(32):
                        dma(kb.gpq, xtile(dst, hh, k),
                            vBq[:, 1, k, :], reads=[arB_b[16 + k // 2]],
                            writes=[dst_b[k]], stream="dump")
                    break

                def gu_evac(m, j, path=path):
                    if debug == "gacc" and path == 0:
                        kb.op(kb.dve, lambda e, j=j: e.tensor_copy(
                            out=wrk[0][:, 0:512], in_=psum[j][:, 0:512]),
                            reads=[ps_b[j]], writes=[wrk_b[0]])
                        dma(kb.gpq, xtile(dst, hh, m),
                            wrk[0][:, 0:512], reads=[wrk_b[0]], writes=[dst_b[m]],
                            stream="dump")
                        return
                    kb.op(kb.act, lambda e, m=m, j=j, path=path: e.activation(
                        out=wrk[0][:, 0:512], in_=psum[j][:, 0:512], func=AF.Identity,
                        scale=(c_swg if path == 0 else c_swu)[:, m:m + 1]),
                        reads=[ps_b[j], const_b], writes=[wrk_b[0]])
                    kb.op(kb.dve, lambda e, path=path: e.tensor_tensor(
                        out=wrk[1][:, 0:512], in0=wrk[0][:, 0:512], in1=srep[path][:],
                        op=ALU.mult),
                        reads=[wrk_b[0], srep_b[path]], writes=[wrk_b[1]])
                    if path == 0:
                        kb.op(kb.act, lambda e: e.activation(
                            out=wrk[0][:, 0:512], in_=wrk[1][:, 0:512], func=AF.Silu),
                            reads=[wrk_b[1]], writes=[wrk_b[0]])
                        kb.op(kb.dve, lambda e, m=m: e.tensor_copy(
                            out=vC5[:, m, :], in_=wrk[0][:, 0:512]),
                            reads=[wrk_b[0]], writes=[arC_b[m // 2]])
                        kb.op(kb.dve, lambda e, m=m: e.tensor_tensor(
                            out=vA[:, 16 + m, :], in0=wrk[0][:, 0:512], in1=vC5[:, m, :],
                            op=ALU.subtract),
                            reads=[wrk_b[0], arC_b[m // 2]], writes=[arA_b[16 + m]])
                    else:
                        kb.op(kb.dve, lambda e, m=m: e.tensor_tensor(
                            out=wrk[0][:, 0:512], in0=vC5[:, m, :],
                            in1=wrk[1][:, 0:512], op=ALU.mult),
                            reads=[arC_b[m // 2], wrk_b[1]], writes=[wrk_b[0]])
                        kb.op(kb.dve, lambda e, m=m: e.tensor_tensor(
                            out=wrk[2][:, 0:512], in0=vA[:, 16 + m, :],
                            in1=wrk[1][:, 0:512], op=ALU.mult),
                            reads=[arA_b[16 + m], wrk_b[1]], writes=[wrk_b[2]])
                        kb.op(kb.dve, lambda e: e.tensor_tensor(
                            out=wrk[0][:, 0:512], in0=wrk[0][:, 0:512],
                            in1=wrk[2][:, 0:512], op=ALU.add),
                            reads=[wrk_b[0], wrk_b[2]], writes=[wrk_b[0]])
                        kb.op(kb.dve, lambda e, m=m: e.tensor_scalar(
                            out=vC5[:, m, :], in0=wrk[0][:, 0:512], scalar1=10.0,
                            scalar2=-10.0, op0=ALU.min, op1=ALU.max),
                            reads=[wrk_b[0]], writes=[arC_b[m // 2]])

                gemm1(wblk(wgT_in if path == 0 else wuT_in, 32),
                      32, list(range(16)),
                      lambda k: (vBq[:, 1, k, :], arB_b[16 + k // 2]), gu_evac)
            if debug == "qg":
                continue
            if debug in ("gate", "gacc"):
                for m in range(16):
                    dma(kb.gpq, xtile(dst, hh, m), vC5[:, m, :],
                        reads=[arC_b[m // 2]], writes=[dst_b[m]], stream="dump")
                continue
            kb.fence(kb.pe, arB_b[0:16])
            for c in range(16):
                dmaT(vBit[:, 0, 0:4, 128 * c:128 * (c + 1)], vC5[:, c, :],
                     reads=[arC_b[c // 2]],
                     writes=[arB_b[2 * ti + c // 8] for ti in range(4)], stream="trI")
            kb.barrier(kb.act, "trI", arB_b[0:8])
            def qsinki(m, s, q, qbs):
                dmaT(vBi[:, 1, 8 * s:8 * s + 8, 128 * m:128 * (m + 1)], q,
                     reads=list(qbs), writes=[arB_b[8 + 4 * s + i] for i in range(4)],
                     stream="trJ")
            quant_all(lambda m: vBit[:, 0, m, :],
                      lambda m, s: [arB_b[2 * m + s]],
                      4, IFF, invi_in, 8, qsinki)
            kb.fence(kb.dve, arB_b[0:8])
            kb.barrier(kb.act, "trJ", arB_b[8:16])
            scale_row_rep(scq[:, 8:12], 4, srep[0], srep_b[0])
            if debug == "qi":
                for k in range(16):
                    dma(kb.gpq, xtile(dst, hh, k),
                        vBi[:, 1, k, :], reads=[arB_b[8 + k // 2]], writes=[dst_b[k]],
                        stream="dump")
                continue

            def d_evac(m, j, hh=hh):
                kb.op(kb.act, lambda e, m=m, j=j: e.activation(
                    out=wrk[0][:, 0:512], in_=psum[j][:, 0:512], func=AF.Identity,
                    scale=c_swd[:, m:m + 1]),
                    reads=[ps_b[j], const_b], writes=[wrk_b[0]])
                dma(kb.spq, xst[2][:, 0:512], xtile(src, hh, m),
                    reads=[src_b[m]], writes=[xst_b[2]], stream="xst2")
                kb.op(kb.dve, lambda e: e.tensor_tensor(
                    out=wrk[1][:, 0:512], in0=wrk[0][:, 0:512], in1=srep[0][:],
                    op=ALU.mult), reads=[wrk_b[0], srep_b[0]], writes=[wrk_b[1]])
                kb.op(kb.dve, lambda e: e.tensor_tensor(
                    out=wrk[0][:, 0:512], in0=wrk[1][:, 0:512], in1=xst[2][:, 0:512],
                    op=ALU.add), reads=[wrk_b[1], xst_b[2]], writes=[wrk_b[0]])
                dma(kb.gpq, xtile(dst, hh, m), wrk[0][:, 0:512],
                    reads=[wrk_b[0]], writes=[dst_b[m]], stream=dst_stream)

            gemm1(wblk(wdfT_in, 16),
                  16, list(range(32)), lambda k: (vBi[:, 1, k, :], arB_b[8 + k // 2]),
                  d_evac)

    def attn_b(src, src_b, dst, dst_b, dst_stream):
        for hh in range(2):
            o5 = 512 * hh
            layernorm(lambda c, hh=hh: xtile(src, hh, c),
                      lambda c: src_b[c])
            dma(kb.spq, cos_sb[:], cosb_in[:, o5:o5 + 512], writes=[cosb_b], stream="cos")
            dma(kb.spq, sin_sb[:], sinb_in[:, o5:o5 + 512], writes=[cosb_b], stream="cos")

            def kvd_evac(m, j, hh=hh, o5=o5):
                if m < 8:
                    kv = m
                    rope_evac(j, 512, None, cos_sb[:], sin_sb[:],
                              vC[:, kv, o5:o5 + 512], arC_b[kv])
                elif m < 16:
                    kv = m - 8
                    kb.op(kb.act, lambda e, j=j: e.activation(
                        out=tile_a[:, 0:512], in_=psum[j][:, 0:512], func=AF.Copy),
                        reads=[ps_b[j]], writes=[tab_b[0]])
                    ti0 = 4 * hh
                    dmaT(vD[:, 12 + ti0:12 + ti0 + 4, kv * 128:(kv + 1) * 128], tile_a[:, 0:512],
                         reads=[tab_b[0]], writes=[arB_b[12 + ti0 + i] for i in range(4)],
                         stream="trV")
                else:
                    kb.op(kb.act, lambda e, j=j, o5=o5: e.activation(
                        out=dtfm[:, o5:o5 + 512], in_=psum[j][:, 0:512], func=AF.Copy),
                        reads=[ps_b[j]], writes=[dtfm_b])

            gemm1(lambda k, m0, mw: (wkvdT[((m0 >> 2) * 32 + k) * 128:
                                           ((m0 >> 2) * 32 + k) * 128 + 128, 0:512]
                                     if mw == 4 else
                                     wkvdT_t[128 * k:128 * (k + 1), 0:128]),
                  32, list(range(17)), lambda k: (vA[:, k, :], arA_b[k]), kvd_evac)

            def qb_evac(m, j, o5=o5):
                rope_evac(j, 512, None, cos_sb[:], sin_sb[:],
                          tile_b[:, 0:512], tab_b[1])
                dma(kb.gpq, qb_hbm[128 * m:128 * (m + 1), o5:o5 + 512], tile_b[:, 0:512],
                    reads=[tab_b[1]], writes=[qb_b], stream="qb")

            gemm1(wblk(wqbT, 32),
                  32, list(range(32)), lambda k: (vA[:, k, :], arA_b[k]), qb_evac)

            def gb_evac(m, j, hh=hh, o5=o5):
                kb.op(kb.act, lambda e, j=j: e.activation(
                    out=tile_a[:, 0:512], in_=psum[j][:, 0:512], func=AF.Silu),
                    reads=[ps_b[j]], writes=[tab_b[0]])
                dmaT(tile_b[:, 0:512].rearrange("p (ti t) -> p ti t", t=128),
                     tile_a[:, 0:512], reads=[tab_b[0]], writes=[tab_b[1]], stream="trG")
                dma(kb.gpq,
                    gate_hbm[o5:o5 + 512, 128 * m:128 * (m + 1)].rearrange(
                        "(ti p) d -> p ti d", p=128),
                    tile_b[:, 0:512].rearrange("p (ti t) -> p ti t", t=128),
                    reads=[tab_b[1]], writes=[gate_b], stream="gate")

            gemm1(wblk(wgbT, 32),
                  32, list(range(32)), lambda k: (vA[:, k, :], arA_b[k]), gb_evac)

        for kv in range(NKV):
            dmaT(vBkt[:, 0:8, kv * 128:(kv + 1) * 128], vC[:, kv, 0:1024],
                 reads=[arC_b[kv]], writes=[arB_b[i] for i in range(8)], stream="trK")
        kb.barrier(kb.act, "trK", arB_b[0:8])
        kb.barrier(kb.act, "trV", arB_b[12:20])

        for n in range(NCH):
            kb.group([lambda e, n=n: e.transpose(psum[4][:, 0:128],
                                                 dtfm[:, 128 * n:128 * (n + 1)], c_ident[:])],
                     reads=[dtfm_b, const_b], writes=[ps_b[4]])
            kb.op(kb.dve, lambda e: e.tensor_tensor(out=wrk[0][:, 0:32],
                                                    in0=psum[4][:, 0:32], in1=c_dtb[:, 0:32],
                                                    op=ALU.add),
                  reads=[ps_b[4], const_b], writes=[wrk_b[0]])
            kb.op(kb.act, lambda e: e.activation(out=wrk[1][:, 0:32], in_=wrk[0][:, 0:32],
                                                 func=AF.Exp),
                  reads=[wrk_b[0]], writes=[wrk_b[1]])
            kb.op(kb.dve, lambda e: e.tensor_scalar_add(out=wrk[1][:, 0:32],
                                                        in0=wrk[1][:, 0:32], scalar1=1.0),
                  reads=[wrk_b[1]], writes=[wrk_b[1]])
            kb.op(kb.act, lambda e, n=n: e.activation(out=dtcols[:, 32 * n:32 * (n + 1)],
                                                      in_=wrk[1][:, 0:32], func=AF.Ln),
                  reads=[wrk_b[1]], writes=[g_b])
            kb.op(kb.dve, lambda e, n=n: e.tensor_tensor(out=wrk[0][:, 0:32],
                                                         in0=dtcols[:, 32 * n:32 * (n + 1)],
                                                         in1=c_negA[:, 0:32], op=ALU.mult),
                  reads=[g_b, const_b], writes=[wrk_b[0]])
            kb.group([lambda e: e.matmul(psum[5][:, 0:32], c_ltri[:], wrk[0][:, 0:32],
                                         start=True, stop=True),
                      lambda e: e.matmul(psum[5][:, 64:96], c_ones[:], wrk[0][:, 0:32],
                                         start=True, stop=True)],
                     reads=[wrk_b[0], const_b], writes=[ps_b[5]])
            kb.op(kb.dve, lambda e, n=n: e.tensor_copy(out=gcols[:, 96 * n:96 * n + 32],
                                                       in_=psum[5][:, 0:32]),
                  reads=[ps_b[5]], writes=[g_b])
            kb.op(kb.dve, lambda e, n=n: e.tensor_copy(out=gcols[:, 96 * n + 32:96 * n + 64],
                                                       in_=psum[5][:, 64:96]),
                  reads=[ps_b[5]], writes=[g_b])
            kb.op(kb.dve, lambda e, n=n: e.tensor_tensor(
                out=gcols[:, 96 * n + 64:96 * n + 96], in0=gcols[:, 96 * n + 32:96 * n + 64],
                in1=gcols[:, 96 * n:96 * n + 32], op=ALU.subtract),
                reads=[g_b], writes=[g_b])
            kb.op(kb.act, lambda e, n=n: e.activation(out=egi[:, 32 * n:32 * (n + 1)],
                                                      in_=gcols[:, 96 * n:96 * n + 32],
                                                      func=AF.Exp),
                  reads=[g_b], writes=[g_b])
            kb.op(kb.act, lambda e, n=n: e.activation(out=edec[:, 32 * n:32 * (n + 1)],
                                                      in_=gcols[:, 96 * n + 64:96 * n + 96],
                                                      func=AF.Exp),
                  reads=[g_b], writes=[g_b])
            kb.op(kb.act, lambda e, n=n: e.activation(out=ecd[:, 32 * n:32 * (n + 1)],
                                                      in_=gcols[:, 96 * n + 32:96 * n + 64],
                                                      func=AF.Exp),
                  reads=[g_b], writes=[g_b])
            kb.group([lambda e, n=n: e.transpose(psum[4][0:32, 0:128],
                                                 gcols[:, 96 * n:96 * n + 32], c_ident[:])],
                     reads=[g_b, const_b], writes=[ps_b[4]])
            kb.op(kb.dve, lambda e, n=n: e.tensor_copy(out=grows[0:32, 128 * n:128 * (n + 1)],
                                                       in_=psum[4][0:32, 0:128]),
                  reads=[ps_b[4]], writes=[g_b])

        kb.op(kb.dve, lambda e: e.memset(state[:], 0.0), writes=[state_b])
        for h in range(NQH):
            kv = h >> 2
            for n in range(NCH):
                kb.op(kb.dve, lambda e, n=n, kv=kv, h=h: e.tensor_scalar_mul(
                    out=tile_a[:, 0:128], in0=vD[:, 12 + n, kv * 128:(kv + 1) * 128],
                    scalar1=dtcols[:, 32 * n + h:32 * n + h + 1]),
                    reads=[arB_b[12 + n], g_b], writes=[tab_b[0]])
                kb.op(kb.dve, lambda e, n=n, kv=kv, h=h: e.tensor_scalar_mul(
                    out=tile_b[:, 0:128], in0=vBkt[:, n, kv * 128:(kv + 1) * 128],
                    scalar1=edec[:, 32 * n + h:32 * n + h + 1]),
                    reads=[arB_b[n], g_b], writes=[tab_b[1]])
                kb.group([lambda e: e.matmul(psum[4][:, 0:128], tile_b[:, 0:128],
                                             tile_a[:, 0:128], start=True, stop=True)],
                         reads=[tab_b[0], tab_b[1]], writes=[ps_b[4]])
                kb.op(kb.dve, lambda e, n=n, h=h: e.scalar_tensor_tensor(
                    out=state[:, 128 * h:128 * (h + 1)], in0=state[:, 128 * h:128 * (h + 1)],
                    scalar=ecd[:, 32 * n + h:32 * n + h + 1], in1=psum[4][:, 0:128],
                    op0=ALU.mult, op1=ALU.add),
                    reads=[ps_b[4], g_b, state_b], writes=[state_b])
        dma(kb.gpq, cc_in.rearrange("(h d) e -> d h e", d=128),
            state[:].rearrange("p (h e) -> p h e", e=HD),
            reads=[state_b], writes=[ccin_b], stream="ccin")
        kb.op(kb.gpq, lambda e: e.collective_compute(
            "AllGather", ALU.bypass, replica_groups=[[0, 1], [2, 3], [4, 5], [6, 7]],
            ins=[cc_in], outs=[cc_out]), reads=[ccin_b], writes=[ccout_b])
        dma(kb.gpq, S0bf.rearrange("p (h e) -> p h e", e=HD),
            cc_out[0:D, :].rearrange("(h d) e -> d h e", d=128),
            reads=[ccout_b], writes=S0_bufs, stream="s0")
        kb.barrier(kb.gpq, "s0", S0_bufs)

        kb.op(kb.dve, lambda e: e.memset(cumg[:], 0.0), writes=[g_b])
        kb.op(kb.dve, lambda e: e.memset(state[:], 0.0), writes=[state_b])
        for n in range(NCH):
            for hg in range(4):
                dma(kb.act, statr[0:1, 0:1024],
                    grows[8 * hg:8 * hg + 8, 128 * n:128 * (n + 1)],
                    reads=[g_b], writes=[statr_b], stream="gath")
                for hl in range(8):
                    h = 8 * hg + hl
                    kv = h >> 2
                    gcol = gcols[:, 96 * n + h:96 * n + h + 1]
                    gtot = gcols[:, 96 * n + 32 + h:96 * n + 33 + h]
                    kb.op(kb.dve, lambda e, hl=hl: e.tensor_copy(
                        out=rowbuf[0:1, 0:128], in_=statr[0:1, 128 * hl:128 * (hl + 1)]),
                        reads=[statr_b], writes=[rowbuf_b])
                    kb.group([lambda e: e.matmul(psum[5][:, 0:128], c_e0[:],
                                                 rowbuf[:, 0:128], start=True, stop=True)],
                             reads=[rowbuf_b, const_b], writes=[ps_b[5]])
                    dma(kb.spq, qtile[:], qb_hbm[128 * h:128 * (h + 1),
                                                 128 * n:128 * (n + 1)],
                        reads=[qb_b], writes=[qtile_b], stream="qtile")
                    kb.op(kb.act, lambda e: e.activation(
                        out=wrk[2][:, 0:128], in_=psum[5][:, 0:128], func=AF.Exp),
                        reads=[ps_b[5]], writes=[wrk_b[2]])
                    kb.op(kb.dve, lambda e: e.tensor_tensor(
                        out=tile_c[:, 0:128], in0=qtile[:], in1=wrk[2][:, 0:128],
                        op=ALU.mult),
                        reads=[qtile_b, wrk_b[2]], writes=[tab_b[3]])
                    kb.op(kb.dve, lambda e, gcol=gcol: e.scalar_tensor_tensor(
                        out=wrk[0][:, 0:128], in0=psum[5][:, 0:128], scalar=gcol,
                        in1=c_maskTb[:], op0=ALU.subtract, op1=ALU.add),
                        reads=[ps_b[5], g_b, const_b], writes=[wrk_b[0]])
                    kb.op(kb.act, lambda e: e.activation(out=tile_d[:], in_=wrk[0][:, 0:128],
                                                         func=AF.Exp),
                          reads=[wrk_b[0]], writes=[tab_b[4]])
                    kb.group([lambda e, n=n, kv=kv: e.matmul(
                        psum[6][:, 0:128], vC[:, kv, 128 * n:128 * (n + 1)], qtile[:],
                        start=True, stop=True)],
                        reads=[arC_b[kv], qtile_b], writes=[ps_b[6]])
                    kb.op(kb.dve, lambda e: e.tensor_tensor(
                        out=tile_e[:], in0=psum[6][:, 0:128], in1=tile_d[:], op=ALU.mult),
                        reads=[ps_b[6], tab_b[4]], writes=[tab_b[5]])
                    kb.op(kb.dve, lambda e, n=n, kv=kv, h=h: e.tensor_scalar_mul(
                        out=tile_a[:, 0:128], in0=vD[:, 12 + n, kv * 128:(kv + 1) * 128],
                        scalar1=dtcols[:, 32 * n + h:32 * n + h + 1]),
                        reads=[arB_b[12 + n], g_b], writes=[tab_b[0]])
                    builders = [lambda e: e.matmul(psum[7][:, 0:128], tile_e[:],
                                                   tile_a[:, 0:128], start=True,
                                                   stop=(n == 0))]
                    yreads = [tab_b[5], tab_b[0]]
                    if n > 0:
                        builders.append(lambda e, h=h: e.matmul(
                            psum[7][:, 0:128], tile_c[:, 0:128], vBpr[:, 64 + h, :],
                            start=False, stop=True))
                        yreads += [tab_b[3], arB_b[8 + h // 8]]
                    kb.group(builders, reads=yreads, writes=[ps_b[7]])
                    kb.group([lambda e, h=h: e.matmul(psum[5][:, 128:256],
                                                      tile_c[:, 0:128],
                                                      S0bf[:, 128 * h:128 * (h + 1)],
                                                      start=True, stop=True)],
                             reads=[tab_b[3], S0_bufs[h // 4]], writes=[ps_b[5]])
                    kb.op(kb.dve, lambda e, n=n, kv=kv, h=h: e.tensor_scalar_mul(
                        out=tile_b[:, 0:128], in0=vBkt[:, n, kv * 128:(kv + 1) * 128],
                        scalar1=edec[:, 32 * n + h:32 * n + h + 1]),
                        reads=[arB_b[n], g_b], writes=[tab_b[1]])
                    kb.group([lambda e: e.matmul(psum[4][:, 0:128], tile_b[:, 0:128],
                                                 tile_a[:, 0:128], start=True, stop=True)],
                             reads=[tab_b[0], tab_b[1]], writes=[ps_b[4]])
                    kb.op(kb.act, lambda e, h=h: e.activation(out=sco[:, 1:2],
                                                              in_=cumg[:, h:h + 1],
                                                              func=AF.Exp),
                          reads=[g_b], writes=[g_b])
                    kb.op(kb.dve, lambda e: e.tensor_tensor(out=sco[:, 2:3], in0=sco[:, 1:2],
                                                            in1=c_uses0[:], op=ALU.mult),
                          reads=[g_b, const_b], writes=[g_b])
                    kb.op(kb.dve, lambda e: e.tensor_copy(out=wrk[1][:, 0:128],
                                                          in_=psum[7][:, 0:128]),
                          reads=[ps_b[7]], writes=[wrk_b[1]])
                    kb.op(kb.dve, lambda e: e.scalar_tensor_tensor(
                        out=wrk[1][:, 0:128], in0=psum[5][:, 128:256], scalar=sco[:, 2:3],
                        in1=wrk[1][:, 0:128], op0=ALU.mult, op1=ALU.add),
                        reads=[ps_b[5], g_b, wrk_b[1]], writes=[wrk_b[1]])
                    kb.op(kb.dve, lambda e, n=n, h=h: e.scalar_tensor_tensor(
                        out=state[:, 128 * h:128 * (h + 1)],
                        in0=state[:, 128 * h:128 * (h + 1)],
                        scalar=ecd[:, 32 * n + h:32 * n + h + 1], in1=psum[4][:, 0:128],
                        op0=ALU.mult, op1=ALU.add),
                        reads=[ps_b[4], g_b, state_b], writes=[state_b])
                    kb.op(kb.act, lambda e, h=h: e.activation(
                        out=vBpr[:, 64 + h, :], in_=state[:, 128 * h:128 * (h + 1)],
                        func=AF.Copy), reads=[state_b], writes=[arB_b[8 + h // 8]])
                    kb.op(kb.dve, lambda e, h=h, gtot=gtot: e.tensor_tensor(
                        out=cumg[:, h:h + 1], in0=cumg[:, h:h + 1], in1=gtot, op=ALU.add),
                        reads=[g_b], writes=[g_b])
                    kb.op(kb.act, lambda e: e.activation(out=wrk[2][:, 0:128],
                                                         in_=wrk[1][:, 0:128],
                                                         func=AF.Square),
                          reads=[wrk_b[1]], writes=[wrk_b[2]])
                    kb.op(kb.dve, lambda e: e.reduce_sum(out=sco[:, 3:4],
                                                         in_=wrk[2][:, 0:128], axis=AX.X),
                          reads=[wrk_b[2]], writes=[g_b])
                    kb.op(kb.dve, lambda e: e.tensor_scalar(
                        out=sco[:, 4:5], in0=sco[:, 3:4], scalar1=1.0 / HD,
                        scalar2=EPS_RMS, op0=ALU.mult, op1=ALU.add),
                        reads=[g_b], writes=[g_b])
                    kb.op(kb.act, lambda e: e.activation(out=sco[:, 4:5], in_=sco[:, 4:5],
                                                         func=AF.Ln),
                          reads=[g_b], writes=[g_b])
                    kb.op(kb.dve, lambda e: e.tensor_scalar_mul(out=sco[:, 4:5],
                                                                in0=sco[:, 4:5],
                                                                scalar1=-0.5),
                          reads=[g_b], writes=[g_b])
                    kb.op(kb.act, lambda e: e.activation(out=sco[:, 4:5], in_=sco[:, 4:5],
                                                         func=AF.Exp),
                          reads=[g_b], writes=[g_b])
                    kb.op(kb.dve, lambda e: e.scalar_tensor_tensor(
                        out=wrk[2][:, 0:128], in0=wrk[1][:, 0:128], scalar=sco[:, 4:5],
                        in1=c_gnw[:], op0=ALU.mult, op1=ALU.mult),
                        reads=[wrk_b[1], g_b, const_b], writes=[wrk_b[2]])
                    dma(kb.spq, tile_f[:],
                        gate_hbm[128 * n:128 * (n + 1), 128 * h:128 * (h + 1)],
                        reads=[gate_b], writes=[tab_b[2]], stream="gtile")
                    kb.op(kb.dve, lambda e: e.tensor_tensor(out=tile_e[:],
                                                            in0=wrk[2][:, 0:128],
                                                            in1=tile_f[:], op=ALU.mult),
                          reads=[wrk_b[2], tab_b[2]], writes=[tab_b[5]])
                    dmaT(tile_d[:], tile_e[:], reads=[tab_b[5]], writes=[tab_b[4]],
                         stream="trY")
                    dma(kb.gpq, y_hbm[128 * h:128 * (h + 1), 128 * n:128 * (n + 1)],
                        tile_d[:], reads=[tab_b[4]], writes=[y_b], stream="y")

        for hh in range(2):
            o5 = 512 * hh

            def wo_evac(m, j, hh=hh, o5=o5):
                dma(kb.spq, xst[2][:, 0:512], xtile(src, hh, m),
                    reads=[src_b[m]], writes=[xst_b[2]], stream="xst2")
                kb.op(kb.dve, lambda e, j=j: e.tensor_tensor(
                    out=wrk[0][:, 0:512], in0=psum[j][:, 0:512], in1=xst[2][:, 0:512],
                    op=ALU.add), reads=[ps_b[j], xst_b[2]], writes=[wrk_b[0]])
                dma(kb.gpq, xtile(dst, hh, m), wrk[0][:, 0:512],
                    reads=[wrk_b[0]], writes=[dst_b[m]], stream=dst_stream)

            def rhs_fn(k, hh=hh, o5=o5):
                i = k % 2
                dma(kb.spq, yb[i][:, 0:512], y_hbm[128 * k:128 * (k + 1), o5:o5 + 512],
                    reads=[y_b], writes=[yb_b[i]], stream=f"yb{i}")
                return (yb[i][:, 0:512], yb_b[i])

            gemm1(wblk(wobT, 32),
                  32, list(range(32)), rhs_fn, wo_evac)


    order = []
    if nsub == -2:
        order = [("f", ffn)]
    if nsub == -3:
        order = [("b", attn_b)]
    if nsub >= 1:
        order.append(("a", attn_a))
    if nsub >= 2:
        order.append(("f", ffn))
    if nsub >= 3:
        order.append(("b", attn_b))
    if nsub >= 4:
        order.append(("f", ffn))

    cur_src, cur_src_b = x_own, [xown_b] * 32
    ping = 0
    for i, (kind, fn) in enumerate(order):
        last = (i == len(order) - 1)
        dst = out_fm if last else xs[ping]
        dst_b = out_b if last else xs_b[ping]
        dst_stream = "out" if last else f"xs{ping}"
        if kind == "a":
            attn_a(dst, dst_b, dst_stream)
        else:
            fn(cur_src, cur_src_b, dst, dst_b, dst_stream)
        kb.barrier(kb.gpq, dst_stream, dst_b)
        cur_src, cur_src_b = dst, dst_b
        ping ^= 1

    final_waits = sorted({b.w for b in out_b if b.w is not None})

    with nc.Block() as block:
        with ExitStack() as sems:
            kb.make_sems(nc, sems)
            kb.finalize(block, final_waits)

    es.close()
    return nc


_NC_CACHE = {}


def kernel(**inputs):
    in_maps = host_prep(inputs)
    key = NSUB
    if key not in _NC_CACHE:
        _NC_CACHE[key] = build_nc(NSUB)
    nc = _NC_CACHE[key]
    res = run_bass_kernel_spmd(nc, in_maps, core_ids=list(range(8)))
    out = np.zeros((B, S, D), np.float32)
    for c in range(8):
        b, h = c >> 1, c & 1
        fm = np.asarray(res.results[c]["out_fm"], np.float32).reshape(2, 32, 128, 512)
        out[b, h * T:(h + 1) * T] = fm.transpose(0, 3, 1, 2).reshape(T, D)
    return out
